# version 05
# baseline (speedup 1.0000x reference)
import contextlib
import numpy as np
import ml_dtypes
import concourse.bass as bass
import concourse.mybir as mybir
from concourse.bass_utils import run_bass_kernel_spmd

"""Shared kernel-builder machinery for the megakernel (raw Bass).

Conventions:
- Residual stream + all activations FEATURE-major: [128-feat-part x tokens].
- Host pre-transposes module input to [B, 1024, 2048] and weight matrices to
  contraction-major; host reshapes per-feature vectors to [128, ntiles].
- bf16 matmul operands everywhere, fp32 PSUM accumulation, fp32 elementwise
  for norms/cumsums/exps (mask exps computed fp32, stored bf16).
"""

F32 = mybir.dt.float32
BF16 = mybir.dt.bfloat16
FP8 = mybir.dt.float8e4
FP16 = mybir.dt.float16
AF = mybir.ActivationFunctionType
ALU = mybir.AluOpType

ENGINES = ("tensor", "vector", "scalar", "gpsimd", "sync")


class Prog:

    def __init__(self, nc, stack):
        self.nc = nc
        self.stack = stack
        self.q = {e: [] for e in ENGINES}
        self.sems = {}
        self.counts = {}
        for name in ("mm", "vec", "act", "gp", "ldw", "st", "ld2",
                     "ldp", "wb0", "wb1", "wo0", "wo1"):
            self._mksem(name)

    def _mksem(self, name):
        self.sems[name] = self.stack.enter_context(self.nc.semaphore(name))
        self.counts[name] = 0

    def emit(self, eng, fn, inc=None, inc_by=1):
        if inc is None:
            self.q[eng].append(lambda e, fn=fn: fn(e))
            return None
        self.counts[inc] += inc_by
        sem = self.sems[inc]
        self.q[eng].append(lambda e, fn=fn, s=sem, k=inc_by: fn(e).then_inc(s, k))
        return self.counts[inc]

    def wait(self, eng, ctr, val):
        if val is None or val <= 0:
            return
        sem = self.sems[ctr]
        self.q[eng].append(lambda e, s=sem, v=val: e.wait_ge(s, v))

    def count(self, ctr):
        return self.counts[ctr]

    def dma(self, eng, out, in_, inc=None, **kw):
        if inc is None:
            self.q[eng].append(lambda e, o=out, i=in_: e.dma_start(out=o, in_=i, **kw))
            return None
        self.counts[inc] += 16
        sem = self.sems[inc]
        self.q[eng].append(
            lambda e, o=out, i=in_, s=sem: e.dma_start(out=o, in_=i, **kw).then_inc(s, 16))
        return self.counts[inc]

    def barrier(self):
        for eng in ENGINES:
            for name, cnt in self.counts.items():
                if cnt > 0:
                    self.wait(eng, name, cnt)

    def finalize(self, block):
        for eng in ENGINES:
            ops = self.q[eng]
            if not ops:
                continue

            def runner(e, ops=ops):
                for op in ops:
                    op(e)
            getattr(block, eng)(runner)


def ceil_div(a, b):
    return -(-a // b)



def host_consts():
    c = {}
    c["c_ident"] = np.eye(128).astype(ml_dtypes.bfloat16)
    c["c_identf"] = np.eye(128).astype(np.float32)
    c["c_triu_i"] = np.triu(np.ones((128, 128))).astype(ml_dtypes.bfloat16)
    c["c_triu_s"] = np.triu(np.ones((128, 128)), 1).astype(ml_dtypes.bfloat16)
    tl = np.tril(np.ones((64, 64)), -1)
    c["c_pair_tril_s"] = np.block(
        [[tl, np.zeros((64, 64))], [np.zeros((64, 64)), tl]]).astype(ml_dtypes.bfloat16)
    tu_i = np.triu(np.ones((64, 64)))
    c["c_pair_triu_i"] = np.block(
        [[tu_i, np.zeros((64, 64))], [np.zeros((64, 64)), tu_i]]).astype(ml_dtypes.bfloat16)
    sel = np.zeros((32, 32, 128), np.float32)
    for h in range(32):
        sel[h, h, :] = 1.0
    c["c_sel32"] = np.ascontiguousarray(sel.transpose(1, 0, 2))
    selp = np.zeros((6, 12, 128), np.float32)
    for j in range(6):
        selp[j, 2 * j, 0:64] = 1.0
        selp[j, 2 * j + 1, 64:128] = 1.0
    c["c_selp"] = np.ascontiguousarray(selp.transpose(1, 0, 2))
    bd = np.block([[np.ones((64, 64)), np.zeros((64, 64))],
                   [np.zeros((64, 64)), np.ones((64, 64))]])
    c["c_bdones"] = bd.astype(ml_dtypes.bfloat16)
    c["c_ones_f"] = np.ones((128, 1), np.float32)
    c["c_ones_b"] = np.ones((128, 1)).astype(ml_dtypes.bfloat16)
    c["c_misc"] = np.array([[1e-5, 1e-6, 0.0, 1.0]], np.float32).repeat(128, 0)
    return c


def vec_tiles(v, p=128):
    n = v.shape[0]
    nt = ceil_div(n, p)
    out = np.zeros((p, nt), v.dtype)
    for t in range(nt):
        seg = v[t * p:(t + 1) * p]
        out[: seg.shape[0], t] = seg
    return np.ascontiguousarray(out)

def _mod_l0m():

    LSEQ = 2048
    D = 1024
    DIN = 2048
    DPROJ = 4256
    NH = 32
    HD = 64
    TG = 512
    NKD = D // 128
    NMT = 34
    NZT = 16
    NXT = 17
    EPS = 1e-5
    NSEQ = 2
    CFG = {"pfx": "", "xin": None, "xout": None}


    class Ctx:
        pass


    def declare(nc, c, dbg=False):
        c.nc = nc
        c.dbg = dbg

        def dram(name, shape, dt, kind="ExternalInput"):
            return nc.dram_tensor(CFG["pfx"] + name, shape, dt, kind=kind).ap()

        c.xT = CFG["xin"] if CFG["xin"] is not None else dram("xT", [NSEQ, D, LSEQ], F32)
        c.w_inT = dram("w_inT", [D, DPROJ], F32)
        c.w_outT = dram("w_outT", [DIN, D], F32)
        c.lnw8 = dram("lnw8", [128, 8], F32)
        c.lnb8 = dram("lnb8", [128, 8], F32)
        c.convw = dram("convw", [128, NXT, 4], F32)
        c.convb = dram("convb", [128, NXT], F32)
        c.dtb = dram("dtb", [32, 1], F32)
        c.alog = dram("alog", [32, 1], F32)
        c.dexp = dram("dexp", [128, NZT], F32)
        c.nrmw16 = dram("nrmw16", [128, NZT], F32)
        c.c_ident = dram("c_ident", [128, 128], BF16)
        c.c_identf = dram("c_identf", [128, 128], F32)
        c.c_triu_i = dram("c_triu_i", [128, 128], BF16)
        c.c_sel32 = dram("c_sel32", [32, 32, 128], F32)
        c.c_sel32b = dram("c_sel32b", [32, 32, 64], BF16)
        c.c_ones_b = dram("c_ones_b", [128, 1], BF16)
        c.c_ones_row = dram("c_ones_row", [1, 128], F32)
        c.c_eps_row = dram("c_eps_row", [1, 1], F32)
        c.c_one32 = dram("c_one32", [32, 1], F32)
        c.x1T = CFG["xout"] if CFG["xout"] is not None else dram("x1T", [NSEQ, D, LSEQ], F32, kind="ExternalOutput")
        c.dbg_t = {}
        if dbg:
            for nm, shp in [("d_xs", [128, NZT, TG]), ("d_b", [64, TG]), ("d_c", [64, TG]),
                            ("d_dt", [32, TG]), ("d_y", [128, NZT, TG]),
                            ("d_zx", [128, NXT, TG]), ("d_yn", [128, NZT, TG]),
                            ("d_cum", [32, TG]), ("d_ys", [128, NZT, TG])]:
                c.dbg_t[nm] = dram(nm, shp, F32, kind="ExternalOutput")
        c.wi_scr = dram("wi_scr", [NKD, 128, DPROJ], BF16, kind="Internal")
        c.wo_scr = dram("wo_scr", [NZT, 128, D], BF16, kind="Internal")
        return c


    def alloc(c, E):
        nc = c.nc

        def S(name, shape, dt):
            t = E(nc.sbuf_tensor(CFG["pfx"] + name, shape, dt))
            setattr(c, name, t)
            return t

        S("x_fm", [128, NKD, TG], F32)
        S("zsil", [128, NZT, TG], BF16)
        S("xbc", [128, NXT, TG + 3], BF16)
        S("yg", [128, NZT, TG], BF16)
        S("xs_s1", [128, NKD, TG], F32)
        c.xs = c.xs_s1[:].bitcast(BF16).rearrange("p a b -> p (a b)").rearrange("p (zt t) -> p zt t", zt=NZT)
        S("y_bf", [128, NZT, TG], BF16)
        c.sq16 = c.y_bf[:, 0:8, :]
        c.xhat = c.y_bf[:, 8:16, :]
        c.xb16 = c.yg[:, 0:8, :]
        c.sqz = c.zsil[:]
        S("t1", [128, TG], F32)
        S("r0", [1, TG], F32)
        S("r1", [1, TG], F32)
        S("r2", [1, TG], F32)
        S("Bt", [64, TG], BF16)
        S("Ct", [64, TG], BF16)
        S("dtraw", [32, TG], F32)
        S("dtv", [32, TG], F32)
        S("spx", [32, TG], F32)
        S("dtf", [32, TG], F32)
        S("cumA", [32, TG], F32)
        S("pfm", [32, TG], BF16)
        S("hs", [64, 2, NH, HD], BF16)
        S("xs_tm", [128, NZT, 128], BF16)
        S("xt_tm", [128, 2, 64], BF16)
        S("b_tm", [128, 64], BF16)
        S("b_hat", [128, 2, 64], BF16)
        S("dt_tm", [128, 32], F32)
        S("ca_tm", [128, 32], F32)
        S("dmsk", [128, 8, 128], F32)
        S("dmin", [128, 8, 128], F32)
        S("emsk", [128, 8, 128], BF16)
        S("wmsk", [128, 8, 128], BF16)
        S("chat", [64, 8, 128], BF16)
        S("gmsk", [128, 128], BF16)
        S("gcol", [64, NH], F32)
        S("ecol", [128, 8], F32)
        S("dxs", [128, NZT, 128], BF16)
        c.wbuf = [S("wbuf0", [128, NKD, 256], BF16), S("wbuf1", [128, NKD, 256], BF16)]
        c.wob = [S("wob0", [128, NZT, 128], BF16), S("wob1", [128, NZT, 128], BF16)]
        S("wbeta", [128, NMT], F32)
        S("diag", [128, NXT, 4, 128], BF16)
        S("lnw_s", [128, 8], F32)
        S("lnb_s", [128, 8], F32)
        S("lnb_b", [128, 8], BF16)
        S("convw_s", [128, NXT, 4], F32)
        S("convb_s", [128, NXT], F32)
        S("dtb_s", [32, 1], F32)
        S("nA_s", [32, 1], F32)
        S("dexp_s", [128, NZT], F32)
        S("nrmw_s", [128, NZT], F32)
        S("ident_s", [128, 128], BF16)
        S("identf_s", [128, 128], F32)
        S("triu_s", [128, 128], BF16)
        S("sel_s", [32, 32, 128], F32)
        S("selb_s", [32, 32, 64], BF16)
        S("onesb_s", [128, 1], BF16)
        S("onesr_s", [1, 128], F32)
        S("epsr_s", [1, 1], F32)
        S("one32_s", [32, 1], F32)

        c.pA = E(nc.psum_tensor(CFG["pfx"] + "pA", [128, 512], F32))
        c.pB = E(nc.psum_tensor(CFG["pfx"] + "pB", [128, 512], F32))
        c.pW = E(nc.psum_tensor(CFG["pfx"] + "pW", [128, 1024], F32))
        c.pP = E(nc.psum_tensor(CFG["pfx"] + "pP", [128, 1024], F32))
        c.pY = E(nc.psum_tensor(CFG["pfx"] + "pY", [128, 512], F32))
        c.pZ = E(nc.psum_tensor(CFG["pfx"] + "pZ", [128, 512], F32))
        c.pWb = c.pW[:].bitcast(BF16)
        c.pZb = c.pZ[:].bitcast(BF16)


    def build(nc, dbg=False, ext=None):
        stack = contextlib.ExitStack()
        E = stack.enter_context
        c = Ctx()
        declare(nc, c, dbg)
        alloc(c, E)
        p = ext if ext is not None else Prog(nc, stack)
        c.p = p
        c.free = {}
        c.wbuf_last = {0: 0, 1: 0}
        c.wob_last = {0: 0, 1: 0}
        c.wout_mm_last = 0
        c.xfm_free = 0
        c.scan_mm_end = 0
        c.conv_mm_last = 0

        def pe_wait(name):
            if name in c.free:
                ctr, v = c.free[name]
                p.wait("tensor", ctr, v)

        c.pe_wait = pe_wait
        prepass(c)
        for b in range(NSEQ):
            p.wait("vector", "mm", c.scan_mm_end)
            p.emit("vector", lambda e: e.memset(c.hs[:], 0.0))
            p.emit("vector", lambda e: e.memset(c.xbc[:, :, 0:3], 0.0), inc="vec")
            for g in range(LSEQ // TG):
                emit_group(c, b, g)
        if ext is not None:
            return stack
        p.wait("gpsimd", "st", p.count("st"))
        p.wait("gpsimd", "gp", p.count("gp"))
        block = E(nc.Block())
        p.finalize(block)
        stack.close()
        return nc


    def prepass(c):
        p = c.p
        pairs = [(c.lnw_s, c.lnw8), (c.lnb_s, c.lnb8), (c.convw_s, c.convw),
                 (c.convb_s, c.convb), (c.dtb_s, c.dtb), (c.nA_s, c.alog),
                 (c.dexp_s, c.dexp), (c.nrmw_s, c.nrmw16),
                 (c.ident_s, c.c_ident), (c.identf_s, c.c_identf), (c.triu_s, c.c_triu_i),
                 (c.sel_s, c.c_sel32), (c.selb_s, c.c_sel32b), (c.onesb_s, c.c_ones_b),
                 (c.onesr_s, c.c_ones_row), (c.epsr_s, c.c_eps_row), (c.one32_s, c.c_one32)]
        for dst, src in pairs:
            p.dma("sync", dst[:], src, inc="ld2")
        n_small = p.count("ld2")
        p.dma("gpsimd", c.wi_scr, c.w_inT.rearrange("(k p) f -> k p f", p=128), inc="ldp")
        p.dma("gpsimd", c.wo_scr, c.w_outT.rearrange("(k p) f -> k p f", p=128), inc="ldp")
        c.wscr_done = p.count("ldp")

        p.wait("scalar", "ld2", n_small)
        p.emit("scalar", lambda e: e.activation(c.spx[0:32, 0:1], c.nA_s[:], AF.Exp))
        p.emit("scalar", lambda e: e.mul(c.nA_s[:], c.spx[0:32, 0:1], -1.0), inc="act")
        c.nA_ready = p.count("act")
        p.wait("vector", "ld2", n_small)
        for i in range(NXT):
            for j in range(4):
                p.emit("vector", lambda e, i=i, j=j: e.tensor_scalar(
                    c.diag[:, i, j, :], c.ident_s[:], c.convw_s[:, i, j:j + 1], None, ALU.mult))
        p.emit("vector", lambda e: e.tensor_copy(c.lnb_b[:], c.lnb_s[:]))
        p.emit("vector", lambda e: e.memset(c.wbeta[:], 0.0), inc="vec")
        c.prep_v = p.count("vec")
        wingemm(c, None, c.prep_v)


    def wingemm(c, v_xh, prep_v=None):
        p = c.p
        beta = v_xh is None
        pend = {}
        for mp in range(17):
            p.wait("gpsimd", "mm", c.wbuf_last[mp % 2])
            if beta and mp == 0:
                p.wait("gpsimd", "ldp", c.wscr_done)
            fs, fe = mp * 256, min(mp * 256 + 256, DPROJ)
            pend[mp] = p.dma("gpsimd", c.wbuf[mp % 2][:, :, 0:fe - fs],
                             c.wi_scr[:, :, fs:fe].rearrange("k p f -> p k f"),
                             inc="wb%d" % (mp % 2))
            for m in (2 * mp, 2 * mp + 1):
                if m >= NMT:
                    continue
                off = (m % 2) * 128
                mw = 128 if m < 33 else 32
                bank = [c.pA, c.pB][m % 2]
                bname = ["pa", "pb"][m % 2]
                p.wait("tensor", "wb%d" % (mp % 2), pend[mp])
                if beta:
                    p.wait("tensor", "vec", prep_v)
                else:
                    p.wait("tensor", "vec", v_xh)
                c.pe_wait(bname)
                mm = None
                for k in range(NKD):
                    if beta:
                        mm = p.emit("tensor", lambda e, bank=bank, mp=mp, k=k, off=off, mw=mw:
                                    e.matmul(bank[0:mw, 0:1], c.wbuf[mp % 2][:, k, off:off + mw],
                                             c.lnb_b[:, k:k + 1], start=(k == 0), stop=(k == NKD - 1)),
                                    inc=("mm" if k == NKD - 1 else None))
                    else:
                        mm = p.emit("tensor", lambda e, bank=bank, mp=mp, k=k, off=off, mw=mw:
                                    e.matmul(bank[0:mw, :], c.wbuf[mp % 2][:, k, off:off + mw],
                                             c.xhat[:, k, :], start=(k == 0), stop=(k == NKD - 1)),
                                    inc=("mm" if k == NKD - 1 else None))
                c.wbuf_last[mp % 2] = mm
                if beta:
                    p.wait("vector", "mm", mm)
                    vv = p.emit("vector", lambda e, bank=bank, m=m, mw=mw: e.tensor_copy(
                        c.wbeta[0:mw, m:m + 1], bank[0:mw, 0:1]), inc="vec")
                    c.free[bname] = ("vec", vv)
                elif m < NZT:
                    p.wait("scalar", "mm", mm)
                    p.wait("scalar", "vec", c.zsil_free)
                    vv = p.emit("scalar", lambda e, bank=bank, m=m: e.activation(
                        c.zsil[:, m, :], bank[:], AF.Silu, bias=c.wbeta[:, m:m + 1]), inc="act")
                    c.free[bname] = ("act", vv)
                elif m < 33:
                    i = m - NZT
                    p.wait("vector", "mm", mm)
                    vv = p.emit("vector", lambda e, bank=bank, i=i, m=m: e.tensor_scalar(
                        c.xbc[:, i, 3:3 + TG], bank[:], c.wbeta[:, m:m + 1], None, ALU.add),
                        inc="vec")
                    c.free[bname] = ("vec", vv)
                else:
                    p.wait("vector", "mm", mm)
                    vv = p.emit("vector", lambda e, bank=bank: e.tensor_scalar(
                        c.dtraw[:], bank[0:32, :], c.wbeta[0:32, 33:34], c.dtb_s[:],
                        ALU.add, ALU.add), inc="vec")
                    c.free[bname] = ("vec", vv)


    def emit_ln_rows_and_bcast(c, denom):
        p = c.p
        p.wait("vector", "mm", p.count("mm"))
        p.emit("vector", lambda e: e.tensor_scalar(
            c.r0[0:1, 0:512], c.pW[0:1, 0:512], 1.0 / denom, None, ALU.mult))
        p.emit("vector", lambda e: e.tensor_tensor(
            c.r1[0:1, 0:512], c.r0[0:1, 0:512], c.r0[0:1, 0:512], ALU.mult))
        p.emit("vector", lambda e: e.scalar_tensor_tensor(
            c.r2[0:1, 0:512], c.pW[0:1, 512:1024], 1.0 / denom, c.r1[0:1, 0:512],
            ALU.mult, ALU.subtract), inc="vec")
        v_rows = p.count("vec")
        c.free["pw"] = ("vec", v_rows)
        p.wait("scalar", "vec", v_rows)
        p.emit("scalar", lambda e: e.activation(
            c.r1[0:1, 0:512], c.r2[0:1, 0:512], AF.Ln, bias=c.epsr_s[:]))
        p.emit("scalar", lambda e: e.activation(
            c.r2[0:1, 0:512], c.r1[0:1, 0:512], AF.Exp, scale=-0.5), inc="act")
        a_rows = p.count("act")
        p.wait("tensor", "act", a_rows)
        c.pe_wait("pp")
        p.emit("tensor", lambda e: e.matmul(
            c.pP[:, 0:512], c.onesr_s[:], c.r0[0:1, 0:512], start=True, stop=True), inc="mm")
        p.emit("tensor", lambda e: e.matmul(
            c.pP[:, 512:1024], c.onesr_s[:], c.r2[0:1, 0:512], start=True, stop=True), inc="mm")
        return p.count("mm")


    def emit_stats_mms(c, xb_ap, sq_ap, v_ready, a_ready):
        p = c.p
        p.wait("tensor", "vec", v_ready)
        p.wait("tensor", "act", a_ready)
        c.pe_wait("pw")
        for k in range(NKD):
            p.emit("tensor", lambda e, k=k: e.matmul(
                c.pW[0:1, 0:512], c.onesb_s[:], xb_ap[:, k, :],
                start=(k == 0), stop=(k == NKD - 1)), inc=("mm" if k == NKD - 1 else None))
        for k in range(NKD):
            p.emit("tensor", lambda e, k=k: e.matmul(
                c.pW[0:1, 512:1024], c.onesb_s[:], sq_ap[:, k, :],
                start=(k == 0), stop=(k == NKD - 1)), inc=("mm" if k == NKD - 1 else None))


    def emit_group(c, b, g):
        p = c.p
        NSUB = TG // 128
        gsl = slice(g * TG, (g + 1) * TG)

        p.wait("sync", "vec", c.xfm_free)
        ld = p.dma("sync", c.x_fm[:],
                   c.xT[b].rearrange("(k p) t -> p k t", p=128)[:, :, gsl], inc="ld2")

        p.wait("vector", "ld2", ld)
        p.wait("vector", "vec", getattr(c, "yg_free", 0))
        p.emit("vector", lambda e: e.tensor_copy(c.xb16, c.x_fm[:]), inc="vec")
        v_xb = p.count("vec")
        p.wait("scalar", "vec", v_xb)
        p.wait("scalar", "mm", c.wout_mm_last)
        p.emit("scalar", lambda e: e.activation(c.sq16, c.xb16, AF.Square), inc="act")
        a_sq = p.count("act")
        emit_stats_mms(c, c.xb16, c.sq16, v_xb, a_sq)
        mm_bc = emit_ln_rows_and_bcast(c, float(D))
        p.wait("vector", "mm", mm_bc)
        v_xh = None
        for k in range(NKD):
            p.emit("vector", lambda e, k=k: e.tensor_tensor(
                c.t1[:], c.x_fm[:, k, :], c.pP[:, 0:512], ALU.subtract))
            v_xh = p.emit("vector", lambda e, k=k: e.scalar_tensor_tensor(
                c.xhat[:, k, :], c.t1[:], c.lnw_s[:, k:k + 1], c.pP[:, 512:1024],
                ALU.mult, ALU.mult), inc="vec")
        c.free["pp"] = ("vec", v_xh)

        if g > 0:
            p.wait("vector", "mm", c.conv_mm_last)
            p.emit("vector", lambda e: e.tensor_copy(
                c.xbc[:, :, 0:3], c.xbc[:, :, TG:TG + 3]), inc="vec")

        c.zsil_free = getattr(c, "v_yg", 0)
        wingemm(c, v_xh)
        v_xbc_done = p.count("vec")

        for i in range(NXT):
            bank = [c.pA, c.pB][i % 2]
            bname = ["pa", "pb"][i % 2]
            p.wait("tensor", "vec", v_xbc_done)
            c.pe_wait(bname)
            mm = None
            for j in range(4):
                mm = p.emit("tensor", lambda e, bank=bank, i=i, j=j: e.matmul(
                    bank[:], c.diag[:, i, j, :], c.xbc[:, i, j:j + TG],
                    start=(j == 0), stop=(j == 3)), inc=("mm" if j == 3 else None))
            c.conv_mm_last = mm
            p.wait("scalar", "mm", mm)
            if i < NZT:
                p.wait("scalar", "mm", c.scan_mm_end)
                p.wait("scalar", "vec", getattr(c, "v_s1_stored", 0))
                p.wait("scalar", "st", getattr(c, "s1_store_done", 0))
                vv = p.emit("scalar", lambda e, bank=bank, i=i: e.activation(
                    c.xs[:, i, :], bank[:], AF.Silu, bias=c.convb_s[:, i:i + 1]), inc="act")
            else:
                p.emit("scalar", lambda e, bank=bank: e.activation(
                    c.Bt[:], bank[0:64, :], AF.Silu, bias=c.convb_s[0:64, 16:17]))
                vv = p.emit("scalar", lambda e, bank=bank: e.activation(
                    c.Ct[:], bank[64:128, :], AF.Silu, bias=c.convb_s[64:128, 16:17]), inc="act")
            c.free[bname] = ("act", vv)
        a_conv_done = p.count("act")

        p.wait("vector", "mm", c.scan_mm_end)
        v_dtev = p.count("vec")
        p.wait("scalar", "vec", v_dtev)
        p.emit("scalar", lambda e: e.activation(c.dtv[:], c.dtraw[:], AF.Abs))
        p.emit("scalar", lambda e: e.activation(c.spx[:], c.dtv[:], AF.Exp, scale=-1.0))
        p.emit("scalar", lambda e: e.activation(c.dtv[:], c.spx[:], AF.Ln, bias=c.one32_s[:]))
        p.emit("scalar", lambda e: e.activation(c.spx[:], c.dtraw[:], AF.Relu), inc="act")
        a_sp = p.count("act")
        p.wait("vector", "act", a_sp)
        p.emit("vector", lambda e: e.tensor_tensor(c.dtf[:], c.spx[:], c.dtv[:], ALU.add))
        p.wait("vector", "act", c.nA_ready)
        p.emit("vector", lambda e: e.tensor_scalar(
            c.dtraw[:], c.dtf[:], c.nA_s[:], None, ALU.mult), inc="vec")
        for s in range(NSUB):
            p.emit("vector", lambda e, s=s: e.tensor_tensor_scan(
                c.cumA[:, s * 128:(s + 1) * 128], c.one32_s[:].broadcast_to([32, 128]),
                c.dtraw[:, s * 128:(s + 1) * 128], 0.0, ALU.mult, ALU.add), inc="vec")
        v_cum = p.count("vec")
        p.wait("scalar", "vec", v_cum)
        p.emit("scalar", lambda e: e.activation(c.pfm[:], c.cumA[:], AF.Exp), inc="act")
        a_pfm = p.count("act")

        for s in range(NSUB):
            emit_scan_sub(c, b, g, s, v_cum, a_pfm, a_conv_done)
        c.scan_mm_end = p.count("mm")

        if c.dbg and b == 0 and g == 0:
            p.wait("gpsimd", "vec", p.count("vec"))
            p.dma("gpsimd", c.dbg_t["d_cum"], c.cumA[:], inc="gp")
            p.dma("gpsimd", c.dbg_t["d_ys"], c.y_bf[:], inc="gp")

        p.wait("vector", "mm", c.scan_mm_end)
        p.emit("vector", lambda e: e.tensor_tensor(
            c.yg[:].rearrange("p a t -> p (a t)"), c.y_bf[:].rearrange("p a t -> p (a t)"),
            c.zsil[:].rearrange("p a t -> p (a t)"), ALU.mult), inc="vec")
        c.v_yg = p.count("vec")
        c.yg_free = None
        p.wait("scalar", "vec", c.v_yg)
        p.emit("scalar", lambda e: e.activation(
            c.sqz.rearrange("p a t -> p (a t)"), c.yg[:].rearrange("p a t -> p (a t)"),
            AF.Square), inc="act")
        a_ysq = p.count("act")
        p.wait("tensor", "act", a_ysq)
        c.pe_wait("pw")
        mm = None
        for i in range(NZT):
            mm = p.emit("tensor", lambda e, i=i: e.matmul(
                c.pW[0:1, 0:512], c.onesb_s[:], c.sqz[:, i, :],
                start=(i == 0), stop=(i == NZT - 1)), inc=("mm" if i == NZT - 1 else None))
        p.wait("scalar", "mm", mm)
        p.emit("scalar", lambda e: e.activation(
            c.r1[0:1, 0:512], c.pW[0:1, 0:512], AF.Ln, bias=c.epsr_s[:], scale=1.0 / DIN))
        p.emit("scalar", lambda e: e.activation(
            c.r2[0:1, 0:512], c.r1[0:1, 0:512], AF.Exp, scale=-0.5), inc="act")
        a_rms = p.count("act")
        c.free["pw"] = ("act", a_rms)
        p.wait("tensor", "act", a_rms)
        c.pe_wait("pp")
        mm_rb = p.emit("tensor", lambda e: e.matmul(
            c.pP[:, 0:512], c.onesr_s[:], c.r2[0:1, 0:512], start=True, stop=True), inc="mm")
        p.wait("vector", "mm", mm_rb)
        v_yn = None
        for i in range(NZT):
            v_yn = p.emit("vector", lambda e, i=i: e.scalar_tensor_tensor(
                c.y_bf[:, i, :], c.yg[:, i, :], c.nrmw_s[:, i:i + 1], c.pP[:, 0:512],
                ALU.mult, ALU.mult), inc="vec")
        c.free["pp"] = ("vec", v_yn)
        c.yg_free = v_yn

        if c.dbg and b == 0 and g == 0:
            emit_dbg(c, v_yn)

        wo_ld = {}

        def load_wo(m):
            p.wait("gpsimd", "mm", c.wob_last[m % 2])
            if b == 0 and g == 0 and m == 0:
                p.wait("gpsimd", "ldp", c.wscr_done)
            wo_ld[m] = p.dma("gpsimd", c.wob[m % 2][:],
                             c.wo_scr[:, :, m * 128:(m + 1) * 128].rearrange("k p f -> p k f"),
                             inc="wo%d" % (m % 2))

        load_wo(0)
        load_wo(1)
        for m in range(NKD):
            bank = [c.pA, c.pB][m % 2]
            bname = ["pa", "pb"][m % 2]
            p.wait("tensor", "wo%d" % (m % 2), wo_ld[m])
            p.wait("tensor", "vec", v_yn)
            c.pe_wait(bname)
            mm = None
            for k in range(NZT):
                mm = p.emit("tensor", lambda e, bank=bank, m=m, k=k: e.matmul(
                    bank[:], c.wob[m % 2][:, k, :], c.y_bf[:, k, :],
                    start=(k == 0), stop=(k == NZT - 1)), inc=("mm" if k == NZT - 1 else None))
            c.wob_last[m % 2] = mm
            c.wout_mm_last = mm
            if m + 2 < NKD:
                load_wo(m + 2)
            p.wait("vector", "mm", mm)
            vv = p.emit("vector", lambda e, bank=bank, m=m: e.scalar_tensor_tensor(
                c.xs_s1[:, m, :], c.x_fm[:, m, :], 2.0, bank[:], ALU.mult, ALU.add), inc="vec")
            c.free[bname] = ("vec", vv)
        v_s1 = p.count("vec")
        c.xfm_free = v_s1

        p.emit("vector", lambda e: e.tensor_copy(c.xb16, c.xs_s1[:]), inc="vec")
        v_xb2 = p.count("vec")
        p.wait("scalar", "vec", v_xb2)
        p.emit("scalar", lambda e: e.activation(c.sq16, c.xb16, AF.Square), inc="act")
        a_sq2 = p.count("act")
        emit_stats_mms(c, c.xb16, c.sq16, v_xb2, a_sq2)
        mm_bc2 = emit_ln_rows_and_bcast(c, float(D))
        p.wait("vector", "mm", mm_bc2)
        v_x1 = None
        for k in range(NKD):
            p.emit("vector", lambda e, k=k: e.tensor_tensor(
                c.t1[:], c.xs_s1[:, k, :], c.pP[:, 0:512], ALU.subtract))
            v_x1 = p.emit("vector", lambda e, k=k: e.tensor_tensor(
                c.xs_s1[:, k, :], c.t1[:], c.pP[:, 512:1024], ALU.mult), inc="vec")
        c.free["pp"] = ("vec", v_x1)
        c.v_s1_stored = v_x1
        p.wait("sync", "vec", v_x1)
        p.dma("sync", c.x1T[b].rearrange("(k p) t -> p k t", p=128)[:, :, gsl],
              c.xs_s1[:], inc="st")
        c.s1_store_done = p.count("st")


    def emit_scan_sub(c, b, g, s, v_cum, a_pfm, a_conv_done):
        p = c.p
        NSUB = TG // 128
        ssl = slice(s * 128, (s + 1) * 128)
        par = (g * NSUB + s) % 2
        xt_free = getattr(c, "xt_free", {0: 0, 1: 0})
        c.xt_free = xt_free

        p.wait("tensor", "act", a_conv_done)
        for half in range(2):
            c.pe_wait("pw")
            mm = None
            for i in range(8):
                t = half * 8 + i
                mm = p.emit("tensor", lambda e, t=t, i=i, half=half: e.transpose(
                    c.pWb[:, half * 1024 + i * 128: half * 1024 + (i + 1) * 128],
                    c.xs[:, t, ssl], c.ident_s[:]), inc="mm")
            p.wait("vector", "mm", mm)
            vv = p.emit("vector", lambda e, half=half: e.tensor_copy(
                c.xs_tm[:, half * 8:(half + 1) * 8, :],
                c.pWb[:, half * 1024:(half + 1) * 1024].rearrange("p (a t) -> p a t", a=8)),
                inc="vec")
            c.free["pw"] = ("vec", vv)
        c.pe_wait("pz")
        mm = p.emit("tensor", lambda e: e.transpose(
            c.pZb[:, 0:64], c.Bt[:, ssl], c.ident_s[0:64, 0:64]), inc="mm")
        p.wait("vector", "mm", mm)
        vv = p.emit("vector", lambda e: e.tensor_copy(c.b_tm[:], c.pZb[:, 0:64]), inc="vec")
        c.free["pz"] = ("vec", vv)
        p.wait("tensor", "vec", v_cum)
        c.pe_wait("py")
        mm = p.emit("tensor", lambda e: e.transpose(
            c.pY[:, 0:32], c.cumA[:, ssl], c.identf_s[0:32, 0:32]), inc="mm")
        mm = p.emit("tensor", lambda e: e.transpose(
            c.pY[:, 32:64], c.dtf[:, ssl], c.identf_s[0:32, 0:32]), inc="mm")
        p.wait("vector", "mm", mm)
        p.emit("vector", lambda e: e.tensor_copy(c.ca_tm[:], c.pY[:, 0:32]))
        vv = p.emit("vector", lambda e: e.tensor_copy(c.dt_tm[:], c.pY[:, 32:64]), inc="vec")
        c.free["py"] = ("vec", vv)
        v_tp = p.count("vec")

        c.pe_wait("pp")
        mm = p.emit("tensor", lambda e: e.matmul(
            c.pP[:, 0:128], c.Bt[:, ssl], c.Ct[:, ssl], start=True, stop=True), inc="mm")
        p.wait("vector", "mm", mm)
        v_g = p.emit("vector", lambda e: e.tensor_tensor(
            c.gmsk[:], c.pP[:, 0:128], c.triu_s[:], ALU.mult), inc="vec")
        c.free["pp"] = ("vec", v_g)
        p.emit("vector", lambda e: e.tensor_tensor(
            c.dxs[:], c.xs[:, :, ssl],
            c.dexp_s[:].unsqueeze(2).broadcast_to([128, NZT, 128]), ALU.mult), inc="vec")

        prev_hg_ymm = 0
        prev_hg_ae = 0
        prev_hg_vread = 0
        for hg in range(4):
            h0 = hg * 8
            c.pe_wait("pw")
            p.wait("tensor", "vec", v_tp)
            mm = None
            for i in range(8):
                mm = p.emit("tensor", lambda e, h=h0 + i, i=i: e.matmul(
                    c.pW[:, i * 128:(i + 1) * 128], c.sel_s[:, h, :], c.cumA[:, ssl],
                    start=True, stop=True), inc="mm")
            p.wait("vector", "mm", mm)
            p.wait("vector", "act", prev_hg_ae)
            p.emit("vector", lambda e, h0=h0: e.tensor_tensor(
                c.dmsk[:], c.ca_tm[:, h0:h0 + 8].unsqueeze(2).broadcast_to([128, 8, 128]),
                c.pW[:].rearrange("p (a t) -> p a t", a=8), ALU.subtract), inc="vec")
            v_d = p.count("vec")
            c.free["pw"] = ("vec", v_d)
            p.wait("scalar", "vec", v_d)
            p.wait("scalar", "vec", prev_hg_vread)
            p.emit("scalar", lambda e: e.activation(c.dmin[:], c.dmsk[:], AF.Relu))
            p.emit("scalar", lambda e: e.activation(c.emsk[:], c.dmin[:], AF.Exp, scale=-1.0),
                   inc="act")
            a_e = p.count("act")
            prev_hg_ae = a_e
            p.wait("vector", "act", a_e)
            p.emit("vector", lambda e: e.tensor_copy(
                c.ecol[:].unsqueeze(2), c.emsk[:, :, 127:128]), inc="vec")
            p.wait("vector", "vec", v_g)
            p.wait("vector", "mm", prev_hg_ymm)
            v_w = p.emit("vector", lambda e: e.tensor_tensor(
                c.wmsk[:], c.emsk[:], c.gmsk[:].unsqueeze(1).broadcast_to([128, 8, 128]),
                ALU.mult), inc="vec")
            c.pe_wait("pp")
            p.wait("tensor", "act", a_pfm)
            mm = None
            for i in range(8):
                mm = p.emit("tensor", lambda e, h=h0 + i, i=i: e.matmul(
                    c.pP[0:64, i * 128:(i + 1) * 128], c.selb_s[:, h, :], c.pfm[:, ssl],
                    start=True, stop=True), inc="mm")
            p.wait("vector", "mm", mm)
            p.emit("vector", lambda e: e.tensor_tensor(
                c.chat[:], c.Ct[:, ssl].unsqueeze(1).broadcast_to([64, 8, 128]),
                c.pP[0:64, :].rearrange("p (a t) -> p a t", a=8), ALU.mult))
            v_chat = p.emit("vector", lambda e, h0=h0: e.tensor_copy(
                c.gcol[:, h0:h0 + 8].unsqueeze(2),
                c.pP[0:64, :].rearrange("p (a t) -> p a t", a=8)[:, :, 127:128]), inc="vec")
            c.free["pp"] = ("vec", v_chat)

            last_vread = 0
            for i in range(8):
                h = h0 + i
                quad = (h % 8) // 4
                bank = [c.pY, c.pZ][quad]
                bqname = ["py", "pz"][quad]
                qoff = (h % 4 // 2) * 128
                poff = (h % 2) * 64
                xtile, xhalf = h // 2, h % 2
                slot = i % 2
                p.wait("vector", "vec", v_tp)
                p.wait("vector", "mm", xt_free[slot])
                v_xt = p.emit("vector", lambda e, slot=slot, xtile=xtile, xhalf=xhalf, h=h:
                              e.tensor_scalar(
                    c.xt_tm[:, slot, :], c.xs_tm[:, xtile, xhalf * 64:(xhalf + 1) * 64],
                    c.dt_tm[:, h:h + 1], None, ALU.mult), inc="vec")
                v_bh = p.emit("vector", lambda e, slot=slot, i=i: e.tensor_scalar(
                    c.b_hat[:, slot, :], c.b_tm[:], c.ecol[:, i:i + 1], None, ALU.mult),
                    inc="vec")
                last_vread = v_bh
                p.wait("tensor", "vec", v_w)
                p.wait("tensor", "vec", v_chat)
                p.wait("tensor", "vec", v_xt)
                if h % 4 == 0:
                    c.pe_wait(bqname)
                tp = (0, poff)
                p.emit("tensor", lambda e, slot=slot, i=i, bank=bank, qoff=qoff, poff=poff,
                       tp=tp: e.matmul(
                    bank[poff:poff + 64, qoff:qoff + 128], c.xt_tm[:, slot, :],
                    c.wmsk[:, i, :], start=True, stop=False, tile_position=tp))
                mmY = p.emit("tensor", lambda e, i=i, h=h, par=par, bank=bank, qoff=qoff,
                             poff=poff, tp=tp: e.matmul(
                    bank[poff:poff + 64, qoff:qoff + 128], c.hs[:, par, h, :],
                    c.chat[:, i, :], start=False, stop=True, tile_position=tp), inc="mm")
                prev_hg_ymm = mmY
                if h % 4 == 3:
                    p.wait("vector", "mm", mmY)
                    vv = p.emit("vector", lambda e, h=h, bank=bank, ssl=ssl: e.tensor_tensor(
                        c.y_bf[:, (h // 4) * 2:(h // 4) * 2 + 2, ssl],
                        c.dxs[:, (h // 4) * 2:(h // 4) * 2 + 2, :],
                        bank[:, 0:256].rearrange("p (a t) -> p a t", a=2), ALU.add), inc="vec")
                    c.free[bqname] = ("vec", vv)
                bankH = [c.pA, c.pB][h % 2]
                bhname = ["pa", "pb"][h % 2]
                p.wait("tensor", "vec", v_bh)
                c.pe_wait(bhname)
                mmH = p.emit("tensor", lambda e, slot=slot, bankH=bankH: e.matmul(
                    bankH[0:64, 0:64], c.b_hat[:, slot, :], c.xt_tm[:, slot, :],
                    start=True, stop=True), inc="mm")
                xt_free[slot] = mmH
                p.wait("vector", "mm", mmH)
                vv = p.emit("vector", lambda e, h=h, par=par, bankH=bankH: e.scalar_tensor_tensor(
                    c.hs[:, 1 - par, h, :], c.hs[:, par, h, :], c.gcol[:, h:h + 1],
                    bankH[0:64, 0:64], ALU.mult, ALU.add), inc="vec")
                c.free[bhname] = ("vec", vv)
            prev_hg_vread = last_vread


    def emit_dbg(c, v_yn):
        p = c.p
        p.wait("gpsimd", "vec", v_yn)
        for nm, src in [("d_xs", c.xs[:, :, :]), ("d_b", c.Bt[:]), ("d_c", c.Ct[:]),
                        ("d_dt", c.dtf[:]), ("d_y", c.y_bf[:]),
                        ("d_zx", c.xbc[:, :, 3:3 + TG]), ("d_yn", c.y_bf[:])]:
            if nm in c.dbg_t:
                p.dma("gpsimd", c.dbg_t[nm], src, inc="gp")
    return (build, CFG)


def _mod_l0f():

    LSEQ = 2048
    D = 1024
    TG = 512
    NKD = D // 128
    NSEQ = 2
    EPS = 1e-5
    CFG = {"pfx": "", "xin": None, "xout": None}


    class Ctx:
        pass


    def build(nc, dbg=False, ext=None):
        stack = contextlib.ExitStack()
        E = stack.enter_context
        c = Ctx()
        c.nc = nc

        def dram(name, shape, dt, kind="ExternalInput"):
            return nc.dram_tensor(CFG["pfx"] + name, shape, dt, kind=kind).ap()

        c.xT = CFG["xin"] if CFG["xin"] is not None else dram("xT", [NSEQ, D, LSEQ], F32)
        c.kern = dram("kern", [D, D], BF16)
        c.s_in = dram("s_in", [1, 1], F32)
        c.s_k = dram("s_k", [1, 1], F32)
        c.bias8 = dram("bias8", [128, 8], F32)
        c.c_ones_b = dram("c_ones_b", [128, 1], BF16)
        c.c_ones_row = dram("c_ones_row", [1, 128], F32)
        c.c_eps_row = dram("c_eps_row", [1, 1], F32)
        c.x2T = CFG["xout"] if CFG["xout"] is not None else dram("x2T", [NSEQ, D, LSEQ], F32, kind="ExternalOutput")

        def S(name, shape, dt):
            t = E(nc.sbuf_tensor(CFG["pfx"] + name, shape, dt))
            setattr(c, name, t)
            return t

        S("x_fm", [128, NKD, TG], F32)
        S("xb", [128, NKD, TG], BF16)
        S("xq", [128, NKD, TG], FP8)
        S("sq", [128, NKD, TG], BF16)
        S("ybf", [128, NKD, TG], BF16)
        S("s1", [128, NKD, TG], F32)
        S("t1", [128, TG], F32)
        S("r0", [1, TG], F32)
        S("r1", [1, TG], F32)
        S("r2", [1, TG], F32)
        S("kbf", [128, NKD, D], BF16)
        S("kq", [128, NKD, D], FP8)
        S("scl", [1, 4], F32)
        S("sc_col", [128, 3], F32)
        S("bias_s", [128, 8], F32)
        S("onesb_s", [128, 1], BF16)
        S("onesr_s", [1, 128], F32)
        S("epsr_s", [1, 1], F32)

        c.pA = E(nc.psum_tensor(CFG["pfx"] + "pA", [128, 512], F32))
        c.pB = E(nc.psum_tensor(CFG["pfx"] + "pB", [128, 512], F32))
        c.pW = E(nc.psum_tensor(CFG["pfx"] + "pW", [128, 1024], F32))
        c.pP = E(nc.psum_tensor(CFG["pfx"] + "pP", [128, 1024], F32))

        p = ext if ext is not None else Prog(nc, stack)
        c.p = p
        c.free = {}

        def pe_wait(name):
            if name in c.free:
                ctr, v = c.free[name]
                p.wait("tensor", ctr, v)

        c.pe_wait = pe_wait

        for dst, src in [(c.bias_s, c.bias8), (c.onesb_s, c.c_ones_b),
                         (c.onesr_s, c.c_ones_row), (c.epsr_s, c.c_eps_row)]:
            p.dma("sync", dst[:], src, inc="ld2")
        p.dma("sync", c.scl[0:1, 0:1], c.s_in, inc="ld2")
        p.dma("sync", c.scl[0:1, 1:2], c.s_k, inc="ld2")
        p.dma("sync", c.kbf[:], c.kern.rearrange("(k p) f -> p k f", p=128), inc="ld2")
        n_ld = p.count("ld2")
        p.wait("vector", "ld2", n_ld)
        p.emit("vector", lambda e: e.reciprocal(c.scl[0:1, 2:3], c.scl[0:1, 0:1]))
        p.emit("vector", lambda e: e.reciprocal(c.scl[0:1, 3:4], c.scl[0:1, 1:2]))
        p.emit("vector", lambda e: e.tensor_tensor(
            c.scl[0:1, 1:2], c.scl[0:1, 0:1], c.scl[0:1, 1:2], ALU.mult), inc="vec")
        v_scl = p.count("vec")
        p.wait("tensor", "vec", v_scl)
        p.emit("tensor", lambda e: e.matmul(
            c.pW[:, 0:1], c.onesr_s[:], c.scl[0:1, 2:3], start=True, stop=True))
        p.emit("tensor", lambda e: e.matmul(
            c.pW[:, 1:2], c.onesr_s[:], c.scl[0:1, 1:2], start=True, stop=True))
        p.emit("tensor", lambda e: e.matmul(
            c.pW[:, 2:3], c.onesr_s[:], c.scl[0:1, 3:4], start=True, stop=True), inc="mm")
        mm_sc = p.count("mm")
        p.wait("vector", "mm", mm_sc)
        p.emit("vector", lambda e: e.tensor_copy(c.sc_col[:], c.pW[:, 0:3]), inc="vec")
        v_sc = p.count("vec")
        c.free["pw"] = ("vec", v_sc)
        p.wait("scalar", "vec", v_sc)
        p.emit("scalar", lambda e: e.activation(
            c.kq[:].rearrange("p a b -> p (a b)"), c.kbf[:].rearrange("p a b -> p (a b)"),
            AF.Copy, scale=c.sc_col[:, 2:3]), inc="act")
        a_kq = p.count("act")

        c.xfm_free = 0
        c.xb_free_mm = 0
        for b in range(NSEQ):
            for g in range(LSEQ // TG):
                emit_group(c, b, g, a_kq)
        if ext is not None:
            return stack
        p.wait("gpsimd", "st", p.count("st"))
        block = E(nc.Block())
        p.finalize(block)
        stack.close()
        return nc


    def emit_group(c, b, g, a_kq):
        p = c.p
        gsl = slice(g * TG, (g + 1) * TG)
        p.wait("sync", "vec", c.xfm_free)
        ld = p.dma("sync", c.x_fm[:],
                   c.xT[b].rearrange("(k p) t -> p k t", p=128)[:, :, gsl], inc="ld2")
        p.wait("vector", "ld2", ld)
        p.wait("vector", "mm", c.xb_free_mm)
        p.emit("vector", lambda e: e.tensor_copy(c.xb[:], c.x_fm[:]), inc="vec")
        v_xb = p.count("vec")
        p.wait("scalar", "vec", v_xb)
        p.emit("scalar", lambda e: e.activation(
            c.xq[:].rearrange("p a b -> p (a b)"), c.xb[:].rearrange("p a b -> p (a b)"),
            AF.Copy, scale=c.sc_col[:, 0:1]), inc="act")
        a_xq = p.count("act")
        for m in range(NKD):
            bank = [c.pA, c.pB][m % 2]
            bname = ["pa", "pb"][m % 2]
            p.wait("tensor", "act", max(a_kq, a_xq))
            c.pe_wait(bname)
            mm = None
            for k in range(NKD):
                mm = p.emit("tensor", lambda e, bank=bank, m=m, k=k: e.matmul(
                    bank[:], c.kq[:, k, m * 128:(m + 1) * 128], c.xq[:, k, :],
                    start=(k == 0), stop=(k == NKD - 1)), inc=("mm" if k == NKD - 1 else None))
            p.wait("vector", "mm", mm)
            vv = p.emit("vector", lambda e, bank=bank, m=m: e.tensor_scalar(
                c.ybf[:, m, :], bank[:], c.sc_col[:, 1:2], c.bias_s[:, m:m + 1],
                ALU.mult, ALU.add), inc="vec")
            c.free[bname] = ("vec", vv)
        for m in range(NKD):
            p.emit("vector", lambda e, m=m: e.tensor_tensor(
                c.s1[:, m, :], c.x_fm[:, m, :], c.ybf[:, m, :], ALU.add), inc="vec")
        v_s1 = p.count("vec")
        c.xfm_free = v_s1
        p.wait("vector", "act", a_xq)
        p.emit("vector", lambda e: e.tensor_copy(c.xb[:], c.s1[:]), inc="vec")
        v_xb2 = p.count("vec")
        p.wait("scalar", "vec", v_xb2)
        p.emit("scalar", lambda e: e.activation(c.sq[:], c.xb[:], AF.Square), inc="act")
        a_sq = p.count("act")
        p.wait("tensor", "vec", v_xb2)
        p.wait("tensor", "act", a_sq)
        c.pe_wait("pw")
        for k in range(NKD):
            p.emit("tensor", lambda e, k=k: e.matmul(
                c.pW[0:1, 0:512], c.onesb_s[:], c.xb[:, k, :],
                start=(k == 0), stop=(k == NKD - 1)), inc=("mm" if k == NKD - 1 else None))
        for k in range(NKD):
            p.emit("tensor", lambda e, k=k: e.matmul(
                c.pW[0:1, 512:1024], c.onesb_s[:], c.sq[:, k, :],
                start=(k == 0), stop=(k == NKD - 1)), inc=("mm" if k == NKD - 1 else None))
        mm_st = p.count("mm")
        c.xb_free_mm = mm_st
        p.wait("vector", "mm", mm_st)
        p.emit("vector", lambda e: e.tensor_scalar(
            c.r0[:], c.pW[0:1, 0:512], 1.0 / D, None, ALU.mult))
        p.emit("vector", lambda e: e.tensor_tensor(c.r1[:], c.r0[:], c.r0[:], ALU.mult))
        p.emit("vector", lambda e: e.scalar_tensor_tensor(
            c.r2[:], c.pW[0:1, 512:1024], 1.0 / D, c.r1[:], ALU.mult, ALU.subtract), inc="vec")
        v_rows = p.count("vec")
        c.free["pw"] = ("vec", v_rows)
        p.wait("scalar", "vec", v_rows)
        p.emit("scalar", lambda e: e.activation(c.r1[:], c.r2[:], AF.Ln, bias=c.epsr_s[:]))
        p.emit("scalar", lambda e: e.activation(c.r2[:], c.r1[:], AF.Exp, scale=-0.5), inc="act")
        a_rows = p.count("act")
        p.wait("tensor", "act", a_rows)
        c.pe_wait("pp")
        p.emit("tensor", lambda e: e.matmul(
            c.pP[:, 0:512], c.onesr_s[:], c.r0[:], start=True, stop=True))
        p.emit("tensor", lambda e: e.matmul(
            c.pP[:, 512:1024], c.onesr_s[:], c.r2[:], start=True, stop=True), inc="mm")
        mm_bc = p.count("mm")
        p.wait("vector", "mm", mm_bc)
        v_x2 = None
        for k in range(NKD):
            p.emit("vector", lambda e, k=k: e.tensor_tensor(
                c.t1[:], c.s1[:, k, :], c.pP[:, 0:512], ALU.subtract))
            v_x2 = p.emit("vector", lambda e, k=k: e.tensor_tensor(
                c.s1[:, k, :], c.t1[:], c.pP[:, 512:1024], ALU.mult), inc="vec")
        c.free["pp"] = ("vec", v_x2)
        p.wait("sync", "vec", v_x2)
        p.dma("sync", c.x2T[b].rearrange("(k p) t -> p k t", p=128)[:, :, gsl],
              c.s1[:], inc="st")
    return (build, CFG)


def _mod_l1d():

    LSEQ = 2048
    D = 1024
    TG = 512
    CH = 64
    NCH = TG // CH
    NKD = D // 128
    NH = 12
    DK = 64
    DV = 128
    WALL = 4736
    NMT = WALL // 128
    NCV = 24
    NSEQ = 2
    EPS5 = 1e-5
    EPS6 = 1e-6
    LN8 = float(np.log(8.0))
    CFG = {"pfx": "", "xin": None, "xout": None}


    class Ctx:
        pass


    def build(nc, dbg=False, ext=None):
        stack = contextlib.ExitStack()
        E = stack.enter_context
        c = Ctx()
        c.nc = nc

        def dram(name, shape, dt, kind="ExternalInput"):
            return nc.dram_tensor(CFG["pfx"] + name, shape, dt, kind=kind).ap()

        c.xT = CFG["xin"] if CFG["xin"] is not None else dram("xT", [NSEQ, D, LSEQ], F32)
        c.wall = dram("wall", [D, WALL], F32)
        c.woT = dram("woT", [NH * DV, D], F32)
        c.convw = dram("convw", [128, NCV, 4], F32)
        c.convb_z = dram("convb_z", [1, 1], F32)
        c.dtb12 = dram("dtb12", [12, 1], F32)
        c.alog12 = dram("alog12", [12, 1], F32)
        c.bb44 = dram("bb44", [44, 1], F32)
        c.gnw = dram("gnw", [128, 1], F32)
        c.c_ident = dram("c_ident", [128, 128], BF16)
        c.c_ident_h = dram("c_ident_h", [128, 128], FP16)
        c.c_bdones_h = dram("c_bdones_h", [128, 128], FP16)
        c.c_ones_h = dram("c_ones_h", [128, 1], FP16)
        c.c_identf = dram("c_identf", [128, 128], F32)
        c.c_bdones = dram("c_bdones", [128, 128], BF16)
        c.c_ptril_s = dram("c_ptril_s", [128, 128], BF16)
        c.c_ptriu_i = dram("c_ptriu_i", [128, 128], BF16)
        c.c_pdiag = dram("c_pdiag", [128, 128], BF16)
        c.c_selone = dram("c_selone", [12, 12, 128], F32)
        c.c_selp = dram("c_selp", [12, 6, 128], F32)
        c.c_selpb = dram("c_selpb", [12, 6, 64], F32)
        c.c_ones_b = dram("c_ones_b", [128, 1], BF16)
        c.c_ones_row = dram("c_ones_row", [1, 128], F32)
        c.c_eps_row = dram("c_eps_row", [1, 1], F32)
        c.c_eps5col = dram("c_eps5col", [128, 1], F32)
        c.c_eps6col = dram("c_eps6col", [128, 1], F32)
        c.c_one12 = dram("c_one12", [12, 1], F32)
        c.c_mln8 = dram("c_mln8", [128, 1], F32)
        c.x3T = CFG["xout"] if CFG["xout"] is not None else dram("x3T", [NSEQ, D, LSEQ], F32, kind="ExternalOutput")
        c.wall_scr = dram("wall_scr", [NKD, 128, WALL], BF16, kind="Internal")
        c.wo_scr = dram("wo_scr", [NH, 128, D], FP16, kind="Internal")

        def S(name, shape, dt):
            t = E(nc.sbuf_tensor(CFG["pfx"] + name, shape, dt))
            setattr(c, name, t)
            return t

        S("x_fm", [128, NKD, TG], F32)
        S("xb", [128, NKD, TG], BF16)
        S("qkv_bc", [128, NCV, TG + 3], FP16)
        S("qkv_s", [128, NCV, TG], FP16)
        S("qn", [128, 6, TG], FP16)
        S("kn", [128, 6, TG], FP16)
        S("ggf", [128, NH, TG], FP16)
        S("sgt", [128, TG], FP16)
        S("gtt", [128, TG], FP16)
        S("gkb", [44, TG], F32)
        S("sp1", [12, TG], F32)
        S("sp2", [12, TG], F32)
        S("sp3", [12, TG], F32)
        S("stackP", [128, 12], F32)
        S("bPS", [128, 6], F32)
        S("wS", [128, 6], F32)
        S("lgamS", [128, 6, NCH], F32)
        S("gS", [128, 6], F32)
        S("emsk", [128, 2, 128], FP16)
        S("emt", [128, 2, 128], FP16)
        S("amat", [128, 2, 128], FP16)
        S("atmat", [128, 128], FP16)
        S("apow", [128, 2, 6, 128], FP16)
        S("apw", [128, 2, 2, 128], FP16)
        S("ubuf", [128, 2, 2, 128], FP16)
        S("vtm", [128, 2, 128], FP16)
        S("kts", [128, 2, 128], FP16)
        S("rbt", [128, 2, 128], F32)
        S("rbt2", [128, 128], F32)
        S("rbt3", [128, 128], F32)
        S("t5", [128, 2, 128], F32)
        S("t7", [128, 2, 128], F32)
        S("ojunk", [128, 2, 128], FP16)
        S("sscol", [128, 2], F32)
        S("rstdo", [128, 2], F32)
        S("sqv", [128, TG], FP16)
        S("rsqt", [128, TG], F32)
        S("SS", [128, 2, 6, DV], FP16)
        S("wtmp", [128, 6], F32)
        S("o_fm", [128, NH, TG], FP16)
        S("ybf", [128, NKD, TG], BF16)
        S("t1", [128, TG], F32)
        S("r0", [1, TG], F32)
        S("r1", [1, TG], F32)
        S("r2", [1, TG], F32)
        c.wbuf = [S("wbuf0", [128, NKD, 256], BF16), S("wbuf1", [128, NKD, 256], BF16)]
        c.wob = [S("wob0", [128, NH, 128], FP16), S("wob1", [128, NH, 128], FP16)]
        S("convw_s", [128, NCV, 4], F32)
        S("diag", [128, NCV, 4, 128], FP16)
        S("dtb_s", [12, 1], F32)
        S("nA2_s", [12, 1], F32)
        S("bb_s", [44, 1], F32)
        S("gnw_s", [128, 1], F32)
        S("ident_s", [128, 128], BF16)
        S("ident_h", [128, 128], FP16)
        S("bdones_h", [128, 128], FP16)
        S("onesh_s", [128, 1], FP16)
        S("identf_s", [128, 128], F32)
        S("bdones_s", [128, 128], BF16)
        S("ptril_s", [128, 128], BF16)
        S("ptriu_s", [128, 128], BF16)
        S("pdiag_s", [128, 128], BF16)
        S("selone_s", [12, 12, 128], F32)
        S("selp_s", [12, 6, 128], F32)
        S("selpb_s", [12, 6, 64], F32)
        S("onesb_s", [128, 1], BF16)
        S("onesr_s", [1, 128], F32)
        S("epsr_s", [1, 1], F32)
        S("eps5c_s", [128, 1], F32)
        S("eps6c_s", [128, 1], F32)
        S("one12_s", [12, 1], F32)
        S("mln8_s", [128, 1], F32)
        c.s1 = c.qkv_s[:].rearrange("p a b -> p (a b)").bitcast(F32)[:, 0:NKD * TG] \
            .rearrange("p (k t) -> p k t", k=NKD)

        c.pA = E(nc.psum_tensor(CFG["pfx"] + "pA", [128, 512], F32))
        c.pB = E(nc.psum_tensor(CFG["pfx"] + "pB", [128, 512], F32))
        c.pW = E(nc.psum_tensor(CFG["pfx"] + "pW", [128, 1024], F32))
        c.pP = E(nc.psum_tensor(CFG["pfx"] + "pP", [128, 1024], F32))
        c.pY = E(nc.psum_tensor(CFG["pfx"] + "pY", [128, 512], F32))
        c.pZ = E(nc.psum_tensor(CFG["pfx"] + "pZ", [128, 512], F32))
        c.pYb = c.pY[:].bitcast(BF16)
        c.pZb = c.pZ[:].bitcast(BF16)
        c.pYh = c.pY[:].bitcast(FP16)
        c.pZh = c.pZ[:].bitcast(FP16)

        p = ext if ext is not None else Prog(nc, stack)
        c.p = p
        c.free = {}

        def pe_wait(name):
            if name in c.free:
                ctr, v = c.free[name]
                p.wait("tensor", ctr, v)

        c.pe_wait = pe_wait

        prepass(c)
        c.p.wait("tensor", "vec", c.psum_clean)
        c.xfm_free = 0
        c.xb_free_mm = 0
        c.wb_last = {0: 0, 1: 0}
        c.wob_last = {0: 0, 1: 0}
        c.conv_mm_last = 0
        c.scan_end_mm = 0
        c.s1_store = 0

        for b in range(NSEQ):
            p.wait("vector", "mm", c.scan_end_mm)
            p.wait("vector", "st", c.wogn_done)
            p.emit("vector", lambda e: e.memset(c.SS[:], 0.0))
            p.emit("vector", lambda e: e.memset(c.qkv_bc[:, :, 0:3], 0.0), inc="vec")
            for g in range(LSEQ // TG):
                emit_group(c, b, g)
        if ext is not None:
            return stack
        p.wait("gpsimd", "st", p.count("st"))
        block = E(nc.Block())
        p.finalize(block)
        stack.close()
        return nc


    def prepass(c):
        p = c.p
        pairs = [(c.convw_s, c.convw), (c.dtb_s, c.dtb12), (c.nA2_s, c.alog12),
                 (c.bb_s, c.bb44), (c.gnw_s, c.gnw),
                 (c.ident_s, c.c_ident), (c.identf_s, c.c_identf),
                 (c.ident_h, c.c_ident_h), (c.bdones_h, c.c_bdones_h),
                 (c.onesh_s, c.c_ones_h),
                 (c.bdones_s, c.c_bdones), (c.ptril_s, c.c_ptril_s),
                 (c.ptriu_s, c.c_ptriu_i), (c.pdiag_s, c.c_pdiag),
                 (c.selone_s, c.c_selone), (c.selp_s, c.c_selp), (c.selpb_s, c.c_selpb),
                 (c.onesb_s, c.c_ones_b), (c.onesr_s, c.c_ones_row),
                 (c.epsr_s, c.c_eps_row), (c.eps5c_s, c.c_eps5col),
                 (c.eps6c_s, c.c_eps6col), (c.one12_s, c.c_one12),
                 (c.mln8_s, c.c_mln8)]
        for dst, src in pairs:
            p.dma("sync", dst[:], src, inc="ld2")
        n_small = p.count("ld2")
        p.dma("gpsimd", c.wall_scr, c.wall.rearrange("(k p) f -> k p f", p=128), inc="ldp")
        p.dma("gpsimd", c.wo_scr, c.woT.rearrange("(k p) f -> k p f", p=128), inc="ldp")
        c.wscr_done = p.count("ldp")
        p.wait("scalar", "ld2", n_small)
        p.emit("scalar", lambda e: e.activation(c.sp1[:, 0:1], c.nA2_s[:], AF.Exp))
        p.emit("scalar", lambda e: e.mul(c.nA2_s[:], c.sp1[:, 0:1], -1.0), inc="act")
        c.nA_ready = p.count("act")
        p.wait("vector", "ld2", n_small)
        for pb_ in [c.pA, c.pB, c.pW, c.pP, c.pY, c.pZ]:
            p.emit("vector", lambda e, pb_=pb_: e.memset(pb_[:], 0.0))
        p.emit("vector", lambda e: e.memset(c.kts[:], 0.0))
        p.emit("vector", lambda e: e.memset(c.gkb[:], 0.0))
        p.emit("vector", lambda e: e.engine_nop(), inc="vec")
        c.psum_clean = p.count("vec")
        for i in range(NCV):
            for j in range(4):
                p.emit("vector", lambda e, i=i, j=j: e.tensor_scalar(
                    c.diag[:, i, j, :], c.ident_s[:], c.convw_s[:, i, j:j + 1], None, ALU.mult))
        p.emit("vector", lambda e: e.engine_nop(), inc="vec")
        c.prep_v = p.count("vec")
        p.wait("sync", "ldp", c.wscr_done)
        woview = c.qkv_s[:].rearrange("p a b -> p (a b)")[:, 0:NH * D].rearrange(
            "p (k f) -> p k f", k=NH)
        wo2view = c.qkv_bc[:].rearrange("p a b -> p (a b)")[:, 0:NH * D].rearrange(
            "p (k f) -> p k f", k=NH)
        lw = p.dma("sync", woview, c.wo_scr.rearrange("k p f -> p k f"), inc="ld2")
        p.wait("vector", "ld2", lw)
        vv = None
        for k in range(NH):
            vv = p.emit("vector", lambda e, k=k, woview=woview, wo2view=wo2view:
                        e.tensor_scalar(wo2view[:, k, :], woview[:, k, :],
                                        c.gnw_s[:], None, ALU.mult), inc="vec")
        p.wait("sync", "vec", vv)
        p.dma("sync", c.wo_scr.rearrange("k p f -> p k f"), wo2view, inc="st")
        c.wogn_done = p.count("st")
        c.wogn_vec = vv


    def emit_group(c, b, g):
        p = c.p
        gsl = slice(g * TG, (g + 1) * TG)

        p.wait("sync", "vec", c.xfm_free)
        ld = p.dma("sync", c.x_fm[:],
                   c.xT[b].rearrange("(k p) t -> p k t", p=128)[:, :, gsl], inc="ld2")
        p.wait("vector", "ld2", ld)
        p.wait("vector", "mm", c.xb_free_mm)
        p.emit("vector", lambda e: e.tensor_copy(c.xb[:], c.x_fm[:]), inc="vec")
        v_xb = p.count("vec")

        if not (g == 0):
            p.wait("vector", "mm", c.conv_mm_last)
            p.emit("vector", lambda e: e.tensor_copy(
                c.qkv_bc[:, :, 0:3], c.qkv_bc[:, :, TG:TG + 3]), inc="vec")
        v_halo = p.count("vec")

        first = (b == 0 and g == 0)
        pend = {}
        for mp in range(19):
            p.wait("gpsimd", "mm", c.wb_last[mp % 2])
            if first and mp == 0:
                p.wait("gpsimd", "ldp", c.wscr_done)
            fs, fe = mp * 256, min(mp * 256 + 256, WALL)
            pend[mp] = p.dma("gpsimd", c.wbuf[mp % 2][:, :, 0:fe - fs],
                             c.wall_scr[:, :, fs:fe].rearrange("k p f -> p k f"),
                             inc="wb%d" % (mp % 2))
            if True:
                for m in (2 * mp, 2 * mp + 1):
                    if m >= NMT:
                        continue
                    off = (m % 2) * 128
                    bank = [c.pA, c.pB][m % 2]
                    bname = ["pa", "pb"][m % 2]
                    p.wait("tensor", "wb%d" % (mp % 2), pend[mp])
                    p.wait("tensor", "vec", v_xb)
                    c.pe_wait(bname)
                    mm = None
                    for k in range(NKD):
                        mm = p.emit("tensor", lambda e, bank=bank, mp=mp, k=k, off=off: e.matmul(
                            bank[:], c.wbuf[mp % 2][:, k, off:off + 128], c.xb[:, k, :],
                            start=(k == 0), stop=(k == NKD - 1)),
                            inc=("mm" if k == NKD - 1 else None))
                    c.wb_last[mp % 2] = mm
                    if m < NCV:
                        p.wait("vector", "mm", mm)
                        p.wait("vector", "vec", v_halo)
                        vv = p.emit("vector", lambda e, bank=bank, m=m: e.tensor_copy(
                            c.qkv_bc[:, m, 3:3 + TG], bank[:]), inc="vec")
                        c.free[bname] = ("vec", vv)
                    elif m < 36:
                        gi = m - 24
                        p.wait("vector", "mm", mm)
                        vv = p.emit("vector", lambda e, bank=bank: e.tensor_copy(
                            c.gtt[:], bank[:]), inc="vec")
                        p.wait("scalar", "vec", vv)
                        aa = p.emit("scalar", lambda e, bank=bank: e.activation(
                            c.sgt[:], bank[:], AF.Sigmoid), inc="act")
                        c.free[bname] = ("act", aa)
                        p.wait("vector", "act", aa)
                        vv = p.emit("vector", lambda e, gi=gi: e.tensor_tensor(
                            c.ggf[:, gi, :], c.gtt[:], c.sgt[:], ALU.mult), inc="vec")
                    else:
                        p.wait("vector", "mm", mm)
                        vv = p.emit("vector", lambda e, bank=bank: e.tensor_scalar(
                            c.gkb[0:12, :], bank[0:12, :], c.dtb_s[:], None, ALU.add),
                            inc="vec")
                        p.wait("scalar", "vec", vv)
                        aa = p.emit("scalar", lambda e, bank=bank: e.activation(
                            c.gkb[32:44, :], bank[32:44, :], AF.Sigmoid,
                            bias=c.bb_s[32:44, 0:1]), inc="act")
                        c.free[bname] = ("act", aa)
        v_qkvbc = p.count("vec")
        a_beta = p.count("act")

        for i in range(NCV):
            bank = [c.pA, c.pB][i % 2]
            bname = ["pa", "pb"][i % 2]
            p.wait("tensor", "vec", v_qkvbc)
            c.pe_wait(bname)
            mm = None
            for j in range(4):
                mm = p.emit("tensor", lambda e, bank=bank, i=i, j=j: e.matmul(
                    bank[:], c.diag[:, i, j, :], c.qkv_bc[:, i, j:j + TG],
                    start=(j == 0), stop=(j == 3)), inc=("mm" if j == 3 else None))
            c.conv_mm_last = mm
            p.wait("scalar", "mm", mm)
            p.wait("scalar", "mm", c.scan_end_mm)
            p.wait("scalar", "st", c.s1_store)
            aa = p.emit("scalar", lambda e, bank=bank, i=i: e.activation(
                c.qkv_s[:, i, :], bank[:], AF.Silu), inc="act")
            c.free[bname] = ("act", aa)
        a_conv = p.count("act")

        prev_mm = 0
        prev_tt = 0
        for i in range(12):
            p.wait("scalar", "act", a_conv)
            p.wait("scalar", "mm", prev_mm)
            p.wait("scalar", "vec", prev_tt)
            p.emit("scalar", lambda e, i=i: e.activation(c.sqv[:], c.qkv_s[:, i, :],
                                                         AF.Square), inc="act")
            a_sq = p.count("act")
            bank = [c.pA, c.pB][i % 2]
            bname = ["pa", "pb"][i % 2]
            p.wait("tensor", "act", a_sq)
            c.pe_wait(bname)
            mm = p.emit("tensor", lambda e, bank=bank: e.matmul(
                bank[:], c.bdones_h[:], c.sqv[:], start=True, stop=True), inc="mm")
            prev_mm = mm
            p.wait("scalar", "mm", mm)
            p.emit("scalar", lambda e, bank=bank: e.activation(
                c.t1[:], bank[:], AF.Ln, bias=c.eps6c_s[:]))
            if i < 6:
                aa = p.emit("scalar", lambda e: e.activation(
                    c.rsqt[:], c.t1[:], AF.Exp, scale=-0.5, bias=c.mln8_s[:]), inc="act")
            else:
                aa = p.emit("scalar", lambda e: e.activation(
                    c.rsqt[:], c.t1[:], AF.Exp, scale=-0.5), inc="act")
            c.free[bname] = ("act", aa)
            p.wait("vector", "act", aa)
            dst = c.qn if i < 6 else c.kn
            prev_tt = p.emit("vector", lambda e, i=i, dst=dst: e.tensor_tensor(
                dst[:, i % 6, :], c.qkv_s[:, i, :], c.rsqt[:], ALU.mult), inc="vec")
        v_qkn = p.count("vec")

        p.wait("scalar", "vec", p.count("vec"))
        p.wait("scalar", "act", a_beta)
        p.emit("scalar", lambda e: e.activation(c.sp1[:], c.gkb[0:12, :], AF.Abs))
        p.emit("scalar", lambda e: e.activation(c.sp2[:], c.sp1[:], AF.Exp, scale=-1.0))
        p.emit("scalar", lambda e: e.activation(c.sp1[:], c.sp2[:], AF.Ln, bias=c.one12_s[:]))
        p.emit("scalar", lambda e: e.activation(c.sp2[:], c.gkb[0:12, :], AF.Relu), inc="act")
        a_sp = p.count("act")
        p.wait("vector", "act", a_sp)
        p.wait("vector", "act", c.nA_ready)
        p.emit("vector", lambda e: e.tensor_tensor(c.sp3[:], c.sp2[:], c.sp1[:], ALU.add))
        p.emit("vector", lambda e: e.tensor_scalar(
            c.sp1[:], c.sp3[:], c.nA2_s[:], None, ALU.mult), inc="vec")
        for ch in range(NCH):
            p.emit("vector", lambda e, ch=ch: e.tensor_tensor_scan(
                c.gkb[0:12, ch * CH:(ch + 1) * CH],
                c.one12_s[:].broadcast_to([12, CH]),
                c.sp1[:, ch * CH:(ch + 1) * CH], 0.0, ALU.mult, ALU.add), inc="vec")
        v_cum = p.count("vec")
        p.wait("tensor", "vec", v_cum)
        c.pe_wait("pw")
        mm = None
        for j in range(6):
            mm = p.emit("tensor", lambda e, j=j: e.matmul(
                c.pW[:, j * 8:(j + 1) * 8], c.selp_s[:, j, :],
                c.gkb[0:12, CH - 1::CH], start=True, stop=True), inc="mm")
        p.wait("vector", "mm", mm)
        p.emit("vector", lambda e: e.tensor_copy(
            c.lgamS[:].rearrange("p a b -> p (a b)"), c.pW[:, 0:48]), inc="vec")
        v_lg = p.count("vec")
        c.free["pw"] = ("vec", v_lg)

        for ch in range(NCH):
            emit_chunk(c, b, g, ch, v_cum, v_qkn)
        c.scan_end_mm = p.count("mm")

        wo_ld = {}

        def load_wo(m):
            p.wait("gpsimd", "mm", c.wob_last[m % 2])
            if first and m == 0:
                p.wait("gpsimd", "st", c.wogn_done)
            wo_ld[m] = p.dma("gpsimd", c.wob[m % 2][:],
                             c.wo_scr[:, :, m * 128:(m + 1) * 128].rearrange("k p f -> p k f"),
                             inc="wo%d" % (m % 2))

        load_wo(0)
        load_wo(1)
        v_ofm = p.count("vec")
        for m in range(NKD):
            bank = [c.pA, c.pB][m % 2]
            bname = ["pa", "pb"][m % 2]
            p.wait("tensor", "wo%d" % (m % 2), wo_ld[m])
            p.wait("tensor", "vec", v_ofm)
            c.pe_wait(bname)
            mm = None
            for k in range(NH):
                mm = p.emit("tensor", lambda e, bank=bank, m=m, k=k: e.matmul(
                    bank[:], c.wob[m % 2][:, k, :], c.o_fm[:, k, :],
                    start=(k == 0), stop=(k == NH - 1)), inc=("mm" if k == NH - 1 else None))
            c.wob_last[m % 2] = mm
            c.xb_free_mm = mm
            if m + 2 < NKD:
                load_wo(m + 2)
            p.wait("vector", "mm", mm)
            vv = p.emit("vector", lambda e, bank=bank, m=m: e.tensor_tensor(
                c.s1[:, m, :], c.x_fm[:, m, :], bank[:], ALU.add), inc="vec")
            c.free[bname] = ("vec", vv)
        v_s1 = p.count("vec")
        c.xfm_free = v_s1

        p.emit("vector", lambda e: e.tensor_copy(c.xb[:], c.s1[:]), inc="vec")
        v_xb2 = p.count("vec")
        p.wait("scalar", "vec", v_xb2)
        p.wait("scalar", "mm", c.scan_end_mm)
        sqview = c.ggf[:].rearrange("p a b -> p (a b)")[:, 0:NKD * TG].rearrange(
            "p (k t) -> p k t", k=NKD)
        p.emit("scalar", lambda e, sqview=sqview: e.activation(
            sqview, c.xb[:], AF.Square), inc="act")
        a_sq2 = p.count("act")
        p.wait("tensor", "vec", v_xb2)
        p.wait("tensor", "act", a_sq2)
        c.pe_wait("pw")
        for k in range(NKD):
            p.emit("tensor", lambda e, k=k: e.matmul(
                c.pW[0:1, 0:512], c.onesb_s[:], c.xb[:, k, :],
                start=(k == 0), stop=(k == NKD - 1)), inc=("mm" if k == NKD - 1 else None))
        for k in range(NKD):
            p.emit("tensor", lambda e, k=k, sqview=sqview: e.matmul(
                c.pW[0:1, 512:1024], c.onesh_s[:], sqview[:, k, :],
                start=(k == 0), stop=(k == NKD - 1)), inc=("mm" if k == NKD - 1 else None))
        mm_st = p.count("mm")
        p.wait("vector", "mm", mm_st)
        p.emit("vector", lambda e: e.tensor_scalar(
            c.r0[:], c.pW[0:1, 0:512], 1.0 / D, None, ALU.mult))
        p.emit("vector", lambda e: e.tensor_tensor(c.r1[:], c.r0[:], c.r0[:], ALU.mult))
        p.emit("vector", lambda e: e.scalar_tensor_tensor(
            c.r2[:], c.pW[0:1, 512:1024], 1.0 / D, c.r1[:], ALU.mult, ALU.subtract), inc="vec")
        v_rows = p.count("vec")
        c.free["pw"] = ("vec", v_rows)
        p.wait("scalar", "vec", v_rows)
        p.emit("scalar", lambda e: e.activation(c.r1[:], c.r2[:], AF.Ln, bias=c.epsr_s[:]))
        p.emit("scalar", lambda e: e.activation(c.r2[:], c.r1[:], AF.Exp, scale=-0.5), inc="act")
        a_rows = p.count("act")
        p.wait("tensor", "act", a_rows)
        c.pe_wait("pp")
        p.emit("tensor", lambda e: e.matmul(
            c.pP[:, 0:512], c.onesr_s[:], c.r0[:], start=True, stop=True))
        p.emit("tensor", lambda e: e.matmul(
            c.pP[:, 512:1024], c.onesr_s[:], c.r2[:], start=True, stop=True), inc="mm")
        mm_bc = p.count("mm")
        p.wait("vector", "mm", mm_bc)
        v_x3 = None
        for k in range(NKD):
            p.emit("vector", lambda e, k=k: e.tensor_tensor(
                c.t1[:], c.s1[:, k, :], c.pP[:, 0:512], ALU.subtract))
            v_x3 = p.emit("vector", lambda e, k=k: e.tensor_tensor(
                c.s1[:, k, :], c.t1[:], c.pP[:, 512:1024], ALU.mult), inc="vec")
        c.free["pp"] = ("vec", v_x3)
        p.wait("sync", "vec", v_x3)
        p.dma("sync", c.x3T[b].rearrange("(k p) t -> p k t", p=128)[:, :, gsl],
              c.s1[:], inc="st")
        c.s1_store = p.count("st")


    def emit_chunk(c, b, g, ch, v_cum, v_qkn):
        p = c.p
        chsl = slice(ch * CH, (ch + 1) * CH)
        par = (g * NCH + ch) % 2
        Sr = c.SS[:, par, :, :]
        Sw = c.SS[:, 1 - par, :, :]

        c.pe_wait("py")
        p.wait("tensor", "vec", v_cum)
        p.emit("tensor", lambda e: e.matmul(
            c.pY[0:64, 0:44], c.gkb[:, chsl], c.identf_s[0:44, 0:44],
            start=True, stop=True, tile_position=(0, 0)))
        mm = p.emit("tensor", lambda e: e.matmul(
            c.pY[64:128, 0:44], c.gkb[:, chsl], c.identf_s[0:44, 0:44],
            start=True, stop=True, tile_position=(0, 64)), inc="mm")
        p.wait("vector", "mm", mm)
        p.wait("vector", "vec", getattr(c, "chunk_shared_free", 0))
        p.wait("vector", "act", getattr(c, "chunk_shared_act", 0))
        for half, r0 in ((0, 0), (1, 64)):
            pv = c.pY[r0:r0 + 64, 0:12].rearrange("p (a two) -> p a two", two=2)
            pv2 = c.pY[r0:r0 + 64, 32:44].rearrange("p (a two) -> p a two", two=2)
            p.emit("vector", lambda e, pv=pv, r0=r0, half=half: e.tensor_copy(
                c.stackP[r0:r0 + 64, 0:6].unsqueeze(2), pv[:, :, half:half + 1]))
            p.emit("vector", lambda e, pv2=pv2, r0=r0, half=half: e.tensor_copy(
                c.stackP[r0:r0 + 64, 6:12].unsqueeze(2), pv2[:, :, half:half + 1]))
        p.emit("vector", lambda e, ch=ch: e.tensor_tensor(
            c.wtmp[:], c.lgamS[:, :, ch], c.stackP[:, 0:6], ALU.subtract), inc="vec")
        v_stk = p.count("vec")
        c.free["py"] = ("vec", v_stk)
        p.wait("scalar", "vec", v_stk)
        p.wait("scalar", "mm", getattr(c, "pair_mm_last", 0))
        p.emit("scalar", lambda e: e.activation(c.bPS[:], c.stackP[:, 0:6], AF.Exp))
        p.emit("scalar", lambda e: e.activation(c.wS[:], c.wtmp[:], AF.Exp))
        p.emit("scalar", lambda e, ch=ch: e.activation(
            c.gS[:], c.lgamS[:, :, ch], AF.Exp), inc="act")
        a_shared = p.count("act")

        if not hasattr(c, "par_end_mm"):
            c.par_end_mm = {0: 0, 1: 0}
            c.par_end_act = {0: 0, 1: 0}
            c.par_t5_read = {0: 0, 1: 0}
            c.par_t7_read = {0: 0, 1: 0}
        for j in range(6):
            jp = j % 2
            emsk = c.emsk[:, jp, :]
            emt = c.emt[:, jp, :]
            amat = c.amat[:, jp, :]
            apow = c.apow[:, jp, :, :]
            apw = c.apw[:, jp, :, :]
            ubuf = c.ubuf[:, jp, :, :]
            vtm = c.vtm[:, jp, :]
            kts = c.kts[:, jp, :]
            rbt = c.rbt[:, jp, :]
            t5 = c.t5[:, jp, :]
            t7 = c.t7[:, jp, :]
            ojunk = c.ojunk[:, jp, :]
            sscol = c.sscol[:, jp:jp + 1]
            rstdo = c.rstdo[:, jp:jp + 1]
            pmm = c.par_end_mm[jp]
            pact = c.par_end_act[jp]
            p.wait("vector", "mm", pmm)
            p.wait("vector", "act", pact)
            p.wait("scalar", "mm", pmm)
            c.pe_wait("pw2")
            p.wait("tensor", "vec", v_cum)
            p.emit("tensor", lambda e, j=j, chsl=chsl: e.matmul(
                c.pW[:, 0:64], c.selone_s[:, 2 * j, :], c.gkb[0:12, chsl],
                start=True, stop=True))
            mm = p.emit("tensor", lambda e, j=j, chsl=chsl: e.matmul(
                c.pW[:, 64:128], c.selone_s[:, 2 * j + 1, :], c.gkb[0:12, chsl],
                start=True, stop=True), inc="mm")
            p.wait("vector", "mm", mm)
            vv = p.emit("vector", lambda e, j=j, rbt=rbt: e.tensor_scalar(
                rbt, c.pW[:, 0:128], c.stackP[:, j:j + 1], None, ALU.subtract),
                inc="vec")
            c.free["pw2"] = ("vec", vv)
            p.wait("scalar", "vec", vv)
            p.emit("scalar", lambda e, t5=t5, rbt=rbt: e.activation(t5, rbt, AF.Relu))
            a_e = p.emit("scalar", lambda e, emsk=emsk, t5=t5: e.activation(
                emsk, t5, AF.Exp, scale=-1.0), inc="act")
            c.pe_wait("pw3")
            p.wait("tensor", "vec", v_qkn)
            p.emit("tensor", lambda e, j=j, chsl=chsl: e.matmul(
                c.pW[0:64, 512:576], c.kn[0:64, j, chsl], c.kn[0:64, j, chsl],
                start=True, stop=True, tile_position=(0, 0)))
            mm = p.emit("tensor", lambda e, j=j, chsl=chsl: e.matmul(
                c.pW[64:128, 576:640], c.kn[64:128, j, chsl], c.kn[64:128, j, chsl],
                start=True, stop=True, tile_position=(64, 64)), inc="mm")
            p.wait("vector", "mm", mm)
            p.wait("vector", "act", a_e)
            p.emit("vector", lambda e, ojunk=ojunk, emsk=emsk: e.tensor_tensor(
                ojunk, emsk, c.pW[:, 512:640], ALU.mult))
            vv = p.emit("vector", lambda e, j=j, amat=amat, ojunk=ojunk:
                        e.scalar_tensor_tensor(
                amat, ojunk, c.stackP[:, 6 + j:7 + j], c.ptril_s[:],
                ALU.mult, ALU.mult), inc="vec")
            c.free["pw3"] = ("vec", vv)
            v_amat = vv
            c.pe_wait("py")
            p.wait("tensor", "vec", v_amat)
            mm = p.emit("tensor", lambda e, amat=amat: e.transpose(
                c.pYh[:, 0:128], amat, c.ident_h[:]), inc="mm")
            p.wait("vector", "mm", mm)
            vv = p.emit("vector", lambda e, apow=apow: e.tensor_copy(
                apow[:, 0, :], c.pYh[:, 0:128]), inc="vec")
            c.free["py"] = ("vec", vv)
            v_pow = [vv]
            c.pe_wait("pz")
            p.emit("tensor", lambda e, emsk=emsk: e.transpose(
                c.pZh[:, 0:128], emsk, c.ident_h[:]), inc="mm")
            mm_emt = p.count("mm")
            p.wait("vector", "mm", mm_emt)
            v_emt = p.emit("vector", lambda e, emt=emt: e.tensor_tensor(
                emt, c.pZh[:, 0:128], c.ptriu_s[:], ALU.mult), inc="vec")
            c.free["pz"] = ("vec", v_emt)
            cur = amat
            v_cur = v_amat
            for lv in range(1, 6):
                bank = [c.pA, c.pB][lv % 2]
                bname = ["pa", "pb"][lv % 2]
                c.pe_wait(bname)
                p.wait("tensor", "vec", max(v_pow[lv - 1], v_cur))
                mm = p.emit("tensor", lambda e, lv=lv, bank=bank, cur=cur, apow=apow:
                            e.matmul(
                    bank[:, 0:128], apow[:, lv - 1, :], cur,
                    start=True, stop=True), inc="mm")
                p.wait("vector", "mm", mm)
                v_cur = p.emit("vector", lambda e, lv=lv, bank=bank, apw=apw:
                               e.tensor_copy(
                    apw[:, lv % 2, :], bank[:, 0:128]), inc="vec")
                c.free[bname] = ("vec", v_cur)
                c.pe_wait("py")
                p.wait("tensor", "vec", v_cur)
                mm = p.emit("tensor", lambda e, lv=lv, apw=apw: e.transpose(
                    c.pYh[:, 0:128], apw[:, lv % 2, :], c.ident_h[:]), inc="mm")
                p.wait("vector", "mm", mm)
                vv = p.emit("vector", lambda e, lv=lv, apow=apow: e.tensor_copy(
                    apow[:, lv, :], c.pYh[:, 0:128]), inc="vec")
                c.free["py"] = ("vec", vv)
                v_pow.append(vv)
                cur = apw[:, lv % 2, :]
            c.pe_wait("pz")
            p.emit("tensor", lambda e, j=j, chsl=chsl: e.matmul(
                c.pZ[0:64, 0:128], c.qkv_s[:, 12 + 2 * j, chsl], c.ident_h[:],
                start=True, stop=True, tile_position=(0, 0)))
            mm = p.emit("tensor", lambda e, j=j, chsl=chsl: e.matmul(
                c.pZ[64:128, 0:128], c.qkv_s[:, 12 + 2 * j + 1, chsl], c.ident_h[:],
                start=True, stop=True, tile_position=(0, 64)), inc="mm")
            p.wait("vector", "mm", mm)
            v_vtm = p.emit("vector", lambda e, vtm=vtm: e.tensor_copy(
                vtm, c.pZ[:, 0:128]), inc="vec")
            c.free["pz"] = ("vec", v_vtm)
            c.pe_wait("py")
            p.emit("tensor", lambda e, j=j, chsl=chsl: e.matmul(
                c.pY[0:64, 0:64], c.kn[0:64, j, chsl], c.ident_h[0:64, 0:64],
                start=True, stop=True, tile_position=(0, 0)))
            mm = p.emit("tensor", lambda e, j=j, chsl=chsl: e.matmul(
                c.pY[64:128, 64:128], c.kn[64:128, j, chsl], c.ident_h[64:128, 64:128],
                start=True, stop=True, tile_position=(64, 64)), inc="mm")
            p.wait("vector", "mm", mm)
            p.wait("vector", "act", a_shared)
            p.emit("vector", lambda e, j=j, kts=kts: e.tensor_scalar(
                kts[0:64, 0:64], c.pY[0:64, 0:64], c.wS[0:64, j:j + 1], None, ALU.mult))
            v_kts = p.emit("vector", lambda e, j=j, kts=kts: e.tensor_scalar(
                kts[64:128, 64:128], c.pY[64:128, 64:128], c.wS[64:128, j:j + 1], None,
                ALU.mult), inc="vec")
            c.free["py"] = ("vec", v_kts)
            c.pe_wait("pp4")
            p.emit("tensor", lambda e, j=j, chsl=chsl, Sr=Sr: e.matmul(
                c.pP[0:64, 0:128], c.kn[0:64, j, chsl], Sr[0:64, j, :],
                start=True, stop=True, tile_position=(0, 0)))
            mm = p.emit("tensor", lambda e, j=j, chsl=chsl, Sr=Sr: e.matmul(
                c.pP[64:128, 0:128], c.kn[64:128, j, chsl], Sr[64:128, j, :],
                start=True, stop=True, tile_position=(64, 64)), inc="mm")
            p.wait("vector", "mm", mm)
            p.emit("vector", lambda e, j=j, rbt=rbt: e.tensor_scalar(
                rbt, c.pP[:, 0:128], c.bPS[:, j:j + 1], None, ALU.mult))
            p.emit("vector", lambda e, t5=t5, vtm=vtm, rbt=rbt: e.tensor_tensor(
                t5, vtm, rbt, ALU.subtract))
            vv = p.emit("vector", lambda e, j=j, ubuf=ubuf, t5=t5: e.tensor_scalar(
                ubuf[:, 0, :], t5, c.stackP[:, 6 + j:7 + j], None, ALU.mult),
                inc="vec")
            c.free["pp4"] = ("vec", vv)
            v_u = vv
            ucur = 0
            for idx, lv in enumerate([5, 4, 3, 2, 1, 0]):
                bank = [c.pA, c.pB][idx % 2]
                bname = ["pa", "pb"][idx % 2]
                c.pe_wait(bname)
                p.wait("tensor", "vec", max(v_pow[lv], v_u))
                mm = p.emit("tensor", lambda e, lv=lv, bank=bank, ucur=ucur, apow=apow,
                            ubuf=ubuf: e.matmul(
                    bank[:, 0:128], apow[:, lv, :], ubuf[:, ucur, :],
                    start=True, stop=True), inc="mm")
                p.wait("vector", "mm", mm)
                op = ALU.add if lv > 0 else ALU.subtract
                v_u = p.emit("vector", lambda e, bank=bank, ucur=ucur, op=op, ubuf=ubuf:
                             e.tensor_tensor(
                    ubuf[:, 1 - ucur, :], ubuf[:, ucur, :], bank[:, 0:128], op),
                    inc="vec")
                c.free[bname] = ("vec", v_u)
                ucur = 1 - ucur
            c.pe_wait("pw3")
            p.emit("tensor", lambda e, j=j, chsl=chsl: e.matmul(
                c.pW[0:64, 512:576], c.kn[0:64, j, chsl], c.qn[0:64, j, chsl],
                start=True, stop=True, tile_position=(0, 0)))
            mm = p.emit("tensor", lambda e, j=j, chsl=chsl: e.matmul(
                c.pW[64:128, 576:640], c.kn[64:128, j, chsl], c.qn[64:128, j, chsl],
                start=True, stop=True, tile_position=(64, 64)), inc="mm")
            p.wait("vector", "mm", mm)
            p.wait("vector", "vec", v_emt)
            v_mq = p.emit("vector", lambda e, ojunk=ojunk, emt=emt: e.tensor_tensor(
                ojunk, emt, c.pW[:, 512:640], ALU.mult), inc="vec")
            c.free["pw3"] = ("vec", v_mq)
            c.pe_wait("pp4")
            p.wait("tensor", "vec", max(v_u, v_mq))
            p.emit("tensor", lambda e, ucur=ucur, ojunk=ojunk, ubuf=ubuf: e.matmul(
                c.pP[:, 256:384], ojunk, ubuf[:, ucur, :], start=True, stop=True))
            p.emit("tensor", lambda e, j=j, chsl=chsl, Sr=Sr: e.matmul(
                c.pP[0:64, 128:256], c.qn[0:64, j, chsl], Sr[0:64, j, :],
                start=True, stop=True, tile_position=(0, 0)))
            mm = p.emit("tensor", lambda e, j=j, chsl=chsl, Sr=Sr: e.matmul(
                c.pP[64:128, 128:256], c.qn[64:128, j, chsl], Sr[64:128, j, :],
                start=True, stop=True, tile_position=(64, 64)), inc="mm")
            p.wait("vector", "mm", mm)
            p.emit("vector", lambda e, j=j, rbt=rbt: e.tensor_scalar(
                rbt, c.pP[:, 128:256], c.bPS[:, j:j + 1], None, ALU.mult))
            p.emit("vector", lambda e, t7=t7, rbt=rbt: e.tensor_tensor(
                t7, c.pP[:, 256:384], rbt, ALU.add))
            vv = p.emit("vector", lambda e, ojunk=ojunk, t7=t7, sscol=sscol:
                        e.scalar_tensor_tensor(
                ojunk, t7, 1.0, t7, ALU.mult, ALU.mult,
                accum_out=sscol), inc="vec")
            c.free["pp4"] = ("vec", vv)
            p.wait("scalar", "vec", vv)
            p.emit("scalar", lambda e, rstdo=rstdo, sscol=sscol: e.activation(
                rstdo, sscol, AF.Ln, scale=1.0 / DV, bias=c.eps5c_s[:]))
            a_r = p.emit("scalar", lambda e, rstdo=rstdo, sscol=sscol: e.activation(
                sscol, rstdo, AF.Exp, scale=-0.5), inc="act")
            p.wait("vector", "act", a_r)
            v_ofin = p.emit("vector", lambda e, emsk=emsk, t7=t7, sscol=sscol:
                            e.tensor_scalar(
                emsk, t7, sscol, None, ALU.mult), inc="vec")
            c.pe_wait("pz")
            p.wait("tensor", "vec", v_ofin)
            mm = p.emit("tensor", lambda e, emsk=emsk: e.transpose(
                c.pZh[:, 0:128], emsk, c.ident_h[:]), inc="mm")
            p.wait("vector", "mm", mm)
            vv = p.emit("vector", lambda e, j=j, chsl=chsl: e.tensor_tensor(
                c.o_fm[:, 2 * j:2 * j + 2, chsl],
                c.pZh[:, 0:128].rearrange("p (a t) -> p a t", a=2),
                c.ggf[:, 2 * j:2 * j + 2, chsl], ALU.mult), inc="vec")
            c.free["pz"] = ("vec", vv)
            c.pe_wait("pp5")
            p.wait("tensor", "vec", max(v_kts, v_u))
            p.emit("tensor", lambda e, ucur=ucur, kts=kts, ubuf=ubuf: e.matmul(
                c.pP[0:64, 512:640], kts[:, 0:64], ubuf[:, ucur, :],
                start=True, stop=True, tile_position=(0, 0)))
            mm = p.emit("tensor", lambda e, ucur=ucur, kts=kts, ubuf=ubuf: e.matmul(
                c.pP[64:128, 512:640], kts[:, 64:128], ubuf[:, ucur, :],
                start=True, stop=True, tile_position=(0, 64)), inc="mm")
            p.wait("vector", "mm", mm)
            p.emit("vector", lambda e, j=j, Sr=Sr, Sw=Sw: e.scalar_tensor_tensor(
                Sw[0:64, j, :], Sr[0:64, j, :], c.gS[0:64, j:j + 1],
                c.pP[0:64, 512:640], ALU.mult, ALU.add))
            vv = p.emit("vector", lambda e, j=j, Sr=Sr, Sw=Sw: e.scalar_tensor_tensor(
                Sw[64:128, j, :], Sr[64:128, j, :], c.gS[64:128, j:j + 1],
                c.pP[64:128, 512:640], ALU.mult, ALU.add), inc="vec")
            c.free["pp5"] = ("vec", vv)
            c.par_end_mm[jp] = p.count("mm")
            c.par_end_act[jp] = p.count("act")
            c.pair_mm_last = p.count("mm")
        c.chunk_shared_free = p.count("vec")
        c.chunk_shared_act = p.count("act")
    return (build, CFG)


def _mod_l1f():

    LSEQ = 2048
    D = 1024
    DH = 4096
    TG = 512
    NKD = D // 128
    NHT = DH // 128
    NSEQ = 2
    DSL = 256
    NSL = DH // DH
    EPS = 1e-5
    CFG = {"pfx": "", "xin": None, "xout": None}


    class Ctx:
        pass


    def host_fold_inputs(up_w, down_w):
        import ml_dtypes

        def stage_mat(w, s, transpose2x2):
            S = np.zeros((32, 128, 128), np.float32)
            stride = 2 ** s
            for i in range(DH):
                k = (i // (2 * stride)) * stride + (i % stride)
                bit = (i // stride) % 2
                dv = w[s, k, bit, bit]
                ov = w[s, k, bit, 1 - bit] if not transpose2x2 else w[s, k, 1 - bit, bit]
                bb, r = divmod(i, 128)
                rj = (i ^ stride) % 128
                S[bb, r, r] = dv
                S[bb, r, rj] = ov
            return S

        def stage_cols(w, transpose2x2):
            dcol = np.zeros((5, 32, 128), np.float32)
            ocol = np.zeros((5, 32, 128), np.float32)
            for si, s in enumerate(range(7, 12)):
                stride = 2 ** s
                for i in range(DH):
                    k = (i // (2 * stride)) * stride + (i % stride)
                    bit = (i // stride) % 2
                    dv = w[s, k, bit, bit]
                    ov = w[s, k, bit, 1 - bit] if not transpose2x2 else w[s, k, 1 - bit, bit]
                    bb, r = divmod(i, 128)
                    dcol[si, bb, r] = dv
                    ocol[si, bb, r] = ov
            return dcol, ocol

        o = {}
        upS = np.zeros((7, 32, 128, 128), np.float32)
        upS[0] = stage_mat(up_w, 6, False).transpose(0, 2, 1)
        for slot, s in enumerate([5, 4, 3, 2, 1, 0], start=1):
            upS[slot] = stage_mat(up_w, s, False)
        dnS = np.zeros((7, 32, 128, 128), np.float32)
        dnS[0] = stage_mat(down_w, 0, True)
        for slot, s in enumerate([1, 2, 3, 4, 5, 6], start=1):
            dnS[slot] = stage_mat(down_w, s, True).transpose(0, 2, 1)
        o["up_S"] = np.ascontiguousarray(upS.transpose(1, 2, 0, 3)).astype(ml_dtypes.bfloat16)
        o["dn_S"] = np.ascontiguousarray(dnS.transpose(1, 2, 0, 3)).astype(ml_dtypes.bfloat16)
        ud, uo = stage_cols(up_w, False)
        dd, do_ = stage_cols(down_w, True)
        o["up_dcol"] = np.ascontiguousarray(ud.transpose(2, 0, 1))
        o["up_ocol"] = np.ascontiguousarray(uo.transpose(2, 0, 1))
        o["dn_dcol"] = np.ascontiguousarray(dd.transpose(2, 0, 1))
        o["dn_ocol"] = np.ascontiguousarray(do_.transpose(2, 0, 1))
        return o


    def build(nc, dbg=False, ext=None):
        stack = contextlib.ExitStack()
        E = stack.enter_context
        c = Ctx()
        c.nc = nc

        def dram(name, shape, dt, kind="ExternalInput"):
            return nc.dram_tensor(CFG["pfx"] + name, shape, dt, kind=kind).ap()

        c.xT = CFG["xin"] if CFG["xin"] is not None else dram("xT", [NSEQ, D, LSEQ], F32)
        c.wexpT = dram("wexpT", [DH, D], F32)
        c.wcon = dram("wcon", [DH, D], F32)
        c.up_S = dram("up_S", [32, 128, 7, 128], BF16)
        c.dn_S = dram("dn_S", [32, 128, 7, 128], BF16)
        c.up_dcol = dram("up_dcol", [128, 5, 32], F32)
        c.up_ocol = dram("up_ocol", [128, 5, 32], F32)
        c.dn_dcol = dram("dn_dcol", [128, 5, 32], F32)
        c.dn_ocol = dram("dn_ocol", [128, 5, 32], F32)
        c.bexp32 = dram("bexp32", [128, 32], F32)
        c.upb32 = dram("upb32", [128, 32], F32)
        c.bcon8 = dram("bcon8", [128, 8], F32)
        c.dwnb8 = dram("dwnb8", [128, 8], F32)
        c.c_ones_b = dram("c_ones_b", [128, 1], BF16)
        c.c_ident = dram("c_ident", [128, 128], BF16)
        c.c_ones_row = dram("c_ones_row", [1, 128], F32)
        c.c_eps_row = dram("c_eps_row", [1, 1], F32)
        c.x4T = CFG["xout"] if CFG["xout"] is not None else dram("x4T", [NSEQ, D, LSEQ], F32, kind="ExternalOutput")
        c.weff_scr = dram("weff_scr", [NKD, 128, DH], BF16, kind="Internal")
        c.vcon_scr = dram("vcon_scr", [NHT, 128, D], BF16, kind="Internal")

        def S(name, shape, dt):
            t = E(nc.sbuf_tensor(CFG["pfx"] + name, shape, dt))
            setattr(c, name, t)
            return t

        S("x_fm", [128, NKD, TG], F32)
        S("xb", [128, NKD, TG], BF16)
        S("sq", [128, NKD, TG], BF16)
        S("hbuf", [128, NHT, TG], BF16)
        S("ybf", [128, NKD, TG], BF16)
        S("s1", [128, NKD, TG], F32)
        S("t1", [128, TG], F32)
        S("r0", [1, TG], F32)
        S("r1", [1, TG], F32)
        S("r2", [1, TG], F32)
        c.wbuf = [S("wbuf0", [128, NKD, 256], BF16), S("wbuf1", [128, NKD, 256], BF16)]
        c.vbuf = [S("vbuf0", [128, NHT, 128], BF16), S("vbuf1", [128, NHT, 128], BF16)]
        S("beff", [128, 32], F32)
        S("btmp", [128, 32], F32)
        S("bt2", [128, 32], F32)
        S("bcola", [128, 32], F32)
        S("bcolx", [128, 32], BF16)
        S("bcon_s", [128, 8], F32)
        S("dwnb_s", [128, 8], F32)
        S("onesb_s", [128, 1], BF16)
        S("ident_s", [128, 128], BF16)
        S("upb_s", [128, 32], F32)
        S("onesr_s", [1, 128], F32)
        S("epsr_s", [1, 1], F32)
        S("smat", [128, 2, 7, 128], BF16)
        S("mchain", [128, 32, 128], BF16)
        S("fb0", [128, NHT, DSL], BF16)
        S("fb1", [128, NHT, DSL], BF16)
        S("ftmp", [128, DSL], BF16)
        S("fcol_d", [128, 5, 32], F32)
        S("fcol_o", [128, 5, 32], F32)
        S("stg", [128, DH], BF16)

        c.pA = E(nc.psum_tensor(CFG["pfx"] + "pA", [128, 512], F32))
        c.pB = E(nc.psum_tensor(CFG["pfx"] + "pB", [128, 512], F32))
        c.pW = E(nc.psum_tensor(CFG["pfx"] + "pW", [128, 1024], F32))
        c.pP = E(nc.psum_tensor(CFG["pfx"] + "pP", [128, 1024], F32))
        c.pWb = c.pW[:].bitcast(BF16)

        p = ext if ext is not None else Prog(nc, stack)
        c.p = p
        c.free = {}

        def pe_wait(name):
            if name in c.free:
                ctr, v = c.free[name]
                p.wait("tensor", ctr, v)

        c.pe_wait = pe_wait

        for dst, src in [(c.bcon_s, c.bcon8), (c.dwnb_s, c.dwnb8), (c.onesb_s, c.c_ones_b),
                         (c.onesr_s, c.c_ones_row), (c.epsr_s, c.c_eps_row),
                         (c.bcola, c.bexp32), (c.upb_s, c.upb32),
                         (c.ident_s, c.c_ident)]:
            p.dma("sync", dst[:], src, inc="ld2")
        c.n_ld0 = p.count("ld2")
        c.smat_free_mm = {0: 0, 1: 0}
        c.mchain_free_mm = 0
        c.weff_stored = 0
        c.vcon_stored = 0

        fold_side(c, up=True)
        fold_side(c, up=False)

        c.xfm_free = 0
        c.xb_free_mm = 0
        c.wb_last = {0: 0, 1: 0}
        c.vb_last = {0: 0, 1: 0}
        for b in range(NSEQ):
            for g in range(LSEQ // TG):
                emit_group(c, b, g)
        if ext is not None:
            return stack
        p.wait("gpsimd", "st", p.count("st"))
        block = E(nc.Block())
        p.finalize(block)
        stack.close()
        return nc


    def fold_side(c, up):
        p = c.p
        Smat_d = c.up_S if up else c.dn_S
        wsrc = c.wexpT if up else c.wcon

        p.dma("sync", c.fcol_d[:], c.up_dcol if up else c.dn_dcol, inc="ld2")
        ldc = p.dma("sync", c.fcol_o[:], c.up_ocol if up else c.dn_ocol, inc="ld2")

        sm_ld = {}
        for bb in range(NHT):
            p.wait("sync", "mm", c.smat_free_mm[bb % 2])
            sm_ld[bb] = p.dma("sync", c.smat[:, bb % 2, :, :], Smat_d[bb], inc="ld2")
            p.wait("vector", "ld2", sm_ld[bb])
            p.wait("vector", "mm", c.mchain_free_mm)
            p.emit("vector", lambda e, bb=bb: e.tensor_copy(
                c.mchain[:, bb, :], c.smat[:, bb % 2, 0, :]), inc="vec")
            mm = 0
            for s in range(1, 7):
                bank = [c.pA, c.pB][s % 2]
                bname = ["pa", "pb"][s % 2]
                p.wait("tensor", "vec", p.count("vec"))
                c.pe_wait(bname)
                mm = p.emit("tensor", lambda e, bb=bb, s=s, bank=bank: e.matmul(
                    bank[:, 0:128], c.smat[:, bb % 2, s, :], c.mchain[:, bb, :],
                    start=True, stop=True), inc="mm")
                p.wait("vector", "mm", mm)
                vv = p.emit("vector", lambda e, bb=bb, bank=bank: e.tensor_copy(
                    c.mchain[:, bb, :], bank[:, 0:128]), inc="vec")
                c.free[bname] = ("vec", vv)
            c.smat_free_mm[bb % 2] = mm
        v_chain = p.count("vec")

        if up:
            p.wait("vector", "ld2", c.n_ld0)
            p.emit("vector", lambda e: e.tensor_copy(c.bcolx[:], c.bcola[:]), inc="vec")
            v_bx = p.count("vec")
            p.wait("tensor", "vec", v_bx)
            c.pe_wait("pw")
            mm = 0
            for bb in range(NHT):
                mm = p.emit("tensor", lambda e, bb=bb: e.matmul(
                    c.pW[:, bb:bb + 1], c.mchain[:, bb, :], c.bcolx[:, bb:bb + 1],
                    start=True, stop=True), inc="mm")
            p.wait("vector", "mm", mm)
            p.wait("vector", "ld2", ldc)
            p.emit("vector", lambda e: e.tensor_copy(c.bcola[:], c.pW[:, 0:32]), inc="vec")
            c.free["pw"] = ("vec", p.count("vec"))
            bufs = [c.bcola, c.bt2]
            for si in range(5):
                sig = 2 ** (si + 7) // 128
                srcb, dstb = bufs[si % 2], bufs[1 - si % 2]
                for t in range(NHT):
                    tp = t ^ sig
                    p.emit("vector", lambda e, srcb=srcb, tp=tp, si=si, t=t: e.tensor_scalar(
                        c.btmp[:, t:t + 1], srcb[:, tp:tp + 1], c.fcol_o[:, si, t:t + 1],
                        None, ALU.mult))
                    p.emit("vector", lambda e, srcb=srcb, dstb=dstb, si=si, t=t:
                           e.scalar_tensor_tensor(
                        dstb[:, t:t + 1], srcb[:, t:t + 1], c.fcol_d[:, si, t:t + 1],
                        c.btmp[:, t:t + 1], ALU.mult, ALU.add))
            p.emit("vector", lambda e: e.tensor_tensor(
                c.beff[:], c.bt2[:], c.upb_s[:], ALU.add), inc="vec")
            c.v_beff = p.count("vec")

        nslices = D // DSL
        for sl in range(nslices):
            dsl = slice(sl * DSL, (sl + 1) * DSL)
            p.wait("gpsimd", "vec", getattr(c, "fb_free_vec", 0))
            p.wait("gpsimd", "mm", getattr(c, "fb_free_mm", 0))
            ldw = p.dma("gpsimd", c.fb0[:],
                        wsrc.rearrange("(t p) d -> p t d", p=128)[:, :, dsl], inc="ldp")
            p.wait("tensor", "ldp", ldw)
            p.wait("tensor", "vec", v_chain)
            for bb in range(NHT):
                bank = [c.pA, c.pB][bb % 2]
                bname = ["pa", "pb"][bb % 2]
                c.pe_wait(bname)
                mm = p.emit("tensor", lambda e, bb=bb, bank=bank: e.matmul(
                    bank[:, 0:DSL], c.mchain[:, bb, :], c.fb0[:, bb, :],
                    start=True, stop=True), inc="mm")
                if up:
                    c.mchain_free_mm = mm
                p.wait("vector", "mm", mm)
                vv = p.emit("vector", lambda e, bb=bb, bank=bank: e.tensor_copy(
                    c.fb1[:, bb, :], bank[:, 0:DSL]), inc="vec")
                c.free[bname] = ("vec", vv)
            bufs = [c.fb1, c.fb0]
            for si in range(5):
                sig = 2 ** (si + 7) // 128
                srcb, dstb = bufs[si % 2], bufs[1 - si % 2]
                for t in range(NHT):
                    tp = t ^ sig
                    p.emit("vector", lambda e, srcb=srcb, tp=tp, si=si, t=t: e.tensor_scalar(
                        c.ftmp[:, 0:DSL], srcb[:, tp, :], c.fcol_o[:, si, t:t + 1], None,
                        ALU.mult))
                    p.emit("vector", lambda e, srcb=srcb, dstb=dstb, si=si, t=t:
                           e.scalar_tensor_tensor(
                        dstb[:, t, :], srcb[:, t, :], c.fcol_d[:, si, t:t + 1],
                        c.ftmp[:, 0:DSL], ALU.mult, ALU.add))
            p.emit("vector", lambda e: e.engine_nop(), inc="vec")
            v_folded = p.count("vec")
            c.fb_free_vec = v_folded
            if up:
                for dk in range(DSL // 128):
                    dg = sl * (DSL // 128) + dk
                    for grp in range(4):
                        c.pe_wait("pw")
                        p.wait("tensor", "vec", v_folded)
                        mm = None
                        for i in range(8):
                            ht = grp * 8 + i
                            half = i // 4
                            mm = p.emit("tensor", lambda e, ht=ht, i=i, dk=dk: e.transpose(
                                c.pWb[:, i * 128:(i + 1) * 128],
                                c.fb0[:, ht, dk * 128:(dk + 1) * 128], c.ident_s[:]),
                                inc="mm")
                        c.fb_free_mm = mm
                        p.wait("vector", "mm", mm)
                        vv = p.emit("vector", lambda e, grp=grp: e.tensor_copy(
                            c.stg[:, grp * 1024:(grp + 1) * 1024],
                            c.pWb[:, 0:1024]), inc="vec")
                        c.free["pw"] = ("vec", vv)
                    v_stg = p.count("vec")
                    p.wait("sync", "vec", v_stg)
                    p.dma("sync", c.weff_scr[dg], c.stg[:], inc="st")
                c.weff_stored = p.count("st")
            else:
                p.wait("sync", "vec", v_folded)
                p.dma("sync",
                      c.vcon_scr[:, :, dsl].rearrange("t p d -> p t d"),
                      c.fb0[:], inc="st")
                c.vcon_stored = p.count("st")


    def emit_group(c, b, g):
        p = c.p
        gsl = slice(g * TG, (g + 1) * TG)
        p.wait("sync", "vec", c.xfm_free)
        ld = p.dma("sync", c.x_fm[:],
                   c.xT[b].rearrange("(k p) t -> p k t", p=128)[:, :, gsl], inc="ld2")
        p.wait("vector", "ld2", ld)
        p.wait("vector", "mm", c.xb_free_mm)
        p.emit("vector", lambda e: e.tensor_copy(c.xb[:], c.x_fm[:]), inc="vec")
        v_xb = p.count("vec")

        first = (b == 0 and g == 0)
        pend = {}
        for mp in range(16):
            p.wait("gpsimd", "mm", c.wb_last[mp % 2])
            if first and mp == 0:
                p.wait("gpsimd", "st", c.weff_stored)
            fs = mp * 256
            pend[mp] = p.dma("gpsimd", c.wbuf[mp % 2][:],
                             c.weff_scr[:, :, fs:fs + 256].rearrange("k p f -> p k f"),
                             inc="wb%d" % (mp % 2))
            for m in (2 * mp, 2 * mp + 1):
                off = (m % 2) * 128
                bank = [c.pA, c.pB][m % 2]
                bname = ["pa", "pb"][m % 2]
                p.wait("tensor", "wb%d" % (mp % 2), pend[mp])
                p.wait("tensor", "vec", v_xb)
                c.pe_wait(bname)
                mm = None
                for k in range(NKD):
                    mm = p.emit("tensor", lambda e, bank=bank, mp=mp, k=k, off=off: e.matmul(
                        bank[:], c.wbuf[mp % 2][:, k, off:off + 128], c.xb[:, k, :],
                        start=(k == 0), stop=(k == NKD - 1)),
                        inc=("mm" if k == NKD - 1 else None))
                c.wb_last[mp % 2] = mm
                p.wait("scalar", "mm", mm)
                p.wait("scalar", "vec", getattr(c, "v_beff", 0))
                vv = p.emit("scalar", lambda e, bank=bank, m=m: e.activation(
                    c.hbuf[:, m, :], bank[:], AF.Gelu, bias=c.beff[:, m:m + 1]), inc="act")
                c.free[bname] = ("act", vv)
        a_h = p.count("act")

        vp = {}

        def load_v(m):
            p.wait("gpsimd", "mm", c.vb_last[m % 2])
            if first and m == 0:
                p.wait("gpsimd", "st", c.vcon_stored)
            vp[m] = p.dma("gpsimd", c.vbuf[m % 2][:],
                          c.vcon_scr[:, :, m * 128:(m + 1) * 128].rearrange("k p f -> p k f"),
                          inc="wo%d" % (m % 2))

        load_v(0)
        load_v(1)
        for m in range(NKD):
            bank = [c.pA, c.pB][m % 2]
            bname = ["pa", "pb"][m % 2]
            p.wait("tensor", "wo%d" % (m % 2), vp[m])
            p.wait("tensor", "act", a_h)
            c.pe_wait(bname)
            mm = None
            for k in range(NHT):
                mm = p.emit("tensor", lambda e, bank=bank, m=m, k=k: e.matmul(
                    bank[:], c.vbuf[m % 2][:, k, :], c.hbuf[:, k, :],
                    start=(k == 0), stop=(k == NHT - 1)), inc=("mm" if k == NHT - 1 else None))
            c.vb_last[m % 2] = mm
            if m + 2 < NKD:
                load_v(m + 2)
            p.wait("vector", "mm", mm)
            vv = p.emit("vector", lambda e, bank=bank, m=m: e.tensor_scalar(
                c.ybf[:, m, :], bank[:], c.bcon_s[:, m:m + 1], c.dwnb_s[:, m:m + 1],
                ALU.add, ALU.add), inc="vec")
            c.free[bname] = ("vec", vv)
        for m in range(NKD):
            p.emit("vector", lambda e, m=m: e.tensor_tensor(
                c.s1[:, m, :], c.x_fm[:, m, :], c.ybf[:, m, :], ALU.add), inc="vec")
        v_s1 = p.count("vec")
        c.xfm_free = v_s1
        p.emit("vector", lambda e: e.tensor_copy(c.xb[:], c.s1[:]), inc="vec")
        v_xb2 = p.count("vec")
        p.wait("scalar", "vec", v_xb2)
        p.emit("scalar", lambda e: e.activation(c.sq[:], c.xb[:], AF.Square), inc="act")
        a_sq = p.count("act")
        p.wait("tensor", "vec", v_xb2)
        p.wait("tensor", "act", a_sq)
        c.pe_wait("pw")
        for k in range(NKD):
            p.emit("tensor", lambda e, k=k: e.matmul(
                c.pW[0:1, 0:512], c.onesb_s[:], c.xb[:, k, :],
                start=(k == 0), stop=(k == NKD - 1)), inc=("mm" if k == NKD - 1 else None))
        for k in range(NKD):
            p.emit("tensor", lambda e, k=k: e.matmul(
                c.pW[0:1, 512:1024], c.onesb_s[:], c.sq[:, k, :],
                start=(k == 0), stop=(k == NKD - 1)), inc=("mm" if k == NKD - 1 else None))
        mm_st = p.count("mm")
        c.xb_free_mm = mm_st
        p.wait("vector", "mm", mm_st)
        p.emit("vector", lambda e: e.tensor_scalar(
            c.r0[:], c.pW[0:1, 0:512], 1.0 / D, None, ALU.mult))
        p.emit("vector", lambda e: e.tensor_tensor(c.r1[:], c.r0[:], c.r0[:], ALU.mult))
        p.emit("vector", lambda e: e.scalar_tensor_tensor(
            c.r2[:], c.pW[0:1, 512:1024], 1.0 / D, c.r1[:], ALU.mult, ALU.subtract), inc="vec")
        v_rows = p.count("vec")
        c.free["pw"] = ("vec", v_rows)
        p.wait("scalar", "vec", v_rows)
        p.emit("scalar", lambda e: e.activation(c.r1[:], c.r2[:], AF.Ln, bias=c.epsr_s[:]))
        p.emit("scalar", lambda e: e.activation(c.r2[:], c.r1[:], AF.Exp, scale=-0.5), inc="act")
        a_rows = p.count("act")
        p.wait("tensor", "act", a_rows)
        c.pe_wait("pp")
        p.emit("tensor", lambda e: e.matmul(
            c.pP[:, 0:512], c.onesr_s[:], c.r0[:], start=True, stop=True))
        p.emit("tensor", lambda e: e.matmul(
            c.pP[:, 512:1024], c.onesr_s[:], c.r2[:], start=True, stop=True), inc="mm")
        mm_bc = p.count("mm")
        p.wait("vector", "mm", mm_bc)
        v_x4 = None
        for k in range(NKD):
            p.emit("vector", lambda e, k=k: e.tensor_tensor(
                c.t1[:], c.s1[:, k, :], c.pP[:, 0:512], ALU.subtract))
            v_x4 = p.emit("vector", lambda e, k=k: e.tensor_tensor(
                c.s1[:, k, :], c.t1[:], c.pP[:, 512:1024], ALU.mult), inc="vec")
        c.free["pp"] = ("vec", v_x4)
        p.wait("sync", "vec", v_x4)
        p.dma("sync", c.x4T[b].rearrange("(k p) t -> p k t", p=128)[:, :, gsl],
              c.s1[:], inc="st")
    return (build, host_fold_inputs, CFG)


_build_l0m, _cfg_l0m = _mod_l0m()
_build_l0f, _cfg_l0f = _mod_l0f()
_build_l1d, _cfg_l1d = _mod_l1d()
_build_l1f, _l1f_fold, _cfg_l1f = _mod_l1f()


def _build_fused():
    nc = bass.Bass("TRN2", target_bir_lowering=False)
    nc.detect_race_conditions = False
    stack = contextlib.ExitStack()
    xin = nc.dram_tensor("xT", [2, 1024, 2048], F32, kind="ExternalInput").ap()
    h1 = nc.dram_tensor("h1", [2, 1024, 2048], F32, kind="Internal").ap()
    h2 = nc.dram_tensor("h2", [2, 1024, 2048], F32, kind="Internal").ap()
    h3 = nc.dram_tensor("h3", [2, 1024, 2048], F32, kind="Internal").ap()
    xout = nc.dram_tensor("x4T", [2, 1024, 2048], F32, kind="ExternalOutput").ap()
    p = Prog(nc, stack)
    plan = [(_build_l0m, _cfg_l0m, "m0_", xin, h1),
            (_build_l0f, _cfg_l0f, "f0_", h1, h2),
            (_build_l1d, _cfg_l1d, "d1_", h2, h3),
            (_build_l1f, _cfg_l1f, "f1_", h3, xout)]
    for bld, cfg, pfx, xi, xo in plan:
        cfg["pfx"], cfg["xin"], cfg["xout"] = pfx, xi, xo
        st = bld(nc, ext=p)
        p.barrier()
        st.close()
        cfg["pfx"], cfg["xin"], cfg["xout"] = "", None, None
    blk = stack.enter_context(nc.Block())
    p.finalize(blk)
    stack.close()
    return nc


def _fused_inputs(d):
    im = {}
    for pfx, sub in (("m0_", _im_l0m(d)), ("f0_", _im_l0f(d)),
                     ("d1_", _im_l1d(d)), ("f1_", _im_l1f(d))):
        for k, v in sub.items():
            im[pfx + k] = v
    return im


def _im_l0m(d):
    c = host_consts()
    selb = np.zeros((32, 32, 64), np.float32)
    for h in range(32):
        selb[h, h, :] = 1.0
    return {
        "w_inT": np.ascontiguousarray(d["L0_attn_W_in"].T),
        "w_outT": np.ascontiguousarray(d["L0_attn_W_out"].T),
        "lnw8": np.ascontiguousarray(d["L0_attn_ln_w"].reshape(8, 128).T),
        "lnb8": np.ascontiguousarray(d["L0_attn_ln_b"].reshape(8, 128).T),
        "convw": np.ascontiguousarray(
            d["L0_attn_conv_w"].reshape(17, 128, 4).transpose(1, 0, 2)),
        "convb": np.ascontiguousarray(d["L0_attn_conv_b"].reshape(17, 128).T),
        "dtb": np.ascontiguousarray(d["L0_attn_dt_bias"][:, None]),
        "alog": np.ascontiguousarray(d["L0_attn_A_log"][:, None]),
        "dexp": np.ascontiguousarray(
            np.repeat(d["L0_attn_D"], 64).reshape(16, 128).T),
        "nrmw16": np.ascontiguousarray(d["L0_attn_norm_w"].reshape(16, 128).T),
        "c_ident": c["c_ident"],
        "c_identf": c["c_identf"],
        "c_triu_i": c["c_triu_i"],
        "c_sel32": np.ascontiguousarray(
            np.stack([np.eye(32)[:, h][:, None] * np.ones((1, 128))
                      for h in range(32)], 1)).astype(np.float32),
        "c_sel32b": np.ascontiguousarray(
            selb.transpose(1, 0, 2)).astype(ml_dtypes.bfloat16),
        "c_ones_b": c["c_ones_b"],
        "c_ones_row": np.ones((1, 128), np.float32),
        "c_eps_row": np.array([[1e-5]], np.float32),
        "c_one32": np.ones((32, 1), np.float32),
    }


def _im_l0f(d):
    return {
        "kern": np.ascontiguousarray(d["L0_ffn_kernel"]),
        "s_in": np.asarray(d["L0_ffn_input_scale"],
                           np.float32).reshape(1, 1),
        "s_k": np.asarray(d["L0_ffn_kernel_scale"],
                          np.float32).reshape(1, 1),
        "bias8": np.ascontiguousarray(
            d["L0_ffn_bias"].astype(np.float32).reshape(8, 128).T),
        "c_ones_b": np.ones((128, 1)).astype(ml_dtypes.bfloat16),
        "c_ones_row": np.ones((1, 128), np.float32),
        "c_eps_row": np.array([[1e-5]], np.float32),
    }


def _im_l1d(d):
    wall = np.zeros((1024, 4736), np.float32)
    wall[:, 0:768] = d["L1_attn_Wq"]
    wall[:, 768:1536] = d["L1_attn_Wk"]
    wall[:, 1536:3072] = d["L1_attn_Wv"]
    wall[:, 3072:4608] = d["L1_attn_Wg"]
    wall[:, 4608:4620] = d["L1_attn_Wgk"]
    wall[:, 4640:4652] = d["L1_attn_Wb"]
    convw = np.concatenate([d["L1_attn_conv_q"], d["L1_attn_conv_k"],
                            d["L1_attn_conv_v"]], 0)
    bb = np.zeros((44, 1), np.float32)
    bb[32:44, 0] = d["L1_attn_b_b"]
    selone = np.zeros((12, 12, 128), np.float32)
    for h in range(12):
        selone[h, h, :] = 1.0
    selp = np.zeros((6, 12, 128), np.float32)
    for j in range(6):
        selp[j, 2 * j, 0:64] = 1.0
        selp[j, 2 * j + 1, 64:128] = 1.0
    tl = np.tril(np.ones((64, 64)), -1)
    tu = np.triu(np.ones((64, 64)))
    dgm = np.ones((64, 64))
    z = np.zeros((64, 64))
    return {
        "wall": wall,
        "woT": np.ascontiguousarray(d["L1_attn_Wo"].astype(np.float32)),
        "convw": np.ascontiguousarray(
            convw.reshape(24, 128, 4).transpose(1, 0, 2)),
        "convb_z": np.zeros((1, 1), np.float32),
        "dtb12": d["L1_attn_dt_bias"].reshape(12, 1).astype(np.float32),
        "alog12": d["L1_attn_A_log"].reshape(12, 1).astype(np.float32),
        "bb44": bb,
        "gnw": d["L1_attn_g_norm_weight"].reshape(128, 1).astype(np.float32),
        "c_ident": np.eye(128).astype(ml_dtypes.bfloat16),
        "c_ident_h": np.eye(128).astype(np.float16),
        "c_bdones_h": np.block([[np.ones((64, 64)), z],
                                [z, np.ones((64, 64))]]).astype(np.float16),
        "c_ones_h": np.ones((128, 1), np.float16),
        "c_identf": np.eye(128).astype(np.float32),
        "c_bdones": np.block([[np.ones((64, 64)), z],
                              [z, np.ones((64, 64))]]).astype(ml_dtypes.bfloat16),
        "c_ptril_s": np.block([[tl, z], [z, tl]]).astype(ml_dtypes.bfloat16),
        "c_ptriu_i": np.block([[tu, z], [z, tu]]).astype(ml_dtypes.bfloat16),
        "c_pdiag": np.block([[dgm, z], [z, dgm]]).astype(ml_dtypes.bfloat16),
        "c_selone": np.ascontiguousarray(selone.transpose(1, 0, 2)),
        "c_selp": np.ascontiguousarray(selp.transpose(1, 0, 2)),
        "c_selpb": np.ascontiguousarray(selp[:, :, 0:64].transpose(1, 0, 2)),
        "c_ones_b": np.ones((128, 1)).astype(ml_dtypes.bfloat16),
        "c_ones_row": np.ones((1, 128), np.float32),
        "c_eps_row": np.array([[1e-5]], np.float32),
        "c_eps5col": np.full((128, 1), 1e-5, np.float32),
        "c_eps6col": np.full((128, 1), 1e-6, np.float32),
        "c_one12": np.ones((12, 1), np.float32),
        "c_mln8": np.full((128, 1), -np.log(8.0), np.float32),
    }


def _im_l1f(d):
    fold = _l1f_fold(np.asarray(d["L1_ffn_up_weights"], np.float32),
                     np.asarray(d["L1_ffn_down_weights"], np.float32))
    return {
        "wexpT": np.ascontiguousarray(d["L1_ffn_W_exp"].T.astype(np.float32)),
        "wcon": np.ascontiguousarray(d["L1_ffn_W_con"].astype(np.float32)),
        "bexp32": np.ascontiguousarray(
            d["L1_ffn_b_exp"].reshape(32, 128).T.astype(np.float32)),
        "upb32": np.ascontiguousarray(
            d["L1_ffn_up_bias"].reshape(32, 128).T.astype(np.float32)),
        "bcon8": np.ascontiguousarray(
            d["L1_ffn_b_con"].reshape(8, 128).T.astype(np.float32)),
        "dwnb8": np.ascontiguousarray(
            d["L1_ffn_down_bias"].reshape(8, 128).T.astype(np.float32)),
        "c_ones_b": np.ones((128, 1)).astype(ml_dtypes.bfloat16),
        "c_ident": np.eye(128).astype(ml_dtypes.bfloat16),
        "c_ones_row": np.ones((1, 128), np.float32),
        "c_eps_row": np.array([[1e-5]], np.float32),
        **fold,
    }


_NC_CACHE = {}


def _programs():
    if "ncs" not in _NC_CACHE:
        ncs = []
        for bld in (_build_l0m, _build_l0f, _build_l1d, _build_l1f):
            nc = bass.Bass("TRN2", target_bir_lowering=False)
            nc.detect_race_conditions = False
            bld(nc)
            ncs.append(nc)
        _NC_CACHE["ncs"] = ncs
    return _NC_CACHE["ncs"]


def kernel(**inputs):
    d = {k: (np.asarray(v, np.float32) if "ffn" not in k else np.asarray(v))
         for k, v in inputs.items()}
    x = d["L0_attn_x"]
    if "fused" not in _NC_CACHE:
        _NC_CACHE["fused"] = _build_fused()
    nc = _NC_CACHE["fused"]
    base = _fused_inputs(d)
    ims = [dict(base, xT=np.ascontiguousarray(
        x[2 * cc:2 * cc + 2].transpose(0, 2, 1))) for cc in range(8)]
    res = run_bass_kernel_spmd(nc, ims, core_ids=list(range(8)))
    out = np.empty((16, 2048, 1024), np.float32)
    for cc in range(8):
        out[2 * cc:2 * cc + 2] = np.asarray(
            res.results[cc]["x4T"]).transpose(0, 2, 1)
    return out


# version 08 (final)
# speedup vs baseline: 1.0602x; 1.0602x over previous
import contextlib
import numpy as np
import ml_dtypes
import concourse.bass as bass
import concourse.mybir as mybir
from concourse.bass_utils import run_bass_kernel_spmd

"""Shared kernel-builder machinery for the megakernel (raw Bass).

Conventions:
- Residual stream + all activations FEATURE-major: [128-feat-part x tokens].
- Host pre-transposes module input to [B, 1024, 2048] and weight matrices to
  contraction-major; host reshapes per-feature vectors to [128, ntiles].
- bf16 matmul operands everywhere, fp32 PSUM accumulation, fp32 elementwise
  for norms/cumsums/exps (mask exps computed fp32, stored bf16).
"""

F32 = mybir.dt.float32
BF16 = mybir.dt.bfloat16
FP8 = mybir.dt.float8e4
FP16 = mybir.dt.float16
AF = mybir.ActivationFunctionType
ALU = mybir.AluOpType

ENGINES = ("tensor", "vector", "scalar", "gpsimd", "sync")


class Prog:

    def __init__(self, nc, stack):
        self.nc = nc
        self.stack = stack
        self.q = {e: [] for e in ENGINES}
        self.sems = {}
        self.counts = {}
        for name in ("mm", "vec", "act", "gp", "ldw", "st", "ld2",
                     "ldp", "wb0", "wb1", "wo0", "wo1"):
            self._mksem(name)

    def _mksem(self, name):
        self.sems[name] = self.stack.enter_context(self.nc.semaphore(name))
        self.counts[name] = 0

    def emit(self, eng, fn, inc=None, inc_by=1):
        if inc is None:
            self.q[eng].append(lambda e, fn=fn: fn(e))
            return None
        self.counts[inc] += inc_by
        sem = self.sems[inc]
        self.q[eng].append(lambda e, fn=fn, s=sem, k=inc_by: fn(e).then_inc(s, k))
        return self.counts[inc]

    def wait(self, eng, ctr, val):
        if val is None or val <= 0:
            return
        sem = self.sems[ctr]
        self.q[eng].append(lambda e, s=sem, v=val: e.wait_ge(s, v))

    def count(self, ctr):
        return self.counts[ctr]

    def dma(self, eng, out, in_, inc=None, **kw):
        if inc is None:
            self.q[eng].append(lambda e, o=out, i=in_: e.dma_start(out=o, in_=i, **kw))
            return None
        self.counts[inc] += 16
        sem = self.sems[inc]
        self.q[eng].append(
            lambda e, o=out, i=in_, s=sem: e.dma_start(out=o, in_=i, **kw).then_inc(s, 16))
        return self.counts[inc]

    def barrier(self):
        for eng in ENGINES:
            for name, cnt in self.counts.items():
                if cnt > 0:
                    self.wait(eng, name, cnt)

    def finalize(self, block):
        for eng in ENGINES:
            ops = self.q[eng]
            if not ops:
                continue

            def runner(e, ops=ops):
                for op in ops:
                    op(e)
            getattr(block, eng)(runner)


def ceil_div(a, b):
    return -(-a // b)



def host_consts():
    c = {}
    c["c_ident"] = np.eye(128).astype(ml_dtypes.bfloat16)
    c["c_identf"] = np.eye(128).astype(np.float32)
    c["c_triu_i"] = np.triu(np.ones((128, 128))).astype(ml_dtypes.bfloat16)
    c["c_triu_s"] = np.triu(np.ones((128, 128)), 1).astype(ml_dtypes.bfloat16)
    tl = np.tril(np.ones((64, 64)), -1)
    c["c_pair_tril_s"] = np.block(
        [[tl, np.zeros((64, 64))], [np.zeros((64, 64)), tl]]).astype(ml_dtypes.bfloat16)
    tu_i = np.triu(np.ones((64, 64)))
    c["c_pair_triu_i"] = np.block(
        [[tu_i, np.zeros((64, 64))], [np.zeros((64, 64)), tu_i]]).astype(ml_dtypes.bfloat16)
    sel = np.zeros((32, 32, 128), np.float32)
    for h in range(32):
        sel[h, h, :] = 1.0
    c["c_sel32"] = np.ascontiguousarray(sel.transpose(1, 0, 2))
    selp = np.zeros((6, 12, 128), np.float32)
    for j in range(6):
        selp[j, 2 * j, 0:64] = 1.0
        selp[j, 2 * j + 1, 64:128] = 1.0
    c["c_selp"] = np.ascontiguousarray(selp.transpose(1, 0, 2))
    bd = np.block([[np.ones((64, 64)), np.zeros((64, 64))],
                   [np.zeros((64, 64)), np.ones((64, 64))]])
    c["c_bdones"] = bd.astype(ml_dtypes.bfloat16)
    c["c_ones_f"] = np.ones((128, 1), np.float32)
    c["c_ones_b"] = np.ones((128, 1)).astype(ml_dtypes.bfloat16)
    c["c_misc"] = np.array([[1e-5, 1e-6, 0.0, 1.0]], np.float32).repeat(128, 0)
    return c


def vec_tiles(v, p=128):
    n = v.shape[0]
    nt = ceil_div(n, p)
    out = np.zeros((p, nt), v.dtype)
    for t in range(nt):
        seg = v[t * p:(t + 1) * p]
        out[: seg.shape[0], t] = seg
    return np.ascontiguousarray(out)

def _mod_l0m():

    LSEQ = 2048
    D = 1024
    DIN = 2048
    DPROJ = 4256
    NH = 32
    HD = 64
    TG = 512
    NKD = D // 128
    NMT = 34
    NZT = 16
    NXT = 17
    EPS = 1e-5
    NSEQ = 2
    CFG = {"pfx": "", "xin": None, "xout": None}


    class Ctx:
        pass


    def declare(nc, c, dbg=False):
        c.nc = nc
        c.dbg = dbg

        def dram(name, shape, dt, kind="ExternalInput"):
            return nc.dram_tensor(CFG["pfx"] + name, shape, dt, kind=kind).ap()

        c.xT = CFG["xin"] if CFG["xin"] is not None else dram("xT", [NSEQ, D, LSEQ], F32)
        c.w_inT = dram("w_inT", [D, DPROJ], F32)
        c.w_outT = dram("w_outT", [DIN, D], F32)
        c.lnw8 = dram("lnw8", [128, 8], F32)
        c.lnb8 = dram("lnb8", [128, 8], F32)
        c.convw = dram("convw", [128, NXT, 4], F32)
        c.convb = dram("convb", [128, NXT], F32)
        c.dtb = dram("dtb", [32, 1], F32)
        c.alog = dram("alog", [32, 1], F32)
        c.dexp = dram("dexp", [128, NZT], F32)
        c.nrmw16 = dram("nrmw16", [128, NZT], F32)
        c.c_ident = dram("c_ident", [128, 128], BF16)
        c.c_identf = dram("c_identf", [128, 128], F32)
        c.c_triu_i = dram("c_triu_i", [128, 128], BF16)
        c.c_sel32 = dram("c_sel32", [32, 32, 128], F32)
        c.c_sel32b = dram("c_sel32b", [32, 32, 64], BF16)
        c.c_ones_b = dram("c_ones_b", [128, 1], BF16)
        c.c_ones_row = dram("c_ones_row", [1, 128], F32)
        c.c_eps_row = dram("c_eps_row", [1, 1], F32)
        c.c_one32 = dram("c_one32", [32, 1], F32)
        c.x1T = CFG["xout"] if CFG["xout"] is not None else dram("x1T", [NSEQ, D, LSEQ], F32, kind="ExternalOutput")
        c.dbg_t = {}
        if dbg:
            for nm, shp in [("d_xs", [128, NZT, TG]), ("d_b", [64, TG]), ("d_c", [64, TG]),
                            ("d_dt", [32, TG]), ("d_y", [128, NZT, TG]),
                            ("d_zx", [128, NXT, TG]), ("d_yn", [128, NZT, TG]),
                            ("d_cum", [32, TG]), ("d_ys", [128, NZT, TG])]:
                c.dbg_t[nm] = dram(nm, shp, F32, kind="ExternalOutput")
        c.wi_scr = dram("wi_scr", [NKD, 128, DPROJ], BF16, kind="Internal")
        c.wo_scr = dram("wo_scr", [NZT, 128, D], BF16, kind="Internal")
        return c


    def alloc(c, E):
        nc = c.nc

        def S(name, shape, dt):
            t = E(nc.sbuf_tensor(CFG["pfx"] + name, shape, dt))
            setattr(c, name, t)
            return t

        S("x_fm", [128, NKD, TG], F32)
        S("zsil", [128, NZT, TG], BF16)
        S("xbc", [128, NXT, TG + 3], BF16)
        S("yg", [128, NZT, TG], BF16)
        S("xs_s1", [128, NKD, TG], F32)
        c.xs = c.xs_s1[:].bitcast(BF16).rearrange("p a b -> p (a b)").rearrange("p (zt t) -> p zt t", zt=NZT)
        S("y_bf", [128, NZT, TG], BF16)
        c.sq16 = c.y_bf[:, 0:8, :]
        c.xhat = c.y_bf[:, 8:16, :]
        c.xb16 = c.yg[:, 0:8, :]
        c.sqz = c.zsil[:]
        S("t1", [128, TG], F32)
        S("r0", [1, TG], F32)
        S("r1", [1, TG], F32)
        S("r2", [1, TG], F32)
        S("Bt", [64, TG], BF16)
        S("Ct", [64, TG], BF16)
        S("dtraw", [32, TG], F32)
        S("dtv", [32, TG], F32)
        S("spx", [32, TG], F32)
        S("dtf", [32, TG], F32)
        S("cumA", [32, TG], F32)
        S("pfm", [32, TG], BF16)
        S("hs", [64, 2, NH, HD], BF16)
        S("xs_tm", [128, NZT, 128], BF16)
        S("xt_tm", [128, 2, 64], BF16)
        S("b_tm", [128, 64], BF16)
        S("b_hat", [128, 2, 64], BF16)
        S("dt_tm", [128, 32], F32)
        S("ca_tm", [128, 32], F32)
        S("dmsk", [128, 8, 128], F32)
        S("dmin", [128, 8, 128], F32)
        S("emsk", [128, 8, 128], BF16)
        S("wmsk", [128, 8, 128], BF16)
        S("chat", [64, 8, 128], BF16)
        S("gmsk", [128, 128], BF16)
        S("gcol", [64, NH], F32)
        S("ecol", [128, 8], F32)
        S("dxs", [128, NZT, 128], BF16)
        c.wbuf = [S("wbuf0", [128, NKD, 256], BF16), S("wbuf1", [128, NKD, 256], BF16)]
        c.wob = [S("wob0", [128, NZT, 128], BF16), S("wob1", [128, NZT, 128], BF16)]
        S("wbeta", [128, NMT], F32)
        S("diag", [128, NXT, 4, 128], BF16)
        S("lnw_s", [128, 8], F32)
        S("lnb_s", [128, 8], F32)
        S("lnb_b", [128, 8], BF16)
        S("convw_s", [128, NXT, 4], F32)
        S("convb_s", [128, NXT], F32)
        S("dtb_s", [32, 1], F32)
        S("nA_s", [32, 1], F32)
        S("dexp_s", [128, NZT], F32)
        S("nrmw_s", [128, NZT], F32)
        S("ident_s", [128, 128], BF16)
        S("identf_s", [128, 128], F32)
        S("triu_s", [128, 128], BF16)
        S("sel_s", [32, 32, 128], F32)
        S("selb_s", [32, 32, 64], BF16)
        S("onesb_s", [128, 1], BF16)
        S("onesr_s", [1, 128], F32)
        S("epsr_s", [1, 1], F32)
        S("one32_s", [32, 1], F32)

        c.pA = E(nc.psum_tensor(CFG["pfx"] + "pA", [128, 512], F32))
        c.pB = E(nc.psum_tensor(CFG["pfx"] + "pB", [128, 512], F32))
        c.pW = E(nc.psum_tensor(CFG["pfx"] + "pW", [128, 1024], F32))
        c.pP = E(nc.psum_tensor(CFG["pfx"] + "pP", [128, 1024], F32))
        c.pY = E(nc.psum_tensor(CFG["pfx"] + "pY", [128, 512], F32))
        c.pZ = E(nc.psum_tensor(CFG["pfx"] + "pZ", [128, 512], F32))
        c.pWb = c.pW[:].bitcast(BF16)
        c.pZb = c.pZ[:].bitcast(BF16)


    def build(nc, dbg=False, ext=None):
        stack = contextlib.ExitStack()
        E = stack.enter_context
        c = Ctx()
        declare(nc, c, dbg)
        alloc(c, E)
        p = ext if ext is not None else Prog(nc, stack)
        c.p = p
        c.free = {}
        c.wbuf_last = {0: 0, 1: 0}
        c.wob_last = {0: 0, 1: 0}
        c.wout_mm_last = 0
        c.xfm_free = 0
        c.scan_mm_end = 0
        c.conv_mm_last = 0

        def pe_wait(name):
            if name in c.free:
                ctr, v = c.free[name]
                p.wait("tensor", ctr, v)

        c.pe_wait = pe_wait
        prepass(c)
        for b in range(NSEQ):
            p.wait("vector", "mm", c.scan_mm_end)
            p.emit("vector", lambda e: e.memset(c.hs[:], 0.0))
            p.emit("vector", lambda e: e.memset(c.xbc[:, :, 0:3], 0.0), inc="vec")
            for g in range(LSEQ // TG):
                emit_group(c, b, g)
        if ext is not None:
            return stack
        p.wait("gpsimd", "st", p.count("st"))
        p.wait("gpsimd", "gp", p.count("gp"))
        block = E(nc.Block())
        p.finalize(block)
        stack.close()
        return nc


    def prepass(c):
        p = c.p
        pairs = [(c.lnw_s, c.lnw8), (c.lnb_s, c.lnb8), (c.convw_s, c.convw),
                 (c.convb_s, c.convb), (c.dtb_s, c.dtb), (c.nA_s, c.alog),
                 (c.dexp_s, c.dexp), (c.nrmw_s, c.nrmw16),
                 (c.ident_s, c.c_ident), (c.identf_s, c.c_identf), (c.triu_s, c.c_triu_i),
                 (c.sel_s, c.c_sel32), (c.selb_s, c.c_sel32b), (c.onesb_s, c.c_ones_b),
                 (c.onesr_s, c.c_ones_row), (c.epsr_s, c.c_eps_row), (c.one32_s, c.c_one32)]
        for dst, src in pairs:
            p.dma("sync", dst[:], src, inc="ld2")
        n_small = p.count("ld2")
        p.dma("gpsimd", c.wi_scr, c.w_inT.rearrange("(k p) f -> k p f", p=128), inc="ldp")
        p.dma("gpsimd", c.wo_scr, c.w_outT.rearrange("(k p) f -> k p f", p=128), inc="ldp")
        c.wscr_done = p.count("ldp")

        p.wait("scalar", "ld2", n_small)
        p.emit("scalar", lambda e: e.activation(c.spx[0:32, 0:1], c.nA_s[:], AF.Exp))
        p.emit("scalar", lambda e: e.mul(c.nA_s[:], c.spx[0:32, 0:1], -1.0), inc="act")
        c.nA_ready = p.count("act")
        p.wait("vector", "ld2", n_small)
        for i in range(NXT):
            for j in range(4):
                p.emit("vector", lambda e, i=i, j=j: e.tensor_scalar(
                    c.diag[:, i, j, :], c.ident_s[:], c.convw_s[:, i, j:j + 1], None, ALU.mult))
        p.emit("vector", lambda e: e.tensor_copy(c.lnb_b[:], c.lnb_s[:]))
        p.emit("vector", lambda e: e.memset(c.wbeta[:], 0.0), inc="vec")
        c.prep_v = p.count("vec")
        wingemm(c, None, c.prep_v)


    def wingemm(c, v_xh, prep_v=None):
        p = c.p
        beta = v_xh is None
        pend = {}
        for mp in range(17):
            p.wait("gpsimd", "mm", c.wbuf_last[mp % 2])
            if beta and mp == 0:
                p.wait("gpsimd", "ldp", c.wscr_done)
            fs, fe = mp * 256, min(mp * 256 + 256, DPROJ)
            pend[mp] = p.dma("gpsimd", c.wbuf[mp % 2][:, :, 0:fe - fs],
                             c.wi_scr[:, :, fs:fe].rearrange("k p f -> p k f"),
                             inc="wb%d" % (mp % 2))
            for m in (2 * mp, 2 * mp + 1):
                if m >= NMT:
                    continue
                off = (m % 2) * 128
                mw = 128 if m < 33 else 32
                bank = [c.pA, c.pB][m % 2]
                bname = ["pa", "pb"][m % 2]
                p.wait("tensor", "wb%d" % (mp % 2), pend[mp])
                if beta:
                    p.wait("tensor", "vec", prep_v)
                else:
                    p.wait("tensor", "vec", v_xh)
                c.pe_wait(bname)
                mm = None
                for k in range(NKD):
                    if beta:
                        mm = p.emit("tensor", lambda e, bank=bank, mp=mp, k=k, off=off, mw=mw:
                                    e.matmul(bank[0:mw, 0:1], c.wbuf[mp % 2][:, k, off:off + mw],
                                             c.lnb_b[:, k:k + 1], start=(k == 0), stop=(k == NKD - 1)),
                                    inc=("mm" if k == NKD - 1 else None))
                    else:
                        mm = p.emit("tensor", lambda e, bank=bank, mp=mp, k=k, off=off, mw=mw:
                                    e.matmul(bank[0:mw, :], c.wbuf[mp % 2][:, k, off:off + mw],
                                             c.xhat[:, k, :], start=(k == 0), stop=(k == NKD - 1)),
                                    inc=("mm" if k == NKD - 1 else None))
                c.wbuf_last[mp % 2] = mm
                if beta:
                    p.wait("vector", "mm", mm)
                    vv = p.emit("vector", lambda e, bank=bank, m=m, mw=mw: e.tensor_copy(
                        c.wbeta[0:mw, m:m + 1], bank[0:mw, 0:1]), inc="vec")
                    c.free[bname] = ("vec", vv)
                elif m < NZT:
                    p.wait("scalar", "mm", mm)
                    p.wait("scalar", "vec", c.zsil_free)
                    vv = p.emit("scalar", lambda e, bank=bank, m=m: e.activation(
                        c.zsil[:, m, :], bank[:], AF.Silu, bias=c.wbeta[:, m:m + 1]), inc="act")
                    c.free[bname] = ("act", vv)
                elif m < 33:
                    i = m - NZT
                    p.wait("vector", "mm", mm)
                    vv = p.emit("vector", lambda e, bank=bank, i=i, m=m: e.tensor_scalar(
                        c.xbc[:, i, 3:3 + TG], bank[:], c.wbeta[:, m:m + 1], None, ALU.add),
                        inc="vec")
                    c.free[bname] = ("vec", vv)
                else:
                    p.wait("vector", "mm", mm)
                    vv = p.emit("vector", lambda e, bank=bank: e.tensor_scalar(
                        c.dtraw[:], bank[0:32, :], c.wbeta[0:32, 33:34], c.dtb_s[:],
                        ALU.add, ALU.add), inc="vec")
                    c.free[bname] = ("vec", vv)


    def emit_ln_rows_and_bcast(c, denom):
        p = c.p
        p.wait("vector", "mm", p.count("mm"))
        p.emit("vector", lambda e: e.tensor_scalar(
            c.r0[0:1, 0:512], c.pW[0:1, 0:512], 1.0 / denom, None, ALU.mult))
        p.emit("vector", lambda e: e.tensor_tensor(
            c.r1[0:1, 0:512], c.r0[0:1, 0:512], c.r0[0:1, 0:512], ALU.mult))
        p.emit("vector", lambda e: e.scalar_tensor_tensor(
            c.r2[0:1, 0:512], c.pW[0:1, 512:1024], 1.0 / denom, c.r1[0:1, 0:512],
            ALU.mult, ALU.subtract), inc="vec")
        v_rows = p.count("vec")
        c.free["pw"] = ("vec", v_rows)
        p.wait("scalar", "vec", v_rows)
        p.emit("scalar", lambda e: e.activation(
            c.r1[0:1, 0:512], c.r2[0:1, 0:512], AF.Ln, bias=c.epsr_s[:]))
        p.emit("scalar", lambda e: e.activation(
            c.r2[0:1, 0:512], c.r1[0:1, 0:512], AF.Exp, scale=-0.5), inc="act")
        a_rows = p.count("act")
        p.wait("tensor", "act", a_rows)
        c.pe_wait("pp")
        p.emit("tensor", lambda e: e.matmul(
            c.pP[:, 0:512], c.onesr_s[:], c.r0[0:1, 0:512], start=True, stop=True), inc="mm")
        p.emit("tensor", lambda e: e.matmul(
            c.pP[:, 512:1024], c.onesr_s[:], c.r2[0:1, 0:512], start=True, stop=True), inc="mm")
        return p.count("mm")


    def emit_stats_mms(c, xb_ap, sq_ap, v_ready, a_ready):
        p = c.p
        p.wait("tensor", "vec", v_ready)
        p.wait("tensor", "act", a_ready)
        c.pe_wait("pw")
        for k in range(NKD):
            p.emit("tensor", lambda e, k=k: e.matmul(
                c.pW[0:1, 0:512], c.onesb_s[:], xb_ap[:, k, :],
                start=(k == 0), stop=(k == NKD - 1)), inc=("mm" if k == NKD - 1 else None))
        for k in range(NKD):
            p.emit("tensor", lambda e, k=k: e.matmul(
                c.pW[0:1, 512:1024], c.onesb_s[:], sq_ap[:, k, :],
                start=(k == 0), stop=(k == NKD - 1)), inc=("mm" if k == NKD - 1 else None))


    def emit_group(c, b, g):
        p = c.p
        NSUB = TG // 128
        gsl = slice(g * TG, (g + 1) * TG)

        p.wait("sync", "vec", c.xfm_free)
        ld = p.dma("sync", c.x_fm[:],
                   c.xT[b].rearrange("(k p) t -> p k t", p=128)[:, :, gsl], inc="ld2")

        p.wait("vector", "ld2", ld)
        p.wait("vector", "vec", getattr(c, "yg_free", 0))
        p.emit("vector", lambda e: e.tensor_copy(c.xb16, c.x_fm[:]), inc="vec")
        v_xb = p.count("vec")
        p.wait("scalar", "vec", v_xb)
        p.wait("scalar", "mm", c.wout_mm_last)
        p.emit("scalar", lambda e: e.activation(c.sq16, c.xb16, AF.Square), inc="act")
        a_sq = p.count("act")
        emit_stats_mms(c, c.xb16, c.sq16, v_xb, a_sq)
        mm_bc = emit_ln_rows_and_bcast(c, float(D))
        p.wait("vector", "mm", mm_bc)
        v_xh = None
        for k in range(NKD):
            p.emit("vector", lambda e, k=k: e.tensor_tensor(
                c.t1[:], c.x_fm[:, k, :], c.pP[:, 0:512], ALU.subtract))
            v_xh = p.emit("vector", lambda e, k=k: e.scalar_tensor_tensor(
                c.xhat[:, k, :], c.t1[:], c.lnw_s[:, k:k + 1], c.pP[:, 512:1024],
                ALU.mult, ALU.mult), inc="vec")
        c.free["pp"] = ("vec", v_xh)

        if g > 0:
            p.wait("vector", "mm", c.conv_mm_last)
            p.emit("vector", lambda e: e.tensor_copy(
                c.xbc[:, :, 0:3], c.xbc[:, :, TG:TG + 3]), inc="vec")

        c.zsil_free = getattr(c, "v_yg", 0)
        wingemm(c, v_xh)
        v_xbc_done = p.count("vec")

        for i in range(NXT):
            bank = [c.pA, c.pB][i % 2]
            bname = ["pa", "pb"][i % 2]
            p.wait("tensor", "vec", v_xbc_done)
            c.pe_wait(bname)
            mm = None
            for j in range(4):
                mm = p.emit("tensor", lambda e, bank=bank, i=i, j=j: e.matmul(
                    bank[:], c.diag[:, i, j, :], c.xbc[:, i, j:j + TG],
                    start=(j == 0), stop=(j == 3)), inc=("mm" if j == 3 else None))
            c.conv_mm_last = mm
            p.wait("scalar", "mm", mm)
            if i < NZT:
                p.wait("scalar", "mm", c.scan_mm_end)
                p.wait("scalar", "vec", getattr(c, "v_s1_stored", 0))
                p.wait("scalar", "st", getattr(c, "s1_store_done", 0))
                vv = p.emit("scalar", lambda e, bank=bank, i=i: e.activation(
                    c.xs[:, i, :], bank[:], AF.Silu, bias=c.convb_s[:, i:i + 1]), inc="act")
            else:
                p.emit("scalar", lambda e, bank=bank: e.activation(
                    c.Bt[:], bank[0:64, :], AF.Silu, bias=c.convb_s[0:64, 16:17]))
                vv = p.emit("scalar", lambda e, bank=bank: e.activation(
                    c.Ct[:], bank[64:128, :], AF.Silu, bias=c.convb_s[64:128, 16:17]), inc="act")
            c.free[bname] = ("act", vv)
        a_conv_done = p.count("act")

        p.wait("vector", "mm", c.scan_mm_end)
        v_dtev = p.count("vec")
        p.wait("scalar", "vec", v_dtev)
        p.emit("scalar", lambda e: e.activation(c.dtv[:], c.dtraw[:], AF.Abs))
        p.emit("scalar", lambda e: e.activation(c.spx[:], c.dtv[:], AF.Exp, scale=-1.0))
        p.emit("scalar", lambda e: e.activation(c.dtv[:], c.spx[:], AF.Ln, bias=c.one32_s[:]))
        p.emit("scalar", lambda e: e.activation(c.spx[:], c.dtraw[:], AF.Relu), inc="act")
        a_sp = p.count("act")
        p.wait("vector", "act", a_sp)
        p.emit("vector", lambda e: e.tensor_tensor(c.dtf[:], c.spx[:], c.dtv[:], ALU.add))
        p.wait("vector", "act", c.nA_ready)
        p.emit("vector", lambda e: e.tensor_scalar(
            c.dtraw[:], c.dtf[:], c.nA_s[:], None, ALU.mult), inc="vec")
        for s in range(NSUB):
            p.emit("vector", lambda e, s=s: e.tensor_tensor_scan(
                c.cumA[:, s * 128:(s + 1) * 128], c.one32_s[:].broadcast_to([32, 128]),
                c.dtraw[:, s * 128:(s + 1) * 128], 0.0, ALU.mult, ALU.add), inc="vec")
        v_cum = p.count("vec")
        p.wait("scalar", "vec", v_cum)
        p.emit("scalar", lambda e: e.activation(c.pfm[:], c.cumA[:], AF.Exp), inc="act")
        a_pfm = p.count("act")

        for s in range(NSUB):
            emit_scan_sub(c, b, g, s, v_cum, a_pfm, a_conv_done)
        c.scan_mm_end = p.count("mm")

        if c.dbg and b == 0 and g == 0:
            p.wait("gpsimd", "vec", p.count("vec"))
            p.dma("gpsimd", c.dbg_t["d_cum"], c.cumA[:], inc="gp")
            p.dma("gpsimd", c.dbg_t["d_ys"], c.y_bf[:], inc="gp")

        p.wait("vector", "mm", c.scan_mm_end)
        p.emit("vector", lambda e: e.tensor_tensor(
            c.yg[:].rearrange("p a t -> p (a t)"), c.y_bf[:].rearrange("p a t -> p (a t)"),
            c.zsil[:].rearrange("p a t -> p (a t)"), ALU.mult), inc="vec")
        c.v_yg = p.count("vec")
        c.yg_free = None
        p.wait("scalar", "vec", c.v_yg)
        p.emit("scalar", lambda e: e.activation(
            c.sqz.rearrange("p a t -> p (a t)"), c.yg[:].rearrange("p a t -> p (a t)"),
            AF.Square), inc="act")
        a_ysq = p.count("act")
        p.wait("tensor", "act", a_ysq)
        c.pe_wait("pw")
        mm = None
        for i in range(NZT):
            mm = p.emit("tensor", lambda e, i=i: e.matmul(
                c.pW[0:1, 0:512], c.onesb_s[:], c.sqz[:, i, :],
                start=(i == 0), stop=(i == NZT - 1)), inc=("mm" if i == NZT - 1 else None))
        p.wait("scalar", "mm", mm)
        p.emit("scalar", lambda e: e.activation(
            c.r1[0:1, 0:512], c.pW[0:1, 0:512], AF.Ln, bias=c.epsr_s[:], scale=1.0 / DIN))
        p.emit("scalar", lambda e: e.activation(
            c.r2[0:1, 0:512], c.r1[0:1, 0:512], AF.Exp, scale=-0.5), inc="act")
        a_rms = p.count("act")
        c.free["pw"] = ("act", a_rms)
        p.wait("tensor", "act", a_rms)
        c.pe_wait("pp")
        mm_rb = p.emit("tensor", lambda e: e.matmul(
            c.pP[:, 0:512], c.onesr_s[:], c.r2[0:1, 0:512], start=True, stop=True), inc="mm")
        p.wait("vector", "mm", mm_rb)
        v_yn = None
        for i in range(NZT):
            v_yn = p.emit("vector", lambda e, i=i: e.scalar_tensor_tensor(
                c.y_bf[:, i, :], c.yg[:, i, :], c.nrmw_s[:, i:i + 1], c.pP[:, 0:512],
                ALU.mult, ALU.mult), inc="vec")
        c.free["pp"] = ("vec", v_yn)
        c.yg_free = v_yn

        if c.dbg and b == 0 and g == 0:
            emit_dbg(c, v_yn)

        wo_ld = {}

        def load_wo(m):
            p.wait("gpsimd", "mm", c.wob_last[m % 2])
            if b == 0 and g == 0 and m == 0:
                p.wait("gpsimd", "ldp", c.wscr_done)
            wo_ld[m] = p.dma("gpsimd", c.wob[m % 2][:],
                             c.wo_scr[:, :, m * 128:(m + 1) * 128].rearrange("k p f -> p k f"),
                             inc="wo%d" % (m % 2))

        load_wo(0)
        load_wo(1)
        for m in range(NKD):
            bank = [c.pA, c.pB][m % 2]
            bname = ["pa", "pb"][m % 2]
            p.wait("tensor", "wo%d" % (m % 2), wo_ld[m])
            p.wait("tensor", "vec", v_yn)
            c.pe_wait(bname)
            mm = None
            for k in range(NZT):
                mm = p.emit("tensor", lambda e, bank=bank, m=m, k=k: e.matmul(
                    bank[:], c.wob[m % 2][:, k, :], c.y_bf[:, k, :],
                    start=(k == 0), stop=(k == NZT - 1)), inc=("mm" if k == NZT - 1 else None))
            c.wob_last[m % 2] = mm
            c.wout_mm_last = mm
            if m + 2 < NKD:
                load_wo(m + 2)
            p.wait("vector", "mm", mm)
            vv = p.emit("vector", lambda e, bank=bank, m=m: e.scalar_tensor_tensor(
                c.xs_s1[:, m, :], c.x_fm[:, m, :], 2.0, bank[:], ALU.mult, ALU.add), inc="vec")
            c.free[bname] = ("vec", vv)
        v_s1 = p.count("vec")
        c.xfm_free = v_s1

        p.emit("vector", lambda e: e.tensor_copy(c.xb16, c.xs_s1[:]), inc="vec")
        v_xb2 = p.count("vec")
        p.wait("scalar", "vec", v_xb2)
        p.emit("scalar", lambda e: e.activation(c.sq16, c.xb16, AF.Square), inc="act")
        a_sq2 = p.count("act")
        emit_stats_mms(c, c.xb16, c.sq16, v_xb2, a_sq2)
        mm_bc2 = emit_ln_rows_and_bcast(c, float(D))
        p.wait("vector", "mm", mm_bc2)
        v_x1 = None
        for k in range(NKD):
            p.emit("vector", lambda e, k=k: e.tensor_tensor(
                c.t1[:], c.xs_s1[:, k, :], c.pP[:, 0:512], ALU.subtract))
            v_x1 = p.emit("vector", lambda e, k=k: e.tensor_tensor(
                c.xs_s1[:, k, :], c.t1[:], c.pP[:, 512:1024], ALU.mult), inc="vec")
        c.free["pp"] = ("vec", v_x1)
        c.v_s1_stored = v_x1
        p.wait("sync", "vec", v_x1)
        p.dma("sync", c.x1T[b].rearrange("(k p) t -> p k t", p=128)[:, :, gsl],
              c.xs_s1[:], inc="st")
        c.s1_store_done = p.count("st")


    def emit_scan_sub(c, b, g, s, v_cum, a_pfm, a_conv_done):
        p = c.p
        NSUB = TG // 128
        ssl = slice(s * 128, (s + 1) * 128)
        par = (g * NSUB + s) % 2
        xt_free = getattr(c, "xt_free", {0: 0, 1: 0})
        c.xt_free = xt_free

        p.wait("tensor", "act", a_conv_done)
        for half in range(2):
            c.pe_wait("pw")
            mm = None
            for i in range(8):
                t = half * 8 + i
                mm = p.emit("tensor", lambda e, t=t, i=i, half=half: e.transpose(
                    c.pWb[:, half * 1024 + i * 128: half * 1024 + (i + 1) * 128],
                    c.xs[:, t, ssl], c.ident_s[:]), inc="mm")
            p.wait("vector", "mm", mm)
            vv = p.emit("vector", lambda e, half=half: e.tensor_copy(
                c.xs_tm[:, half * 8:(half + 1) * 8, :],
                c.pWb[:, half * 1024:(half + 1) * 1024].rearrange("p (a t) -> p a t", a=8)),
                inc="vec")
            c.free["pw"] = ("vec", vv)
        c.pe_wait("pz")
        mm = p.emit("tensor", lambda e: e.transpose(
            c.pZb[:, 0:64], c.Bt[:, ssl], c.ident_s[0:64, 0:64]), inc="mm")
        p.wait("vector", "mm", mm)
        vv = p.emit("vector", lambda e: e.tensor_copy(c.b_tm[:], c.pZb[:, 0:64]), inc="vec")
        c.free["pz"] = ("vec", vv)
        p.wait("tensor", "vec", v_cum)
        c.pe_wait("py")
        mm = p.emit("tensor", lambda e: e.transpose(
            c.pY[:, 0:32], c.cumA[:, ssl], c.identf_s[0:32, 0:32]), inc="mm")
        mm = p.emit("tensor", lambda e: e.transpose(
            c.pY[:, 32:64], c.dtf[:, ssl], c.identf_s[0:32, 0:32]), inc="mm")
        p.wait("vector", "mm", mm)
        p.emit("vector", lambda e: e.tensor_copy(c.ca_tm[:], c.pY[:, 0:32]))
        vv = p.emit("vector", lambda e: e.tensor_copy(c.dt_tm[:], c.pY[:, 32:64]), inc="vec")
        c.free["py"] = ("vec", vv)
        v_tp = p.count("vec")

        c.pe_wait("pp")
        mm = p.emit("tensor", lambda e: e.matmul(
            c.pP[:, 0:128], c.Bt[:, ssl], c.Ct[:, ssl], start=True, stop=True), inc="mm")
        p.wait("vector", "mm", mm)
        v_g = p.emit("vector", lambda e: e.tensor_tensor(
            c.gmsk[:], c.pP[:, 0:128], c.triu_s[:], ALU.mult), inc="vec")
        c.free["pp"] = ("vec", v_g)
        p.emit("vector", lambda e: e.tensor_tensor(
            c.dxs[:], c.xs[:, :, ssl],
            c.dexp_s[:].unsqueeze(2).broadcast_to([128, NZT, 128]), ALU.mult), inc="vec")

        prev_hg_ymm = 0
        prev_hg_ae = 0
        prev_hg_vread = 0
        for hg in range(4):
            h0 = hg * 8
            c.pe_wait("pw")
            p.wait("tensor", "vec", v_tp)
            mm = None
            for i in range(8):
                mm = p.emit("tensor", lambda e, h=h0 + i, i=i: e.matmul(
                    c.pW[:, i * 128:(i + 1) * 128], c.sel_s[:, h, :], c.cumA[:, ssl],
                    start=True, stop=True), inc="mm")
            p.wait("vector", "mm", mm)
            p.wait("vector", "act", prev_hg_ae)
            p.emit("vector", lambda e, h0=h0: e.tensor_tensor(
                c.dmsk[:], c.ca_tm[:, h0:h0 + 8].unsqueeze(2).broadcast_to([128, 8, 128]),
                c.pW[:].rearrange("p (a t) -> p a t", a=8), ALU.subtract), inc="vec")
            v_d = p.count("vec")
            c.free["pw"] = ("vec", v_d)
            p.wait("scalar", "vec", v_d)
            p.wait("scalar", "vec", prev_hg_vread)
            p.emit("scalar", lambda e: e.activation(c.dmin[:], c.dmsk[:], AF.Relu))
            p.emit("scalar", lambda e: e.activation(c.emsk[:], c.dmin[:], AF.Exp, scale=-1.0),
                   inc="act")
            a_e = p.count("act")
            prev_hg_ae = a_e
            p.wait("vector", "act", a_e)
            p.emit("vector", lambda e: e.tensor_copy(
                c.ecol[:].unsqueeze(2), c.emsk[:, :, 127:128]), inc="vec")
            p.wait("vector", "vec", v_g)
            p.wait("vector", "mm", prev_hg_ymm)
            v_w = p.emit("vector", lambda e: e.tensor_tensor(
                c.wmsk[:], c.emsk[:], c.gmsk[:].unsqueeze(1).broadcast_to([128, 8, 128]),
                ALU.mult), inc="vec")
            c.pe_wait("pp")
            p.wait("tensor", "act", a_pfm)
            mm = None
            for i in range(8):
                mm = p.emit("tensor", lambda e, h=h0 + i, i=i: e.matmul(
                    c.pP[0:64, i * 128:(i + 1) * 128], c.selb_s[:, h, :], c.pfm[:, ssl],
                    start=True, stop=True), inc="mm")
            p.wait("vector", "mm", mm)
            p.emit("vector", lambda e: e.tensor_tensor(
                c.chat[:], c.Ct[:, ssl].unsqueeze(1).broadcast_to([64, 8, 128]),
                c.pP[0:64, :].rearrange("p (a t) -> p a t", a=8), ALU.mult))
            v_chat = p.emit("vector", lambda e, h0=h0: e.tensor_copy(
                c.gcol[:, h0:h0 + 8].unsqueeze(2),
                c.pP[0:64, :].rearrange("p (a t) -> p a t", a=8)[:, :, 127:128]), inc="vec")
            c.free["pp"] = ("vec", v_chat)

            last_vread = 0
            for i in range(8):
                h = h0 + i
                quad = (h % 8) // 4
                bank = [c.pY, c.pZ][quad]
                bqname = ["py", "pz"][quad]
                qoff = (h % 4 // 2) * 128
                poff = (h % 2) * 64
                xtile, xhalf = h // 2, h % 2
                slot = i % 2
                p.wait("vector", "vec", v_tp)
                p.wait("vector", "mm", xt_free[slot])
                v_xt = p.emit("vector", lambda e, slot=slot, xtile=xtile, xhalf=xhalf, h=h:
                              e.tensor_scalar(
                    c.xt_tm[:, slot, :], c.xs_tm[:, xtile, xhalf * 64:(xhalf + 1) * 64],
                    c.dt_tm[:, h:h + 1], None, ALU.mult), inc="vec")
                v_bh = p.emit("vector", lambda e, slot=slot, i=i: e.tensor_scalar(
                    c.b_hat[:, slot, :], c.b_tm[:], c.ecol[:, i:i + 1], None, ALU.mult),
                    inc="vec")
                last_vread = v_bh
                p.wait("tensor", "vec", v_w)
                p.wait("tensor", "vec", v_chat)
                p.wait("tensor", "vec", v_xt)
                if h % 4 == 0:
                    c.pe_wait(bqname)
                tp = (0, poff)
                p.emit("tensor", lambda e, slot=slot, i=i, bank=bank, qoff=qoff, poff=poff,
                       tp=tp: e.matmul(
                    bank[poff:poff + 64, qoff:qoff + 128], c.xt_tm[:, slot, :],
                    c.wmsk[:, i, :], start=True, stop=False, tile_position=tp))
                mmY = p.emit("tensor", lambda e, i=i, h=h, par=par, bank=bank, qoff=qoff,
                             poff=poff, tp=tp: e.matmul(
                    bank[poff:poff + 64, qoff:qoff + 128], c.hs[:, par, h, :],
                    c.chat[:, i, :], start=False, stop=True, tile_position=tp), inc="mm")
                prev_hg_ymm = mmY
                if h % 4 == 3:
                    p.wait("vector", "mm", mmY)
                    vv = p.emit("vector", lambda e, h=h, bank=bank, ssl=ssl: e.tensor_tensor(
                        c.y_bf[:, (h // 4) * 2:(h // 4) * 2 + 2, ssl],
                        c.dxs[:, (h // 4) * 2:(h // 4) * 2 + 2, :],
                        bank[:, 0:256].rearrange("p (a t) -> p a t", a=2), ALU.add), inc="vec")
                    c.free[bqname] = ("vec", vv)
                bankH = [c.pA, c.pB][h % 2]
                bhname = ["pa", "pb"][h % 2]
                p.wait("tensor", "vec", v_bh)
                c.pe_wait(bhname)
                mmH = p.emit("tensor", lambda e, slot=slot, bankH=bankH: e.matmul(
                    bankH[0:64, 0:64], c.b_hat[:, slot, :], c.xt_tm[:, slot, :],
                    start=True, stop=True), inc="mm")
                xt_free[slot] = mmH
                p.wait("vector", "mm", mmH)
                vv = p.emit("vector", lambda e, h=h, par=par, bankH=bankH: e.scalar_tensor_tensor(
                    c.hs[:, 1 - par, h, :], c.hs[:, par, h, :], c.gcol[:, h:h + 1],
                    bankH[0:64, 0:64], ALU.mult, ALU.add), inc="vec")
                c.free[bhname] = ("vec", vv)
            prev_hg_vread = last_vread


    def emit_dbg(c, v_yn):
        p = c.p
        p.wait("gpsimd", "vec", v_yn)
        for nm, src in [("d_xs", c.xs[:, :, :]), ("d_b", c.Bt[:]), ("d_c", c.Ct[:]),
                        ("d_dt", c.dtf[:]), ("d_y", c.y_bf[:]),
                        ("d_zx", c.xbc[:, :, 3:3 + TG]), ("d_yn", c.y_bf[:])]:
            if nm in c.dbg_t:
                p.dma("gpsimd", c.dbg_t[nm], src, inc="gp")
    return (build, CFG)


def _mod_l0f():

    LSEQ = 2048
    D = 1024
    TG = 512
    NKD = D // 128
    NSEQ = 2
    EPS = 1e-5
    CFG = {"pfx": "", "xin": None, "xout": None}


    class Ctx:
        pass


    def build(nc, dbg=False, ext=None):
        stack = contextlib.ExitStack()
        E = stack.enter_context
        c = Ctx()
        c.nc = nc

        def dram(name, shape, dt, kind="ExternalInput"):
            return nc.dram_tensor(CFG["pfx"] + name, shape, dt, kind=kind).ap()

        c.xT = CFG["xin"] if CFG["xin"] is not None else dram("xT", [NSEQ, D, LSEQ], F32)
        c.kern = dram("kern", [D, D], BF16)
        c.s_in = dram("s_in", [1, 1], F32)
        c.s_k = dram("s_k", [1, 1], F32)
        c.bias8 = dram("bias8", [128, 8], F32)
        c.c_ones_b = dram("c_ones_b", [128, 1], BF16)
        c.c_ones_row = dram("c_ones_row", [1, 128], F32)
        c.c_eps_row = dram("c_eps_row", [1, 1], F32)
        c.x2T = CFG["xout"] if CFG["xout"] is not None else dram("x2T", [NSEQ, D, LSEQ], F32, kind="ExternalOutput")

        def S(name, shape, dt):
            t = E(nc.sbuf_tensor(CFG["pfx"] + name, shape, dt))
            setattr(c, name, t)
            return t

        S("x_fm", [128, NKD, TG], F32)
        S("xb", [128, NKD, TG], BF16)
        S("xq", [128, NKD, TG], FP8)
        S("sq", [128, NKD, TG], BF16)
        S("ybf", [128, NKD, TG], BF16)
        S("s1", [128, NKD, TG], F32)
        S("t1", [128, TG], F32)
        S("r0", [1, TG], F32)
        S("r1", [1, TG], F32)
        S("r2", [1, TG], F32)
        S("kbf", [128, NKD, D], BF16)
        S("kq", [128, NKD, D], FP8)
        S("scl", [1, 4], F32)
        S("sc_col", [128, 3], F32)
        S("bias_s", [128, 8], F32)
        S("onesb_s", [128, 1], BF16)
        S("onesr_s", [1, 128], F32)
        S("epsr_s", [1, 1], F32)

        c.pA = E(nc.psum_tensor(CFG["pfx"] + "pA", [128, 512], F32))
        c.pB = E(nc.psum_tensor(CFG["pfx"] + "pB", [128, 512], F32))
        c.pW = E(nc.psum_tensor(CFG["pfx"] + "pW", [128, 1024], F32))
        c.pP = E(nc.psum_tensor(CFG["pfx"] + "pP", [128, 1024], F32))

        p = ext if ext is not None else Prog(nc, stack)
        c.p = p
        c.free = {}

        def pe_wait(name):
            if name in c.free:
                ctr, v = c.free[name]
                p.wait("tensor", ctr, v)

        c.pe_wait = pe_wait

        for dst, src in [(c.bias_s, c.bias8), (c.onesb_s, c.c_ones_b),
                         (c.onesr_s, c.c_ones_row), (c.epsr_s, c.c_eps_row)]:
            p.dma("sync", dst[:], src, inc="ld2")
        p.dma("sync", c.scl[0:1, 0:1], c.s_in, inc="ld2")
        p.dma("sync", c.scl[0:1, 1:2], c.s_k, inc="ld2")
        p.dma("sync", c.kbf[:], c.kern.rearrange("(k p) f -> p k f", p=128), inc="ld2")
        n_ld = p.count("ld2")
        p.wait("vector", "ld2", n_ld)
        p.emit("vector", lambda e: e.reciprocal(c.scl[0:1, 2:3], c.scl[0:1, 0:1]))
        p.emit("vector", lambda e: e.reciprocal(c.scl[0:1, 3:4], c.scl[0:1, 1:2]))
        p.emit("vector", lambda e: e.tensor_tensor(
            c.scl[0:1, 1:2], c.scl[0:1, 0:1], c.scl[0:1, 1:2], ALU.mult), inc="vec")
        v_scl = p.count("vec")
        p.wait("tensor", "vec", v_scl)
        p.emit("tensor", lambda e: e.matmul(
            c.pW[:, 0:1], c.onesr_s[:], c.scl[0:1, 2:3], start=True, stop=True))
        p.emit("tensor", lambda e: e.matmul(
            c.pW[:, 1:2], c.onesr_s[:], c.scl[0:1, 1:2], start=True, stop=True))
        p.emit("tensor", lambda e: e.matmul(
            c.pW[:, 2:3], c.onesr_s[:], c.scl[0:1, 3:4], start=True, stop=True), inc="mm")
        mm_sc = p.count("mm")
        p.wait("vector", "mm", mm_sc)
        p.emit("vector", lambda e: e.tensor_copy(c.sc_col[:], c.pW[:, 0:3]), inc="vec")
        v_sc = p.count("vec")
        c.free["pw"] = ("vec", v_sc)
        p.wait("scalar", "vec", v_sc)
        p.emit("scalar", lambda e: e.activation(
            c.kq[:].rearrange("p a b -> p (a b)"), c.kbf[:].rearrange("p a b -> p (a b)"),
            AF.Copy, scale=c.sc_col[:, 2:3]), inc="act")
        a_kq = p.count("act")

        c.xfm_free = 0
        c.xb_free_mm = 0
        for b in range(NSEQ):
            for g in range(LSEQ // TG):
                emit_group(c, b, g, a_kq)
        if ext is not None:
            return stack
        p.wait("gpsimd", "st", p.count("st"))
        block = E(nc.Block())
        p.finalize(block)
        stack.close()
        return nc


    def emit_group(c, b, g, a_kq):
        p = c.p
        gsl = slice(g * TG, (g + 1) * TG)
        p.wait("sync", "vec", c.xfm_free)
        ld = p.dma("sync", c.x_fm[:],
                   c.xT[b].rearrange("(k p) t -> p k t", p=128)[:, :, gsl], inc="ld2")
        p.wait("vector", "ld2", ld)
        p.wait("vector", "mm", c.xb_free_mm)
        p.emit("vector", lambda e: e.tensor_copy(c.xb[:], c.x_fm[:]), inc="vec")
        v_xb = p.count("vec")
        p.wait("scalar", "vec", v_xb)
        p.emit("scalar", lambda e: e.activation(
            c.xq[:].rearrange("p a b -> p (a b)"), c.xb[:].rearrange("p a b -> p (a b)"),
            AF.Copy, scale=c.sc_col[:, 0:1]), inc="act")
        a_xq = p.count("act")
        for m in range(NKD):
            bank = [c.pA, c.pB][m % 2]
            bname = ["pa", "pb"][m % 2]
            p.wait("tensor", "act", max(a_kq, a_xq))
            c.pe_wait(bname)
            mm = None
            for k in range(NKD):
                mm = p.emit("tensor", lambda e, bank=bank, m=m, k=k: e.matmul(
                    bank[:], c.kq[:, k, m * 128:(m + 1) * 128], c.xq[:, k, :],
                    start=(k == 0), stop=(k == NKD - 1)), inc=("mm" if k == NKD - 1 else None))
            p.wait("vector", "mm", mm)
            vv = p.emit("vector", lambda e, bank=bank, m=m: e.tensor_scalar(
                c.ybf[:, m, :], bank[:], c.sc_col[:, 1:2], c.bias_s[:, m:m + 1],
                ALU.mult, ALU.add), inc="vec")
            c.free[bname] = ("vec", vv)
        for m in range(NKD):
            p.emit("vector", lambda e, m=m: e.tensor_tensor(
                c.s1[:, m, :], c.x_fm[:, m, :], c.ybf[:, m, :], ALU.add), inc="vec")
        v_s1 = p.count("vec")
        c.xfm_free = v_s1
        p.wait("vector", "act", a_xq)
        p.emit("vector", lambda e: e.tensor_copy(c.xb[:], c.s1[:]), inc="vec")
        v_xb2 = p.count("vec")
        p.wait("scalar", "vec", v_xb2)
        p.emit("scalar", lambda e: e.activation(c.sq[:], c.xb[:], AF.Square), inc="act")
        a_sq = p.count("act")
        p.wait("tensor", "vec", v_xb2)
        p.wait("tensor", "act", a_sq)
        c.pe_wait("pw")
        for k in range(NKD):
            p.emit("tensor", lambda e, k=k: e.matmul(
                c.pW[0:1, 0:512], c.onesb_s[:], c.xb[:, k, :],
                start=(k == 0), stop=(k == NKD - 1)), inc=("mm" if k == NKD - 1 else None))
        for k in range(NKD):
            p.emit("tensor", lambda e, k=k: e.matmul(
                c.pW[0:1, 512:1024], c.onesb_s[:], c.sq[:, k, :],
                start=(k == 0), stop=(k == NKD - 1)), inc=("mm" if k == NKD - 1 else None))
        mm_st = p.count("mm")
        c.xb_free_mm = mm_st
        p.wait("vector", "mm", mm_st)
        p.emit("vector", lambda e: e.tensor_scalar(
            c.r0[:], c.pW[0:1, 0:512], 1.0 / D, None, ALU.mult))
        p.emit("vector", lambda e: e.tensor_tensor(c.r1[:], c.r0[:], c.r0[:], ALU.mult))
        p.emit("vector", lambda e: e.scalar_tensor_tensor(
            c.r2[:], c.pW[0:1, 512:1024], 1.0 / D, c.r1[:], ALU.mult, ALU.subtract), inc="vec")
        v_rows = p.count("vec")
        c.free["pw"] = ("vec", v_rows)
        p.wait("scalar", "vec", v_rows)
        p.emit("scalar", lambda e: e.activation(c.r1[:], c.r2[:], AF.Ln, bias=c.epsr_s[:]))
        p.emit("scalar", lambda e: e.activation(c.r2[:], c.r1[:], AF.Exp, scale=-0.5), inc="act")
        a_rows = p.count("act")
        p.wait("tensor", "act", a_rows)
        c.pe_wait("pp")
        p.emit("tensor", lambda e: e.matmul(
            c.pP[:, 0:512], c.onesr_s[:], c.r0[:], start=True, stop=True))
        p.emit("tensor", lambda e: e.matmul(
            c.pP[:, 512:1024], c.onesr_s[:], c.r2[:], start=True, stop=True), inc="mm")
        mm_bc = p.count("mm")
        p.wait("vector", "mm", mm_bc)
        v_x2 = None
        for k in range(NKD):
            p.emit("vector", lambda e, k=k: e.tensor_tensor(
                c.t1[:], c.s1[:, k, :], c.pP[:, 0:512], ALU.subtract))
            v_x2 = p.emit("vector", lambda e, k=k: e.tensor_tensor(
                c.s1[:, k, :], c.t1[:], c.pP[:, 512:1024], ALU.mult), inc="vec")
        c.free["pp"] = ("vec", v_x2)
        p.wait("sync", "vec", v_x2)
        p.dma("sync", c.x2T[b].rearrange("(k p) t -> p k t", p=128)[:, :, gsl],
              c.s1[:], inc="st")
    return (build, CFG)


def _mod_l1d():

    LSEQ = 2048
    D = 1024
    TG = 512
    CH = 64
    NCH = TG // CH
    NKD = D // 128
    NH = 12
    DK = 64
    DV = 128
    WALL = 4736
    NMT = WALL // 128
    NCV = 24
    NSEQ = 2
    EPS5 = 1e-5
    EPS6 = 1e-6
    LN8 = float(np.log(8.0))
    CFG = {"pfx": "", "xin": None, "xout": None}


    class Ctx:
        pass


    def build(nc, dbg=False, ext=None):
        stack = contextlib.ExitStack()
        E = stack.enter_context
        c = Ctx()
        c.nc = nc

        def dram(name, shape, dt, kind="ExternalInput"):
            return nc.dram_tensor(CFG["pfx"] + name, shape, dt, kind=kind).ap()

        c.xT = CFG["xin"] if CFG["xin"] is not None else dram("xT", [NSEQ, D, LSEQ], F32)
        c.wall = dram("wall", [D, WALL], F32)
        c.woT = dram("woT", [NH * DV, D], F32)
        c.convw = dram("convw", [128, NCV, 4], F32)
        c.convb_z = dram("convb_z", [1, 1], F32)
        c.dtb12 = dram("dtb12", [12, 1], F32)
        c.alog12 = dram("alog12", [12, 1], F32)
        c.bb44 = dram("bb44", [44, 1], F32)
        c.gnw = dram("gnw", [128, 1], F32)
        c.c_ident = dram("c_ident", [128, 128], BF16)
        c.c_ident_h = dram("c_ident_h", [128, 128], FP16)
        c.c_bdones_h = dram("c_bdones_h", [128, 128], FP16)
        c.c_ones_h = dram("c_ones_h", [128, 1], FP16)
        c.c_identf = dram("c_identf", [128, 128], F32)
        c.c_bdones = dram("c_bdones", [128, 128], BF16)
        c.c_ptril_s = dram("c_ptril_s", [128, 128], BF16)
        c.c_ptriu_i = dram("c_ptriu_i", [128, 128], BF16)
        c.c_pdiag = dram("c_pdiag", [128, 128], BF16)
        c.c_selone = dram("c_selone", [12, 12, 128], F32)
        c.c_selp = dram("c_selp", [12, 6, 128], F32)
        c.c_selpb = dram("c_selpb", [12, 6, 64], F32)
        c.c_ones_b = dram("c_ones_b", [128, 1], BF16)
        c.c_ones_row = dram("c_ones_row", [1, 128], F32)
        c.c_eps_row = dram("c_eps_row", [1, 1], F32)
        c.c_eps5col = dram("c_eps5col", [128, 1], F32)
        c.c_eps6col = dram("c_eps6col", [128, 1], F32)
        c.c_one12 = dram("c_one12", [12, 1], F32)
        c.c_mln8 = dram("c_mln8", [128, 1], F32)
        c.x3T = CFG["xout"] if CFG["xout"] is not None else dram("x3T", [NSEQ, D, LSEQ], F32, kind="ExternalOutput")
        c.wall_scr = dram("wall_scr", [NKD, 128, WALL], BF16, kind="Internal")
        c.wo_scr = dram("wo_scr", [NH, 128, D], FP16, kind="Internal")

        def S(name, shape, dt):
            t = E(nc.sbuf_tensor(CFG["pfx"] + name, shape, dt))
            setattr(c, name, t)
            return t

        S("x_fm", [128, NKD, TG], F32)
        S("xb", [128, NKD, TG], BF16)
        S("qkv_bc", [128, NCV, TG + 3], FP16)
        S("qkv_s", [128, NCV, TG], FP16)
        S("qn", [128, 6, TG], FP16)
        S("kn", [128, 6, TG], FP16)
        S("ggf", [128, NH, TG], FP16)
        S("sgt", [128, TG], FP16)
        S("gtt", [128, TG], FP16)
        S("gkb", [44, TG], F32)
        S("sp1", [12, TG], F32)
        S("sp2", [12, TG], F32)
        S("sp3", [12, TG], F32)
        S("stackP", [128, 12], F32)
        S("bPS", [128, 6], F32)
        S("wS", [128, 6], F32)
        S("lgamS", [128, 6, NCH], F32)
        S("gS", [128, 6], F32)
        S("emsk", [128, 2, 128], FP16)
        S("emt", [128, 2, 128], FP16)
        S("amat", [128, 2, 128], FP16)
        S("atmat", [128, 128], FP16)
        S("apow", [128, 2, 6, 128], FP16)
        S("apw", [128, 2, 2, 128], FP16)
        S("ubuf", [128, 2, 2, 128], FP16)
        S("vtm", [128, 2, 128], FP16)
        S("kts", [128, 2, 128], FP16)
        S("rbt", [128, 2, 128], F32)
        S("rbt2", [128, 128], F32)
        S("rbt3", [128, 128], F32)
        S("t5", [128, 2, 128], F32)
        S("t7", [128, 2, 128], F32)
        S("ojunk", [128, 2, 128], FP16)
        S("sscol", [128, 2], F32)
        S("rstdo", [128, 2], F32)
        S("sqv", [128, 2, TG], FP16)
        S("rsqt", [128, 2, TG], F32)
        S("SS", [128, 2, 6, DV], FP16)
        S("wtmp", [128, 6], F32)
        S("o_fm", [128, NH, TG], FP16)
        S("ybf", [128, NKD, TG], BF16)
        S("t1", [128, TG], F32)
        S("r0", [1, TG], F32)
        S("r1", [1, TG], F32)
        S("r2", [1, TG], F32)
        c.wbuf = [S("wbuf0", [128, NKD, 256], BF16), S("wbuf1", [128, NKD, 256], BF16)]
        c.wob = [S("wob0", [128, NH, 128], FP16), S("wob1", [128, NH, 128], FP16)]
        S("convw_s", [128, NCV, 4], F32)
        S("diag", [128, NCV, 4, 128], FP16)
        S("dtb_s", [12, 1], F32)
        S("nA2_s", [12, 1], F32)
        S("bb_s", [44, 1], F32)
        S("gnw_s", [128, 1], F32)
        S("ident_s", [128, 128], BF16)
        S("ident_h", [128, 128], FP16)
        S("bdones_h", [128, 128], FP16)
        S("onesh_s", [128, 1], FP16)
        S("identf_s", [128, 128], F32)
        S("bdones_s", [128, 128], BF16)
        S("ptril_s", [128, 128], BF16)
        S("ptriu_s", [128, 128], BF16)
        S("pdiag_s", [128, 128], BF16)
        S("selone_s", [12, 12, 128], F32)
        S("selp_s", [12, 6, 128], F32)
        S("selpb_s", [12, 6, 64], F32)
        S("onesb_s", [128, 1], BF16)
        S("onesr_s", [1, 128], F32)
        S("epsr_s", [1, 1], F32)
        S("eps5c_s", [128, 1], F32)
        S("eps6c_s", [128, 1], F32)
        S("one12_s", [12, 1], F32)
        S("mln8_s", [128, 1], F32)
        c.s1 = c.qkv_s[:].rearrange("p a b -> p (a b)").bitcast(F32)[:, 0:NKD * TG] \
            .rearrange("p (k t) -> p k t", k=NKD)

        c.pA = E(nc.psum_tensor(CFG["pfx"] + "pA", [128, 512], F32))
        c.pB = E(nc.psum_tensor(CFG["pfx"] + "pB", [128, 512], F32))
        c.pW = E(nc.psum_tensor(CFG["pfx"] + "pW", [128, 1024], F32))
        c.pP = E(nc.psum_tensor(CFG["pfx"] + "pP", [128, 1024], F32))
        c.pY = E(nc.psum_tensor(CFG["pfx"] + "pY", [128, 512], F32))
        c.pZ = E(nc.psum_tensor(CFG["pfx"] + "pZ", [128, 512], F32))
        c.pYb = c.pY[:].bitcast(BF16)
        c.pZb = c.pZ[:].bitcast(BF16)
        c.pYh = c.pY[:].bitcast(FP16)
        c.pZh = c.pZ[:].bitcast(FP16)

        p = ext if ext is not None else Prog(nc, stack)
        c.p = p
        c.free = {}

        def pe_wait(name):
            if name in c.free:
                ctr, v = c.free[name]
                p.wait("tensor", ctr, v)

        c.pe_wait = pe_wait

        prepass(c)
        c.p.wait("tensor", "vec", c.psum_clean)
        c.xfm_free = 0
        c.xb_free_mm = 0
        c.wb_last = {0: 0, 1: 0}
        c.wob_last = {0: 0, 1: 0}
        c.conv_mm_last = 0
        c.scan_end_mm = 0
        c.s1_store = 0

        for b in range(NSEQ):
            p.wait("vector", "mm", c.scan_end_mm)
            p.wait("vector", "st", c.wogn_done)
            p.emit("vector", lambda e: e.memset(c.SS[:], 0.0))
            p.emit("vector", lambda e: e.memset(c.qkv_bc[:, :, 0:3], 0.0), inc="vec")
            for g in range(LSEQ // TG):
                emit_group(c, b, g)
        if ext is not None:
            return stack
        p.wait("gpsimd", "st", p.count("st"))
        block = E(nc.Block())
        p.finalize(block)
        stack.close()
        return nc


    def prepass(c):
        p = c.p
        pairs = [(c.convw_s, c.convw), (c.dtb_s, c.dtb12), (c.nA2_s, c.alog12),
                 (c.bb_s, c.bb44), (c.gnw_s, c.gnw),
                 (c.ident_s, c.c_ident), (c.identf_s, c.c_identf),
                 (c.ident_h, c.c_ident_h), (c.bdones_h, c.c_bdones_h),
                 (c.onesh_s, c.c_ones_h),
                 (c.bdones_s, c.c_bdones), (c.ptril_s, c.c_ptril_s),
                 (c.ptriu_s, c.c_ptriu_i), (c.pdiag_s, c.c_pdiag),
                 (c.selone_s, c.c_selone), (c.selp_s, c.c_selp), (c.selpb_s, c.c_selpb),
                 (c.onesb_s, c.c_ones_b), (c.onesr_s, c.c_ones_row),
                 (c.epsr_s, c.c_eps_row), (c.eps5c_s, c.c_eps5col),
                 (c.eps6c_s, c.c_eps6col), (c.one12_s, c.c_one12),
                 (c.mln8_s, c.c_mln8)]
        for dst, src in pairs:
            p.dma("sync", dst[:], src, inc="ld2")
        n_small = p.count("ld2")
        p.dma("gpsimd", c.wall_scr, c.wall.rearrange("(k p) f -> k p f", p=128), inc="ldp")
        p.dma("gpsimd", c.wo_scr, c.woT.rearrange("(k p) f -> k p f", p=128), inc="ldp")
        c.wscr_done = p.count("ldp")
        p.wait("scalar", "ld2", n_small)
        p.emit("scalar", lambda e: e.activation(c.sp1[:, 0:1], c.nA2_s[:], AF.Exp))
        p.emit("scalar", lambda e: e.mul(c.nA2_s[:], c.sp1[:, 0:1], -1.0), inc="act")
        c.nA_ready = p.count("act")
        p.wait("vector", "ld2", n_small)
        for pb_ in [c.pA, c.pB, c.pW, c.pP, c.pY, c.pZ]:
            p.emit("vector", lambda e, pb_=pb_: e.memset(pb_[:], 0.0))
        p.emit("vector", lambda e: e.memset(c.kts[:], 0.0))
        p.emit("vector", lambda e: e.memset(c.gkb[:], 0.0))
        p.emit("vector", lambda e: e.engine_nop(), inc="vec")
        c.psum_clean = p.count("vec")
        for i in range(NCV):
            for j in range(4):
                p.emit("vector", lambda e, i=i, j=j: e.tensor_scalar(
                    c.diag[:, i, j, :], c.ident_s[:], c.convw_s[:, i, j:j + 1], None, ALU.mult))
        p.emit("vector", lambda e: e.engine_nop(), inc="vec")
        c.prep_v = p.count("vec")
        p.wait("sync", "ldp", c.wscr_done)
        woview = c.qkv_s[:].rearrange("p a b -> p (a b)")[:, 0:NH * D].rearrange(
            "p (k f) -> p k f", k=NH)
        wo2view = c.qkv_bc[:].rearrange("p a b -> p (a b)")[:, 0:NH * D].rearrange(
            "p (k f) -> p k f", k=NH)
        lw = p.dma("sync", woview, c.wo_scr.rearrange("k p f -> p k f"), inc="ld2")
        p.wait("vector", "ld2", lw)
        vv = None
        for k in range(NH):
            vv = p.emit("vector", lambda e, k=k, woview=woview, wo2view=wo2view:
                        e.tensor_scalar(wo2view[:, k, :], woview[:, k, :],
                                        c.gnw_s[:], None, ALU.mult), inc="vec")
        p.wait("sync", "vec", vv)
        p.dma("sync", c.wo_scr.rearrange("k p f -> p k f"), wo2view, inc="st")
        c.wogn_done = p.count("st")
        c.wogn_vec = vv


    def emit_group(c, b, g):
        p = c.p
        gsl = slice(g * TG, (g + 1) * TG)

        p.wait("sync", "vec", c.xfm_free)
        ld = p.dma("sync", c.x_fm[:],
                   c.xT[b].rearrange("(k p) t -> p k t", p=128)[:, :, gsl], inc="ld2")
        p.wait("vector", "ld2", ld)
        p.wait("vector", "mm", c.xb_free_mm)
        p.emit("vector", lambda e: e.tensor_copy(c.xb[:], c.x_fm[:]), inc="vec")
        v_xb = p.count("vec")

        if not (g == 0):
            p.wait("vector", "mm", c.conv_mm_last)
            p.emit("vector", lambda e: e.tensor_copy(
                c.qkv_bc[:, :, 0:3], c.qkv_bc[:, :, TG:TG + 3]), inc="vec")
        v_halo = p.count("vec")

        first = (b == 0 and g == 0)
        pend = {}
        for mp in range(19):
            p.wait("gpsimd", "mm", c.wb_last[mp % 2])
            if first and mp == 0:
                p.wait("gpsimd", "ldp", c.wscr_done)
            fs, fe = mp * 256, min(mp * 256 + 256, WALL)
            pend[mp] = p.dma("gpsimd", c.wbuf[mp % 2][:, :, 0:fe - fs],
                             c.wall_scr[:, :, fs:fe].rearrange("k p f -> p k f"),
                             inc="wb%d" % (mp % 2))
            if True:
                for m in (2 * mp, 2 * mp + 1):
                    if m >= NMT:
                        continue
                    off = (m % 2) * 128
                    bank = [c.pA, c.pB][m % 2]
                    bname = ["pa", "pb"][m % 2]
                    p.wait("tensor", "wb%d" % (mp % 2), pend[mp])
                    p.wait("tensor", "vec", v_xb)
                    c.pe_wait(bname)
                    mm = None
                    for k in range(NKD):
                        mm = p.emit("tensor", lambda e, bank=bank, mp=mp, k=k, off=off: e.matmul(
                            bank[:], c.wbuf[mp % 2][:, k, off:off + 128], c.xb[:, k, :],
                            start=(k == 0), stop=(k == NKD - 1)),
                            inc=("mm" if k == NKD - 1 else None))
                    c.wb_last[mp % 2] = mm
                    if m < NCV:
                        p.wait("vector", "mm", mm)
                        p.wait("vector", "vec", v_halo)
                        vv = p.emit("vector", lambda e, bank=bank, m=m: e.tensor_copy(
                            c.qkv_bc[:, m, 3:3 + TG], bank[:]), inc="vec")
                        c.free[bname] = ("vec", vv)
                    elif m < 36:
                        gi = m - 24
                        p.wait("vector", "mm", mm)
                        vv = p.emit("vector", lambda e, bank=bank: e.tensor_copy(
                            c.gtt[:], bank[:]), inc="vec")
                        p.wait("scalar", "vec", vv)
                        aa = p.emit("scalar", lambda e, bank=bank: e.activation(
                            c.sgt[:], bank[:], AF.Sigmoid), inc="act")
                        c.free[bname] = ("act", aa)
                        p.wait("vector", "act", aa)
                        vv = p.emit("vector", lambda e, gi=gi: e.tensor_tensor(
                            c.ggf[:, gi, :], c.gtt[:], c.sgt[:], ALU.mult), inc="vec")
                    else:
                        p.wait("vector", "mm", mm)
                        vv = p.emit("vector", lambda e, bank=bank: e.tensor_scalar(
                            c.gkb[0:12, :], bank[0:12, :], c.dtb_s[:], None, ALU.add),
                            inc="vec")
                        p.wait("scalar", "vec", vv)
                        aa = p.emit("scalar", lambda e, bank=bank: e.activation(
                            c.gkb[32:44, :], bank[32:44, :], AF.Sigmoid,
                            bias=c.bb_s[32:44, 0:1]), inc="act")
                        c.free[bname] = ("act", aa)
        v_qkvbc = p.count("vec")
        a_beta = p.count("act")

        for i in range(NCV):
            bank = [c.pA, c.pB][i % 2]
            bname = ["pa", "pb"][i % 2]
            p.wait("tensor", "vec", v_qkvbc)
            c.pe_wait(bname)
            mm = None
            for j in range(4):
                mm = p.emit("tensor", lambda e, bank=bank, i=i, j=j: e.matmul(
                    bank[:], c.diag[:, i, j, :], c.qkv_bc[:, i, j:j + TG],
                    start=(j == 0), stop=(j == 3)), inc=("mm" if j == 3 else None))
            c.conv_mm_last = mm
            p.wait("scalar", "mm", mm)
            p.wait("scalar", "mm", c.scan_end_mm)
            p.wait("scalar", "st", c.s1_store)
            aa = p.emit("scalar", lambda e, bank=bank, i=i: e.activation(
                c.qkv_s[:, i, :], bank[:], AF.Silu), inc="act")
            c.free[bname] = ("act", aa)
        a_conv = p.count("act")

        pmm = {0: 0, 1: 0}
        ptt = {0: 0, 1: 0}
        for i in range(12):
            ip = i % 2
            sqv = c.sqv[:, ip, :]
            rsqt = c.rsqt[:, ip, :]
            p.wait("vector", "act", a_conv)
            p.wait("vector", "mm", pmm[ip])
            p.emit("vector", lambda e, i=i, sqv=sqv: e.tensor_tensor(
                sqv, c.qkv_s[:, i, :], c.qkv_s[:, i, :], ALU.mult), inc="vec")
            v_sq = p.count("vec")
            bank = [c.pA, c.pB][i % 2]
            bname = ["pa", "pb"][i % 2]
            p.wait("tensor", "vec", v_sq)
            c.pe_wait(bname)
            mm = p.emit("tensor", lambda e, bank=bank, sqv=sqv: e.matmul(
                bank[:], c.bdones_h[:], sqv, start=True, stop=True), inc="mm")
            pmm[ip] = mm
            p.wait("scalar", "mm", mm)
            p.wait("scalar", "vec", ptt[ip])
            p.emit("scalar", lambda e, bank=bank: e.activation(
                c.t1[:], bank[:], AF.Ln, bias=c.eps6c_s[:]))
            if i < 6:
                aa = p.emit("scalar", lambda e, rsqt=rsqt: e.activation(
                    rsqt, c.t1[:], AF.Exp, scale=-0.5, bias=c.mln8_s[:]), inc="act")
            else:
                aa = p.emit("scalar", lambda e, rsqt=rsqt: e.activation(
                    rsqt, c.t1[:], AF.Exp, scale=-0.5), inc="act")
            c.free[bname] = ("act", aa)
            p.wait("vector", "act", aa)
            dst = c.qn if i < 6 else c.kn
            ptt[ip] = p.emit("vector", lambda e, i=i, dst=dst, rsqt=rsqt: e.tensor_tensor(
                dst[:, i % 6, :], c.qkv_s[:, i, :], rsqt, ALU.mult), inc="vec")
        v_qkn = p.count("vec")

        p.wait("scalar", "vec", p.count("vec"))
        p.wait("scalar", "act", a_beta)
        p.emit("scalar", lambda e: e.activation(c.sp1[:], c.gkb[0:12, :], AF.Abs))
        p.emit("scalar", lambda e: e.activation(c.sp2[:], c.sp1[:], AF.Exp, scale=-1.0))
        p.emit("scalar", lambda e: e.activation(c.sp1[:], c.sp2[:], AF.Ln, bias=c.one12_s[:]))
        p.emit("scalar", lambda e: e.activation(c.sp2[:], c.gkb[0:12, :], AF.Relu), inc="act")
        a_sp = p.count("act")
        p.wait("vector", "act", a_sp)
        p.wait("vector", "act", c.nA_ready)
        p.emit("vector", lambda e: e.tensor_tensor(c.sp3[:], c.sp2[:], c.sp1[:], ALU.add))
        p.emit("vector", lambda e: e.tensor_scalar(
            c.sp1[:], c.sp3[:], c.nA2_s[:], None, ALU.mult), inc="vec")
        for ch in range(NCH):
            p.emit("vector", lambda e, ch=ch: e.tensor_tensor_scan(
                c.gkb[0:12, ch * CH:(ch + 1) * CH],
                c.one12_s[:].broadcast_to([12, CH]),
                c.sp1[:, ch * CH:(ch + 1) * CH], 0.0, ALU.mult, ALU.add), inc="vec")
        v_cum = p.count("vec")
        p.wait("tensor", "vec", v_cum)
        c.pe_wait("pw")
        mm = None
        for j in range(6):
            mm = p.emit("tensor", lambda e, j=j: e.matmul(
                c.pW[:, j * 8:(j + 1) * 8], c.selp_s[:, j, :],
                c.gkb[0:12, CH - 1::CH], start=True, stop=True), inc="mm")
        p.wait("vector", "mm", mm)
        p.emit("vector", lambda e: e.tensor_copy(
            c.lgamS[:].rearrange("p a b -> p (a b)"), c.pW[:, 0:48]), inc="vec")
        v_lg = p.count("vec")
        c.free["pw"] = ("vec", v_lg)

        for ch in range(NCH):
            emit_chunk(c, b, g, ch, v_cum, v_qkn)
        c.scan_end_mm = p.count("mm")

        wo_ld = {}

        def load_wo(m):
            p.wait("gpsimd", "mm", c.wob_last[m % 2])
            if first and m == 0:
                p.wait("gpsimd", "st", c.wogn_done)
            wo_ld[m] = p.dma("gpsimd", c.wob[m % 2][:],
                             c.wo_scr[:, :, m * 128:(m + 1) * 128].rearrange("k p f -> p k f"),
                             inc="wo%d" % (m % 2))

        load_wo(0)
        load_wo(1)
        v_ofm = p.count("vec")
        for m in range(NKD):
            bank = [c.pA, c.pB][m % 2]
            bname = ["pa", "pb"][m % 2]
            p.wait("tensor", "wo%d" % (m % 2), wo_ld[m])
            p.wait("tensor", "vec", v_ofm)
            c.pe_wait(bname)
            mm = None
            for k in range(NH):
                mm = p.emit("tensor", lambda e, bank=bank, m=m, k=k: e.matmul(
                    bank[:], c.wob[m % 2][:, k, :], c.o_fm[:, k, :],
                    start=(k == 0), stop=(k == NH - 1)), inc=("mm" if k == NH - 1 else None))
            c.wob_last[m % 2] = mm
            c.xb_free_mm = mm
            if m + 2 < NKD:
                load_wo(m + 2)
            p.wait("vector", "mm", mm)
            vv = p.emit("vector", lambda e, bank=bank, m=m: e.tensor_tensor(
                c.s1[:, m, :], c.x_fm[:, m, :], bank[:], ALU.add), inc="vec")
            c.free[bname] = ("vec", vv)
        v_s1 = p.count("vec")
        c.xfm_free = v_s1

        p.emit("vector", lambda e: e.tensor_copy(c.xb[:], c.s1[:]), inc="vec")
        v_xb2 = p.count("vec")
        p.wait("scalar", "vec", v_xb2)
        p.wait("scalar", "mm", c.scan_end_mm)
        sqview = c.ggf[:].rearrange("p a b -> p (a b)")[:, 0:NKD * TG].rearrange(
            "p (k t) -> p k t", k=NKD)
        p.emit("scalar", lambda e, sqview=sqview: e.activation(
            sqview, c.xb[:], AF.Square), inc="act")
        a_sq2 = p.count("act")
        p.wait("tensor", "vec", v_xb2)
        p.wait("tensor", "act", a_sq2)
        c.pe_wait("pw")
        for k in range(NKD):
            p.emit("tensor", lambda e, k=k: e.matmul(
                c.pW[0:1, 0:512], c.onesb_s[:], c.xb[:, k, :],
                start=(k == 0), stop=(k == NKD - 1)), inc=("mm" if k == NKD - 1 else None))
        for k in range(NKD):
            p.emit("tensor", lambda e, k=k, sqview=sqview: e.matmul(
                c.pW[0:1, 512:1024], c.onesh_s[:], sqview[:, k, :],
                start=(k == 0), stop=(k == NKD - 1)), inc=("mm" if k == NKD - 1 else None))
        mm_st = p.count("mm")
        p.wait("vector", "mm", mm_st)
        p.emit("vector", lambda e: e.tensor_scalar(
            c.r0[:], c.pW[0:1, 0:512], 1.0 / D, None, ALU.mult))
        p.emit("vector", lambda e: e.tensor_tensor(c.r1[:], c.r0[:], c.r0[:], ALU.mult))
        p.emit("vector", lambda e: e.scalar_tensor_tensor(
            c.r2[:], c.pW[0:1, 512:1024], 1.0 / D, c.r1[:], ALU.mult, ALU.subtract), inc="vec")
        v_rows = p.count("vec")
        c.free["pw"] = ("vec", v_rows)
        p.wait("scalar", "vec", v_rows)
        p.emit("scalar", lambda e: e.activation(c.r1[:], c.r2[:], AF.Ln, bias=c.epsr_s[:]))
        p.emit("scalar", lambda e: e.activation(c.r2[:], c.r1[:], AF.Exp, scale=-0.5), inc="act")
        a_rows = p.count("act")
        p.wait("tensor", "act", a_rows)
        c.pe_wait("pp")
        p.emit("tensor", lambda e: e.matmul(
            c.pP[:, 0:512], c.onesr_s[:], c.r0[:], start=True, stop=True))
        p.emit("tensor", lambda e: e.matmul(
            c.pP[:, 512:1024], c.onesr_s[:], c.r2[:], start=True, stop=True), inc="mm")
        mm_bc = p.count("mm")
        p.wait("vector", "mm", mm_bc)
        v_x3 = None
        for k in range(NKD):
            p.emit("vector", lambda e, k=k: e.tensor_tensor(
                c.t1[:], c.s1[:, k, :], c.pP[:, 0:512], ALU.subtract))
            v_x3 = p.emit("vector", lambda e, k=k: e.tensor_tensor(
                c.s1[:, k, :], c.t1[:], c.pP[:, 512:1024], ALU.mult), inc="vec")
        c.free["pp"] = ("vec", v_x3)
        p.wait("sync", "vec", v_x3)
        p.dma("sync", c.x3T[b].rearrange("(k p) t -> p k t", p=128)[:, :, gsl],
              c.s1[:], inc="st")
        c.s1_store = p.count("st")


    def emit_chunk(c, b, g, ch, v_cum, v_qkn):
        p = c.p
        chsl = slice(ch * CH, (ch + 1) * CH)
        par = (g * NCH + ch) % 2
        Sr = c.SS[:, par, :, :]
        Sw = c.SS[:, 1 - par, :, :]

        c.pe_wait("py")
        p.wait("tensor", "vec", v_cum)
        p.emit("tensor", lambda e: e.matmul(
            c.pY[0:64, 0:44], c.gkb[:, chsl], c.identf_s[0:44, 0:44],
            start=True, stop=True, tile_position=(0, 0)))
        mm = p.emit("tensor", lambda e: e.matmul(
            c.pY[64:128, 0:44], c.gkb[:, chsl], c.identf_s[0:44, 0:44],
            start=True, stop=True, tile_position=(0, 64)), inc="mm")
        p.wait("vector", "mm", mm)
        p.wait("vector", "vec", getattr(c, "chunk_shared_free", 0))
        p.wait("vector", "act", getattr(c, "chunk_shared_act", 0))
        for half, r0 in ((0, 0), (1, 64)):
            pv = c.pY[r0:r0 + 64, 0:12].rearrange("p (a two) -> p a two", two=2)
            pv2 = c.pY[r0:r0 + 64, 32:44].rearrange("p (a two) -> p a two", two=2)
            p.emit("vector", lambda e, pv=pv, r0=r0, half=half: e.tensor_copy(
                c.stackP[r0:r0 + 64, 0:6].unsqueeze(2), pv[:, :, half:half + 1]))
            p.emit("vector", lambda e, pv2=pv2, r0=r0, half=half: e.tensor_copy(
                c.stackP[r0:r0 + 64, 6:12].unsqueeze(2), pv2[:, :, half:half + 1]))
        p.emit("vector", lambda e, ch=ch: e.tensor_tensor(
            c.wtmp[:], c.lgamS[:, :, ch], c.stackP[:, 0:6], ALU.subtract), inc="vec")
        v_stk = p.count("vec")
        c.free["py"] = ("vec", v_stk)
        p.wait("scalar", "vec", v_stk)
        p.wait("scalar", "mm", getattr(c, "pair_mm_last", 0))
        p.emit("scalar", lambda e: e.activation(c.bPS[:], c.stackP[:, 0:6], AF.Exp))
        p.emit("scalar", lambda e: e.activation(c.wS[:], c.wtmp[:], AF.Exp))
        p.emit("scalar", lambda e, ch=ch: e.activation(
            c.gS[:], c.lgamS[:, :, ch], AF.Exp), inc="act")
        a_shared = p.count("act")

        if not hasattr(c, "par_end_mm"):
            c.par_end_mm = {0: 0, 1: 0}
            c.par_end_act = {0: 0, 1: 0}
            c.par_t5_read = {0: 0, 1: 0}
            c.par_t7_read = {0: 0, 1: 0}
        for j in range(6):
            jp = j % 2
            emsk = c.emsk[:, jp, :]
            emt = c.emt[:, jp, :]
            amat = c.amat[:, jp, :]
            apow = c.apow[:, jp, :, :]
            apw = c.apw[:, jp, :, :]
            ubuf = c.ubuf[:, jp, :, :]
            vtm = c.vtm[:, jp, :]
            kts = c.kts[:, jp, :]
            rbt = c.rbt[:, jp, :]
            t5 = c.t5[:, jp, :]
            t7 = c.t7[:, jp, :]
            ojunk = c.ojunk[:, jp, :]
            sscol = c.sscol[:, jp:jp + 1]
            rstdo = c.rstdo[:, jp:jp + 1]
            pmm = c.par_end_mm[jp]
            pact = c.par_end_act[jp]
            p.wait("vector", "mm", pmm)
            p.wait("vector", "act", pact)
            p.wait("scalar", "mm", pmm)
            c.pe_wait("pw2")
            p.wait("tensor", "vec", v_cum)
            p.emit("tensor", lambda e, j=j, chsl=chsl: e.matmul(
                c.pW[:, 0:64], c.selone_s[:, 2 * j, :], c.gkb[0:12, chsl],
                start=True, stop=True))
            mm = p.emit("tensor", lambda e, j=j, chsl=chsl: e.matmul(
                c.pW[:, 64:128], c.selone_s[:, 2 * j + 1, :], c.gkb[0:12, chsl],
                start=True, stop=True), inc="mm")
            p.wait("vector", "mm", mm)
            vv = p.emit("vector", lambda e, j=j, rbt=rbt: e.tensor_scalar(
                rbt, c.pW[:, 0:128], c.stackP[:, j:j + 1], None, ALU.subtract),
                inc="vec")
            c.free["pw2"] = ("vec", vv)
            p.wait("scalar", "vec", vv)
            p.emit("scalar", lambda e, t5=t5, rbt=rbt: e.activation(t5, rbt, AF.Relu))
            a_e = p.emit("scalar", lambda e, emsk=emsk, t5=t5: e.activation(
                emsk, t5, AF.Exp, scale=-1.0), inc="act")
            c.pe_wait("pw3")
            p.wait("tensor", "vec", v_qkn)
            p.emit("tensor", lambda e, j=j, chsl=chsl: e.matmul(
                c.pW[0:64, 512:576], c.kn[0:64, j, chsl], c.kn[0:64, j, chsl],
                start=True, stop=True, tile_position=(0, 0)))
            mm = p.emit("tensor", lambda e, j=j, chsl=chsl: e.matmul(
                c.pW[64:128, 576:640], c.kn[64:128, j, chsl], c.kn[64:128, j, chsl],
                start=True, stop=True, tile_position=(64, 64)), inc="mm")
            p.wait("vector", "mm", mm)
            p.wait("vector", "act", a_e)
            p.emit("vector", lambda e, ojunk=ojunk, emsk=emsk: e.tensor_tensor(
                ojunk, emsk, c.pW[:, 512:640], ALU.mult))
            vv = p.emit("vector", lambda e, j=j, amat=amat, ojunk=ojunk:
                        e.scalar_tensor_tensor(
                amat, ojunk, c.stackP[:, 6 + j:7 + j], c.ptril_s[:],
                ALU.mult, ALU.mult), inc="vec")
            c.free["pw3"] = ("vec", vv)
            v_amat = vv
            c.pe_wait("py")
            p.wait("tensor", "vec", v_amat)
            mm = p.emit("tensor", lambda e, amat=amat: e.transpose(
                c.pYh[:, 0:128], amat, c.ident_h[:]), inc="mm")
            p.wait("vector", "mm", mm)
            vv = p.emit("vector", lambda e, apow=apow: e.tensor_copy(
                apow[:, 0, :], c.pYh[:, 0:128]), inc="vec")
            c.free["py"] = ("vec", vv)
            v_pow = [vv]
            c.pe_wait("pz")
            p.emit("tensor", lambda e, emsk=emsk: e.transpose(
                c.pZh[:, 0:128], emsk, c.ident_h[:]), inc="mm")
            mm_emt = p.count("mm")
            p.wait("vector", "mm", mm_emt)
            v_emt = p.emit("vector", lambda e, emt=emt: e.tensor_tensor(
                emt, c.pZh[:, 0:128], c.ptriu_s[:], ALU.mult), inc="vec")
            c.free["pz"] = ("vec", v_emt)
            cur = amat
            v_cur = v_amat
            for lv in range(1, 6):
                bank = [c.pA, c.pB][lv % 2]
                bname = ["pa", "pb"][lv % 2]
                c.pe_wait(bname)
                p.wait("tensor", "vec", max(v_pow[lv - 1], v_cur))
                mm = p.emit("tensor", lambda e, lv=lv, bank=bank, cur=cur, apow=apow:
                            e.matmul(
                    bank[:, 0:128], apow[:, lv - 1, :], cur,
                    start=True, stop=True), inc="mm")
                p.wait("vector", "mm", mm)
                v_cur = p.emit("vector", lambda e, lv=lv, bank=bank, apw=apw:
                               e.tensor_copy(
                    apw[:, lv % 2, :], bank[:, 0:128]), inc="vec")
                c.free[bname] = ("vec", v_cur)
                v_pow_prev_done = v_cur
                c.pe_wait("py")
                mm = p.emit("tensor", lambda e, lv=lv, cur=cur, apow=apow: e.matmul(
                    c.pY[:, 0:128], cur, apow[:, lv - 1, :],
                    start=True, stop=True), inc="mm")
                p.wait("vector", "mm", mm)
                vv = p.emit("vector", lambda e, lv=lv, apow=apow: e.tensor_copy(
                    apow[:, lv, :], c.pY[:, 0:128]), inc="vec")
                c.free["py"] = ("vec", vv)
                v_pow.append(vv)
                cur = apw[:, lv % 2, :]
            c.pe_wait("pz")
            p.emit("tensor", lambda e, j=j, chsl=chsl: e.matmul(
                c.pZ[0:64, 0:128], c.qkv_s[:, 12 + 2 * j, chsl], c.ident_h[:],
                start=True, stop=True, tile_position=(0, 0)))
            mm = p.emit("tensor", lambda e, j=j, chsl=chsl: e.matmul(
                c.pZ[64:128, 0:128], c.qkv_s[:, 12 + 2 * j + 1, chsl], c.ident_h[:],
                start=True, stop=True, tile_position=(0, 64)), inc="mm")
            p.wait("vector", "mm", mm)
            v_vtm = p.emit("vector", lambda e, vtm=vtm: e.tensor_copy(
                vtm, c.pZ[:, 0:128]), inc="vec")
            c.free["pz"] = ("vec", v_vtm)
            c.pe_wait("py")
            p.emit("tensor", lambda e, j=j, chsl=chsl: e.matmul(
                c.pY[0:64, 0:64], c.kn[0:64, j, chsl], c.ident_h[0:64, 0:64],
                start=True, stop=True, tile_position=(0, 0)))
            mm = p.emit("tensor", lambda e, j=j, chsl=chsl: e.matmul(
                c.pY[64:128, 64:128], c.kn[64:128, j, chsl], c.ident_h[64:128, 64:128],
                start=True, stop=True, tile_position=(64, 64)), inc="mm")
            p.wait("vector", "mm", mm)
            p.wait("vector", "act", a_shared)
            p.emit("vector", lambda e, j=j, kts=kts: e.tensor_scalar(
                kts[0:64, 0:64], c.pY[0:64, 0:64], c.wS[0:64, j:j + 1], None, ALU.mult))
            v_kts = p.emit("vector", lambda e, j=j, kts=kts: e.tensor_scalar(
                kts[64:128, 64:128], c.pY[64:128, 64:128], c.wS[64:128, j:j + 1], None,
                ALU.mult), inc="vec")
            c.free["py"] = ("vec", v_kts)
            c.pe_wait("pp4")
            p.emit("tensor", lambda e, j=j, chsl=chsl, Sr=Sr: e.matmul(
                c.pP[0:64, 0:128], c.kn[0:64, j, chsl], Sr[0:64, j, :],
                start=True, stop=True, tile_position=(0, 0)))
            mm = p.emit("tensor", lambda e, j=j, chsl=chsl, Sr=Sr: e.matmul(
                c.pP[64:128, 0:128], c.kn[64:128, j, chsl], Sr[64:128, j, :],
                start=True, stop=True, tile_position=(64, 64)), inc="mm")
            p.wait("vector", "mm", mm)
            p.emit("vector", lambda e, j=j, rbt=rbt: e.tensor_scalar(
                rbt, c.pP[:, 0:128], c.bPS[:, j:j + 1], None, ALU.mult))
            p.emit("vector", lambda e, t5=t5, vtm=vtm, rbt=rbt: e.tensor_tensor(
                t5, vtm, rbt, ALU.subtract))
            vv = p.emit("vector", lambda e, j=j, ubuf=ubuf, t5=t5: e.tensor_scalar(
                ubuf[:, 0, :], t5, c.stackP[:, 6 + j:7 + j], None, ALU.mult),
                inc="vec")
            c.free["pp4"] = ("vec", vv)
            v_u = vv
            ucur = 0
            for idx, lv in enumerate([5, 4, 3, 2, 1, 0]):
                bank = [c.pA, c.pB][idx % 2]
                bname = ["pa", "pb"][idx % 2]
                c.pe_wait(bname)
                p.wait("tensor", "vec", max(v_pow[lv], v_u))
                mm = p.emit("tensor", lambda e, lv=lv, bank=bank, ucur=ucur, apow=apow,
                            ubuf=ubuf: e.matmul(
                    bank[:, 0:128], apow[:, lv, :], ubuf[:, ucur, :],
                    start=True, stop=True), inc="mm")
                p.wait("vector", "mm", mm)
                op = ALU.add if lv > 0 else ALU.subtract
                v_u = p.emit("vector", lambda e, bank=bank, ucur=ucur, op=op, ubuf=ubuf:
                             e.tensor_tensor(
                    ubuf[:, 1 - ucur, :], ubuf[:, ucur, :], bank[:, 0:128], op),
                    inc="vec")
                c.free[bname] = ("vec", v_u)
                ucur = 1 - ucur
            c.pe_wait("pw3")
            p.emit("tensor", lambda e, j=j, chsl=chsl: e.matmul(
                c.pW[0:64, 512:576], c.kn[0:64, j, chsl], c.qn[0:64, j, chsl],
                start=True, stop=True, tile_position=(0, 0)))
            mm = p.emit("tensor", lambda e, j=j, chsl=chsl: e.matmul(
                c.pW[64:128, 576:640], c.kn[64:128, j, chsl], c.qn[64:128, j, chsl],
                start=True, stop=True, tile_position=(64, 64)), inc="mm")
            p.wait("vector", "mm", mm)
            p.wait("vector", "vec", v_emt)
            v_mq = p.emit("vector", lambda e, ojunk=ojunk, emt=emt: e.tensor_tensor(
                ojunk, emt, c.pW[:, 512:640], ALU.mult), inc="vec")
            c.free["pw3"] = ("vec", v_mq)
            c.pe_wait("pp4")
            p.wait("tensor", "vec", max(v_u, v_mq))
            p.emit("tensor", lambda e, ucur=ucur, ojunk=ojunk, ubuf=ubuf: e.matmul(
                c.pP[:, 256:384], ojunk, ubuf[:, ucur, :], start=True, stop=True))
            p.emit("tensor", lambda e, j=j, chsl=chsl, Sr=Sr: e.matmul(
                c.pP[0:64, 128:256], c.qn[0:64, j, chsl], Sr[0:64, j, :],
                start=True, stop=True, tile_position=(0, 0)))
            mm = p.emit("tensor", lambda e, j=j, chsl=chsl, Sr=Sr: e.matmul(
                c.pP[64:128, 128:256], c.qn[64:128, j, chsl], Sr[64:128, j, :],
                start=True, stop=True, tile_position=(64, 64)), inc="mm")
            p.wait("vector", "mm", mm)
            p.emit("vector", lambda e, j=j, rbt=rbt: e.tensor_scalar(
                rbt, c.pP[:, 128:256], c.bPS[:, j:j + 1], None, ALU.mult))
            p.emit("vector", lambda e, t7=t7, rbt=rbt: e.tensor_tensor(
                t7, c.pP[:, 256:384], rbt, ALU.add))
            vv = p.emit("vector", lambda e, ojunk=ojunk, t7=t7, sscol=sscol:
                        e.scalar_tensor_tensor(
                ojunk, t7, 1.0, t7, ALU.mult, ALU.mult,
                accum_out=sscol), inc="vec")
            c.free["pp4"] = ("vec", vv)
            p.wait("scalar", "vec", vv)
            p.emit("scalar", lambda e, rstdo=rstdo, sscol=sscol: e.activation(
                rstdo, sscol, AF.Ln, scale=1.0 / DV, bias=c.eps5c_s[:]))
            a_r = p.emit("scalar", lambda e, rstdo=rstdo, sscol=sscol: e.activation(
                sscol, rstdo, AF.Exp, scale=-0.5), inc="act")
            p.wait("vector", "act", a_r)
            v_ofin = p.emit("vector", lambda e, emsk=emsk, t7=t7, sscol=sscol:
                            e.tensor_scalar(
                emsk, t7, sscol, None, ALU.mult), inc="vec")
            c.pe_wait("pz")
            p.wait("tensor", "vec", v_ofin)
            mm = p.emit("tensor", lambda e, emsk=emsk: e.transpose(
                c.pZh[:, 0:128], emsk, c.ident_h[:]), inc="mm")
            p.wait("vector", "mm", mm)
            vv = p.emit("vector", lambda e, j=j, chsl=chsl: e.tensor_tensor(
                c.o_fm[:, 2 * j:2 * j + 2, chsl],
                c.pZh[:, 0:128].rearrange("p (a t) -> p a t", a=2),
                c.ggf[:, 2 * j:2 * j + 2, chsl], ALU.mult), inc="vec")
            c.free["pz"] = ("vec", vv)
            c.pe_wait("pp5")
            p.wait("tensor", "vec", max(v_kts, v_u))
            p.emit("tensor", lambda e, ucur=ucur, kts=kts, ubuf=ubuf: e.matmul(
                c.pP[0:64, 512:640], kts[:, 0:64], ubuf[:, ucur, :],
                start=True, stop=True, tile_position=(0, 0)))
            mm = p.emit("tensor", lambda e, ucur=ucur, kts=kts, ubuf=ubuf: e.matmul(
                c.pP[64:128, 512:640], kts[:, 64:128], ubuf[:, ucur, :],
                start=True, stop=True, tile_position=(0, 64)), inc="mm")
            p.wait("vector", "mm", mm)
            p.emit("vector", lambda e, j=j, Sr=Sr, Sw=Sw: e.scalar_tensor_tensor(
                Sw[0:64, j, :], Sr[0:64, j, :], c.gS[0:64, j:j + 1],
                c.pP[0:64, 512:640], ALU.mult, ALU.add))
            vv = p.emit("vector", lambda e, j=j, Sr=Sr, Sw=Sw: e.scalar_tensor_tensor(
                Sw[64:128, j, :], Sr[64:128, j, :], c.gS[64:128, j:j + 1],
                c.pP[64:128, 512:640], ALU.mult, ALU.add), inc="vec")
            c.free["pp5"] = ("vec", vv)
            c.par_end_mm[jp] = p.count("mm")
            c.par_end_act[jp] = p.count("act")
            c.pair_mm_last = p.count("mm")
        c.chunk_shared_free = p.count("vec")
        c.chunk_shared_act = p.count("act")
    return (build, CFG)


def _mod_l1f():

    LSEQ = 2048
    D = 1024
    DH = 4096
    TG = 512
    NKD = D // 128
    NHT = DH // 128
    NSEQ = 2
    DSL = 256
    NSL = DH // DH
    EPS = 1e-5
    CFG = {"pfx": "", "xin": None, "xout": None}


    class Ctx:
        pass


    def host_fold_inputs(up_w, down_w):
        import ml_dtypes

        def stage_mat(w, s, transpose2x2):
            S = np.zeros((32, 128, 128), np.float32)
            stride = 2 ** s
            for i in range(DH):
                k = (i // (2 * stride)) * stride + (i % stride)
                bit = (i // stride) % 2
                dv = w[s, k, bit, bit]
                ov = w[s, k, bit, 1 - bit] if not transpose2x2 else w[s, k, 1 - bit, bit]
                bb, r = divmod(i, 128)
                rj = (i ^ stride) % 128
                S[bb, r, r] = dv
                S[bb, r, rj] = ov
            return S

        def stage_cols(w, transpose2x2):
            dcol = np.zeros((5, 32, 128), np.float32)
            ocol = np.zeros((5, 32, 128), np.float32)
            for si, s in enumerate(range(7, 12)):
                stride = 2 ** s
                for i in range(DH):
                    k = (i // (2 * stride)) * stride + (i % stride)
                    bit = (i // stride) % 2
                    dv = w[s, k, bit, bit]
                    ov = w[s, k, bit, 1 - bit] if not transpose2x2 else w[s, k, 1 - bit, bit]
                    bb, r = divmod(i, 128)
                    dcol[si, bb, r] = dv
                    ocol[si, bb, r] = ov
            return dcol, ocol

        o = {}
        upS = np.zeros((7, 32, 128, 128), np.float32)
        upS[0] = stage_mat(up_w, 6, False).transpose(0, 2, 1)
        for slot, s in enumerate([5, 4, 3, 2, 1, 0], start=1):
            upS[slot] = stage_mat(up_w, s, False)
        dnS = np.zeros((7, 32, 128, 128), np.float32)
        dnS[0] = stage_mat(down_w, 0, True)
        for slot, s in enumerate([1, 2, 3, 4, 5, 6], start=1):
            dnS[slot] = stage_mat(down_w, s, True).transpose(0, 2, 1)
        o["up_S"] = np.ascontiguousarray(upS.transpose(1, 2, 0, 3)).astype(ml_dtypes.bfloat16)
        o["dn_S"] = np.ascontiguousarray(dnS.transpose(1, 2, 0, 3)).astype(ml_dtypes.bfloat16)
        ud, uo = stage_cols(up_w, False)
        dd, do_ = stage_cols(down_w, True)
        o["up_dcol"] = np.ascontiguousarray(ud.transpose(2, 0, 1))
        o["up_ocol"] = np.ascontiguousarray(uo.transpose(2, 0, 1))
        o["dn_dcol"] = np.ascontiguousarray(dd.transpose(2, 0, 1))
        o["dn_ocol"] = np.ascontiguousarray(do_.transpose(2, 0, 1))
        return o


    def build(nc, dbg=False, ext=None):
        stack = contextlib.ExitStack()
        E = stack.enter_context
        c = Ctx()
        c.nc = nc

        def dram(name, shape, dt, kind="ExternalInput"):
            return nc.dram_tensor(CFG["pfx"] + name, shape, dt, kind=kind).ap()

        c.xT = CFG["xin"] if CFG["xin"] is not None else dram("xT", [NSEQ, D, LSEQ], F32)
        c.wexpT = dram("wexpT", [DH, D], F32)
        c.wcon = dram("wcon", [DH, D], F32)
        c.up_S = dram("up_S", [32, 128, 7, 128], BF16)
        c.dn_S = dram("dn_S", [32, 128, 7, 128], BF16)
        c.up_dcol = dram("up_dcol", [128, 5, 32], F32)
        c.up_ocol = dram("up_ocol", [128, 5, 32], F32)
        c.dn_dcol = dram("dn_dcol", [128, 5, 32], F32)
        c.dn_ocol = dram("dn_ocol", [128, 5, 32], F32)
        c.bexp32 = dram("bexp32", [128, 32], F32)
        c.upb32 = dram("upb32", [128, 32], F32)
        c.bcon8 = dram("bcon8", [128, 8], F32)
        c.dwnb8 = dram("dwnb8", [128, 8], F32)
        c.c_ones_b = dram("c_ones_b", [128, 1], BF16)
        c.c_ident = dram("c_ident", [128, 128], BF16)
        c.c_ones_row = dram("c_ones_row", [1, 128], F32)
        c.c_eps_row = dram("c_eps_row", [1, 1], F32)
        c.x4T = CFG["xout"] if CFG["xout"] is not None else dram("x4T", [NSEQ, D, LSEQ], F32, kind="ExternalOutput")
        c.weff_scr = dram("weff_scr", [NKD, 128, DH], BF16, kind="Internal")
        c.vcon_scr = dram("vcon_scr", [NHT, 128, D], BF16, kind="Internal")

        def S(name, shape, dt):
            t = E(nc.sbuf_tensor(CFG["pfx"] + name, shape, dt))
            setattr(c, name, t)
            return t

        S("x_fm", [128, NKD, TG], F32)
        S("xb", [128, NKD, TG], BF16)
        S("sq", [128, NKD, TG], BF16)
        S("hbuf", [128, NHT, TG], BF16)
        S("ybf", [128, NKD, TG], BF16)
        S("s1", [128, NKD, TG], F32)
        S("t1", [128, TG], F32)
        S("r0", [1, TG], F32)
        S("r1", [1, TG], F32)
        S("r2", [1, TG], F32)
        c.wbuf = [S("wbuf0", [128, NKD, 256], BF16), S("wbuf1", [128, NKD, 256], BF16)]
        c.vbuf = [S("vbuf0", [128, NHT, 128], BF16), S("vbuf1", [128, NHT, 128], BF16)]
        S("beff", [128, 32], F32)
        S("btmp", [128, 32], F32)
        S("bt2", [128, 32], F32)
        S("bcola", [128, 32], F32)
        S("bcolx", [128, 32], BF16)
        S("bcon_s", [128, 8], F32)
        S("dwnb_s", [128, 8], F32)
        S("onesb_s", [128, 1], BF16)
        S("ident_s", [128, 128], BF16)
        S("upb_s", [128, 32], F32)
        S("onesr_s", [1, 128], F32)
        S("epsr_s", [1, 1], F32)
        S("smat", [128, 2, 7, 128], BF16)
        S("mchain", [128, 32, 128], BF16)
        S("fb0", [128, NHT, DSL], BF16)
        S("fb1", [128, NHT, DSL], BF16)
        S("ftmp", [128, DSL], BF16)
        S("fcol_d", [128, 5, 32], F32)
        S("fcol_o", [128, 5, 32], F32)
        S("stg", [128, DH], BF16)

        c.pA = E(nc.psum_tensor(CFG["pfx"] + "pA", [128, 512], F32))
        c.pB = E(nc.psum_tensor(CFG["pfx"] + "pB", [128, 512], F32))
        c.pW = E(nc.psum_tensor(CFG["pfx"] + "pW", [128, 1024], F32))
        c.pP = E(nc.psum_tensor(CFG["pfx"] + "pP", [128, 1024], F32))
        c.pWb = c.pW[:].bitcast(BF16)

        p = ext if ext is not None else Prog(nc, stack)
        c.p = p
        c.free = {}

        def pe_wait(name):
            if name in c.free:
                ctr, v = c.free[name]
                p.wait("tensor", ctr, v)

        c.pe_wait = pe_wait

        for dst, src in [(c.bcon_s, c.bcon8), (c.dwnb_s, c.dwnb8), (c.onesb_s, c.c_ones_b),
                         (c.onesr_s, c.c_ones_row), (c.epsr_s, c.c_eps_row),
                         (c.bcola, c.bexp32), (c.upb_s, c.upb32),
                         (c.ident_s, c.c_ident)]:
            p.dma("sync", dst[:], src, inc="ld2")
        c.n_ld0 = p.count("ld2")
        c.smat_free_mm = {0: 0, 1: 0}
        c.mchain_free_mm = 0
        c.weff_stored = 0
        c.vcon_stored = 0

        fold_side(c, up=True)
        fold_side(c, up=False)

        c.xfm_free = 0
        c.xb_free_mm = 0
        c.wb_last = {0: 0, 1: 0}
        c.vb_last = {0: 0, 1: 0}
        for b in range(NSEQ):
            for g in range(LSEQ // TG):
                emit_group(c, b, g)
        if ext is not None:
            return stack
        p.wait("gpsimd", "st", p.count("st"))
        block = E(nc.Block())
        p.finalize(block)
        stack.close()
        return nc


    def fold_side(c, up):
        p = c.p
        Smat_d = c.up_S if up else c.dn_S
        wsrc = c.wexpT if up else c.wcon

        p.dma("sync", c.fcol_d[:], c.up_dcol if up else c.dn_dcol, inc="ld2")
        ldc = p.dma("sync", c.fcol_o[:], c.up_ocol if up else c.dn_ocol, inc="ld2")

        sm_ld = {}
        for bb in range(NHT):
            p.wait("sync", "mm", c.smat_free_mm[bb % 2])
            sm_ld[bb] = p.dma("sync", c.smat[:, bb % 2, :, :], Smat_d[bb], inc="ld2")
            p.wait("vector", "ld2", sm_ld[bb])
            p.wait("vector", "mm", c.mchain_free_mm)
            p.emit("vector", lambda e, bb=bb: e.tensor_copy(
                c.mchain[:, bb, :], c.smat[:, bb % 2, 0, :]), inc="vec")
            mm = 0
            for s in range(1, 7):
                bank = [c.pA, c.pB][s % 2]
                bname = ["pa", "pb"][s % 2]
                p.wait("tensor", "vec", p.count("vec"))
                c.pe_wait(bname)
                mm = p.emit("tensor", lambda e, bb=bb, s=s, bank=bank: e.matmul(
                    bank[:, 0:128], c.smat[:, bb % 2, s, :], c.mchain[:, bb, :],
                    start=True, stop=True), inc="mm")
                p.wait("vector", "mm", mm)
                vv = p.emit("vector", lambda e, bb=bb, bank=bank: e.tensor_copy(
                    c.mchain[:, bb, :], bank[:, 0:128]), inc="vec")
                c.free[bname] = ("vec", vv)
            c.smat_free_mm[bb % 2] = mm
        v_chain = p.count("vec")

        if up:
            p.wait("vector", "ld2", c.n_ld0)
            p.emit("vector", lambda e: e.tensor_copy(c.bcolx[:], c.bcola[:]), inc="vec")
            v_bx = p.count("vec")
            p.wait("tensor", "vec", v_bx)
            c.pe_wait("pw")
            mm = 0
            for bb in range(NHT):
                mm = p.emit("tensor", lambda e, bb=bb: e.matmul(
                    c.pW[:, bb:bb + 1], c.mchain[:, bb, :], c.bcolx[:, bb:bb + 1],
                    start=True, stop=True), inc="mm")
            p.wait("vector", "mm", mm)
            p.wait("vector", "ld2", ldc)
            p.emit("vector", lambda e: e.tensor_copy(c.bcola[:], c.pW[:, 0:32]), inc="vec")
            c.free["pw"] = ("vec", p.count("vec"))
            bufs = [c.bcola, c.bt2]
            for si in range(5):
                sig = 2 ** (si + 7) // 128
                srcb, dstb = bufs[si % 2], bufs[1 - si % 2]
                for t in range(NHT):
                    tp = t ^ sig
                    p.emit("vector", lambda e, srcb=srcb, tp=tp, si=si, t=t: e.tensor_scalar(
                        c.btmp[:, t:t + 1], srcb[:, tp:tp + 1], c.fcol_o[:, si, t:t + 1],
                        None, ALU.mult))
                    p.emit("vector", lambda e, srcb=srcb, dstb=dstb, si=si, t=t:
                           e.scalar_tensor_tensor(
                        dstb[:, t:t + 1], srcb[:, t:t + 1], c.fcol_d[:, si, t:t + 1],
                        c.btmp[:, t:t + 1], ALU.mult, ALU.add))
            p.emit("vector", lambda e: e.tensor_tensor(
                c.beff[:], c.bt2[:], c.upb_s[:], ALU.add), inc="vec")
            c.v_beff = p.count("vec")

        nslices = D // DSL
        for sl in range(nslices):
            dsl = slice(sl * DSL, (sl + 1) * DSL)
            p.wait("gpsimd", "vec", getattr(c, "fb_free_vec", 0))
            p.wait("gpsimd", "mm", getattr(c, "fb_free_mm", 0))
            ldw = p.dma("gpsimd", c.fb0[:],
                        wsrc.rearrange("(t p) d -> p t d", p=128)[:, :, dsl], inc="ldp")
            p.wait("tensor", "ldp", ldw)
            p.wait("tensor", "vec", v_chain)
            for bb in range(NHT):
                bank = [c.pA, c.pB][bb % 2]
                bname = ["pa", "pb"][bb % 2]
                c.pe_wait(bname)
                mm = p.emit("tensor", lambda e, bb=bb, bank=bank: e.matmul(
                    bank[:, 0:DSL], c.mchain[:, bb, :], c.fb0[:, bb, :],
                    start=True, stop=True), inc="mm")
                if up:
                    c.mchain_free_mm = mm
                p.wait("vector", "mm", mm)
                vv = p.emit("vector", lambda e, bb=bb, bank=bank: e.tensor_copy(
                    c.fb1[:, bb, :], bank[:, 0:DSL]), inc="vec")
                c.free[bname] = ("vec", vv)
            bufs = [c.fb1, c.fb0]
            for si in range(5):
                sig = 2 ** (si + 7) // 128
                srcb, dstb = bufs[si % 2], bufs[1 - si % 2]
                for t in range(NHT):
                    tp = t ^ sig
                    p.emit("vector", lambda e, srcb=srcb, tp=tp, si=si, t=t: e.tensor_scalar(
                        c.ftmp[:, 0:DSL], srcb[:, tp, :], c.fcol_o[:, si, t:t + 1], None,
                        ALU.mult))
                    p.emit("vector", lambda e, srcb=srcb, dstb=dstb, si=si, t=t:
                           e.scalar_tensor_tensor(
                        dstb[:, t, :], srcb[:, t, :], c.fcol_d[:, si, t:t + 1],
                        c.ftmp[:, 0:DSL], ALU.mult, ALU.add))
            p.emit("vector", lambda e: e.engine_nop(), inc="vec")
            v_folded = p.count("vec")
            c.fb_free_vec = v_folded
            if up:
                for dk in range(DSL // 128):
                    dg = sl * (DSL // 128) + dk
                    for grp in range(4):
                        c.pe_wait("pw")
                        p.wait("tensor", "vec", v_folded)
                        mm = None
                        for i in range(8):
                            ht = grp * 8 + i
                            half = i // 4
                            mm = p.emit("tensor", lambda e, ht=ht, i=i, dk=dk: e.transpose(
                                c.pWb[:, i * 128:(i + 1) * 128],
                                c.fb0[:, ht, dk * 128:(dk + 1) * 128], c.ident_s[:]),
                                inc="mm")
                        c.fb_free_mm = mm
                        p.wait("vector", "mm", mm)
                        vv = p.emit("vector", lambda e, grp=grp: e.tensor_copy(
                            c.stg[:, grp * 1024:(grp + 1) * 1024],
                            c.pWb[:, 0:1024]), inc="vec")
                        c.free["pw"] = ("vec", vv)
                    v_stg = p.count("vec")
                    p.wait("sync", "vec", v_stg)
                    p.dma("sync", c.weff_scr[dg], c.stg[:], inc="st")
                c.weff_stored = p.count("st")
            else:
                p.wait("sync", "vec", v_folded)
                p.dma("sync",
                      c.vcon_scr[:, :, dsl].rearrange("t p d -> p t d"),
                      c.fb0[:], inc="st")
                c.vcon_stored = p.count("st")


    def emit_group(c, b, g):
        p = c.p
        gsl = slice(g * TG, (g + 1) * TG)
        p.wait("sync", "vec", c.xfm_free)
        ld = p.dma("sync", c.x_fm[:],
                   c.xT[b].rearrange("(k p) t -> p k t", p=128)[:, :, gsl], inc="ld2")
        p.wait("vector", "ld2", ld)
        p.wait("vector", "mm", c.xb_free_mm)
        p.emit("vector", lambda e: e.tensor_copy(c.xb[:], c.x_fm[:]), inc="vec")
        v_xb = p.count("vec")

        first = (b == 0 and g == 0)
        pend = {}
        for mp in range(16):
            p.wait("gpsimd", "mm", c.wb_last[mp % 2])
            if first and mp == 0:
                p.wait("gpsimd", "st", c.weff_stored)
            fs = mp * 256
            pend[mp] = p.dma("gpsimd", c.wbuf[mp % 2][:],
                             c.weff_scr[:, :, fs:fs + 256].rearrange("k p f -> p k f"),
                             inc="wb%d" % (mp % 2))
            for m in (2 * mp, 2 * mp + 1):
                off = (m % 2) * 128
                bank = [c.pA, c.pB][m % 2]
                bname = ["pa", "pb"][m % 2]
                p.wait("tensor", "wb%d" % (mp % 2), pend[mp])
                p.wait("tensor", "vec", v_xb)
                c.pe_wait(bname)
                mm = None
                for k in range(NKD):
                    mm = p.emit("tensor", lambda e, bank=bank, mp=mp, k=k, off=off: e.matmul(
                        bank[:], c.wbuf[mp % 2][:, k, off:off + 128], c.xb[:, k, :],
                        start=(k == 0), stop=(k == NKD - 1)),
                        inc=("mm" if k == NKD - 1 else None))
                c.wb_last[mp % 2] = mm
                p.wait("scalar", "mm", mm)
                p.wait("scalar", "vec", getattr(c, "v_beff", 0))
                vv = p.emit("scalar", lambda e, bank=bank, m=m: e.activation(
                    c.hbuf[:, m, :], bank[:], AF.Gelu, bias=c.beff[:, m:m + 1]), inc="act")
                c.free[bname] = ("act", vv)
        a_h = p.count("act")

        vp = {}

        def load_v(m):
            p.wait("gpsimd", "mm", c.vb_last[m % 2])
            if first and m == 0:
                p.wait("gpsimd", "st", c.vcon_stored)
            vp[m] = p.dma("gpsimd", c.vbuf[m % 2][:],
                          c.vcon_scr[:, :, m * 128:(m + 1) * 128].rearrange("k p f -> p k f"),
                          inc="wo%d" % (m % 2))

        load_v(0)
        load_v(1)
        for m in range(NKD):
            bank = [c.pA, c.pB][m % 2]
            bname = ["pa", "pb"][m % 2]
            p.wait("tensor", "wo%d" % (m % 2), vp[m])
            p.wait("tensor", "act", a_h)
            c.pe_wait(bname)
            mm = None
            for k in range(NHT):
                mm = p.emit("tensor", lambda e, bank=bank, m=m, k=k: e.matmul(
                    bank[:], c.vbuf[m % 2][:, k, :], c.hbuf[:, k, :],
                    start=(k == 0), stop=(k == NHT - 1)), inc=("mm" if k == NHT - 1 else None))
            c.vb_last[m % 2] = mm
            if m + 2 < NKD:
                load_v(m + 2)
            p.wait("vector", "mm", mm)
            vv = p.emit("vector", lambda e, bank=bank, m=m: e.tensor_scalar(
                c.ybf[:, m, :], bank[:], c.bcon_s[:, m:m + 1], c.dwnb_s[:, m:m + 1],
                ALU.add, ALU.add), inc="vec")
            c.free[bname] = ("vec", vv)
        for m in range(NKD):
            p.emit("vector", lambda e, m=m: e.tensor_tensor(
                c.s1[:, m, :], c.x_fm[:, m, :], c.ybf[:, m, :], ALU.add), inc="vec")
        v_s1 = p.count("vec")
        c.xfm_free = v_s1
        p.emit("vector", lambda e: e.tensor_copy(c.xb[:], c.s1[:]), inc="vec")
        v_xb2 = p.count("vec")
        p.wait("scalar", "vec", v_xb2)
        p.emit("scalar", lambda e: e.activation(c.sq[:], c.xb[:], AF.Square), inc="act")
        a_sq = p.count("act")
        p.wait("tensor", "vec", v_xb2)
        p.wait("tensor", "act", a_sq)
        c.pe_wait("pw")
        for k in range(NKD):
            p.emit("tensor", lambda e, k=k: e.matmul(
                c.pW[0:1, 0:512], c.onesb_s[:], c.xb[:, k, :],
                start=(k == 0), stop=(k == NKD - 1)), inc=("mm" if k == NKD - 1 else None))
        for k in range(NKD):
            p.emit("tensor", lambda e, k=k: e.matmul(
                c.pW[0:1, 512:1024], c.onesb_s[:], c.sq[:, k, :],
                start=(k == 0), stop=(k == NKD - 1)), inc=("mm" if k == NKD - 1 else None))
        mm_st = p.count("mm")
        c.xb_free_mm = mm_st
        p.wait("vector", "mm", mm_st)
        p.emit("vector", lambda e: e.tensor_scalar(
            c.r0[:], c.pW[0:1, 0:512], 1.0 / D, None, ALU.mult))
        p.emit("vector", lambda e: e.tensor_tensor(c.r1[:], c.r0[:], c.r0[:], ALU.mult))
        p.emit("vector", lambda e: e.scalar_tensor_tensor(
            c.r2[:], c.pW[0:1, 512:1024], 1.0 / D, c.r1[:], ALU.mult, ALU.subtract), inc="vec")
        v_rows = p.count("vec")
        c.free["pw"] = ("vec", v_rows)
        p.wait("scalar", "vec", v_rows)
        p.emit("scalar", lambda e: e.activation(c.r1[:], c.r2[:], AF.Ln, bias=c.epsr_s[:]))
        p.emit("scalar", lambda e: e.activation(c.r2[:], c.r1[:], AF.Exp, scale=-0.5), inc="act")
        a_rows = p.count("act")
        p.wait("tensor", "act", a_rows)
        c.pe_wait("pp")
        p.emit("tensor", lambda e: e.matmul(
            c.pP[:, 0:512], c.onesr_s[:], c.r0[:], start=True, stop=True))
        p.emit("tensor", lambda e: e.matmul(
            c.pP[:, 512:1024], c.onesr_s[:], c.r2[:], start=True, stop=True), inc="mm")
        mm_bc = p.count("mm")
        p.wait("vector", "mm", mm_bc)
        v_x4 = None
        for k in range(NKD):
            p.emit("vector", lambda e, k=k: e.tensor_tensor(
                c.t1[:], c.s1[:, k, :], c.pP[:, 0:512], ALU.subtract))
            v_x4 = p.emit("vector", lambda e, k=k: e.tensor_tensor(
                c.s1[:, k, :], c.t1[:], c.pP[:, 512:1024], ALU.mult), inc="vec")
        c.free["pp"] = ("vec", v_x4)
        p.wait("sync", "vec", v_x4)
        p.dma("sync", c.x4T[b].rearrange("(k p) t -> p k t", p=128)[:, :, gsl],
              c.s1[:], inc="st")
    return (build, host_fold_inputs, CFG)


_build_l0m, _cfg_l0m = _mod_l0m()
_build_l0f, _cfg_l0f = _mod_l0f()
_build_l1d, _cfg_l1d = _mod_l1d()
_build_l1f, _l1f_fold, _cfg_l1f = _mod_l1f()


def _build_fused():
    nc = bass.Bass("TRN2", target_bir_lowering=False)
    nc.detect_race_conditions = False
    stack = contextlib.ExitStack()
    xin = nc.dram_tensor("xT", [2, 1024, 2048], F32, kind="ExternalInput").ap()
    h1 = nc.dram_tensor("h1", [2, 1024, 2048], F32, kind="Internal").ap()
    h2 = nc.dram_tensor("h2", [2, 1024, 2048], F32, kind="Internal").ap()
    h3 = nc.dram_tensor("h3", [2, 1024, 2048], F32, kind="Internal").ap()
    xout = nc.dram_tensor("x4T", [2, 1024, 2048], F32, kind="ExternalOutput").ap()
    p = Prog(nc, stack)
    plan = [(_build_l0m, _cfg_l0m, "m0_", xin, h1),
            (_build_l0f, _cfg_l0f, "f0_", h1, h2),
            (_build_l1d, _cfg_l1d, "d1_", h2, h3),
            (_build_l1f, _cfg_l1f, "f1_", h3, xout)]
    for bld, cfg, pfx, xi, xo in plan:
        cfg["pfx"], cfg["xin"], cfg["xout"] = pfx, xi, xo
        st = bld(nc, ext=p)
        p.barrier()
        st.close()
        cfg["pfx"], cfg["xin"], cfg["xout"] = "", None, None
    blk = stack.enter_context(nc.Block())
    p.finalize(blk)
    stack.close()
    return nc


def _fused_inputs(d):
    im = {}
    for pfx, sub in (("m0_", _im_l0m(d)), ("f0_", _im_l0f(d)),
                     ("d1_", _im_l1d(d)), ("f1_", _im_l1f(d))):
        for k, v in sub.items():
            im[pfx + k] = v
    return im


def _im_l0m(d):
    c = host_consts()
    selb = np.zeros((32, 32, 64), np.float32)
    for h in range(32):
        selb[h, h, :] = 1.0
    return {
        "w_inT": np.ascontiguousarray(d["L0_attn_W_in"].T),
        "w_outT": np.ascontiguousarray(d["L0_attn_W_out"].T),
        "lnw8": np.ascontiguousarray(d["L0_attn_ln_w"].reshape(8, 128).T),
        "lnb8": np.ascontiguousarray(d["L0_attn_ln_b"].reshape(8, 128).T),
        "convw": np.ascontiguousarray(
            d["L0_attn_conv_w"].reshape(17, 128, 4).transpose(1, 0, 2)),
        "convb": np.ascontiguousarray(d["L0_attn_conv_b"].reshape(17, 128).T),
        "dtb": np.ascontiguousarray(d["L0_attn_dt_bias"][:, None]),
        "alog": np.ascontiguousarray(d["L0_attn_A_log"][:, None]),
        "dexp": np.ascontiguousarray(
            np.repeat(d["L0_attn_D"], 64).reshape(16, 128).T),
        "nrmw16": np.ascontiguousarray(d["L0_attn_norm_w"].reshape(16, 128).T),
        "c_ident": c["c_ident"],
        "c_identf": c["c_identf"],
        "c_triu_i": c["c_triu_i"],
        "c_sel32": np.ascontiguousarray(
            np.stack([np.eye(32)[:, h][:, None] * np.ones((1, 128))
                      for h in range(32)], 1)).astype(np.float32),
        "c_sel32b": np.ascontiguousarray(
            selb.transpose(1, 0, 2)).astype(ml_dtypes.bfloat16),
        "c_ones_b": c["c_ones_b"],
        "c_ones_row": np.ones((1, 128), np.float32),
        "c_eps_row": np.array([[1e-5]], np.float32),
        "c_one32": np.ones((32, 1), np.float32),
    }


def _im_l0f(d):
    return {
        "kern": np.ascontiguousarray(d["L0_ffn_kernel"]),
        "s_in": np.asarray(d["L0_ffn_input_scale"],
                           np.float32).reshape(1, 1),
        "s_k": np.asarray(d["L0_ffn_kernel_scale"],
                          np.float32).reshape(1, 1),
        "bias8": np.ascontiguousarray(
            d["L0_ffn_bias"].astype(np.float32).reshape(8, 128).T),
        "c_ones_b": np.ones((128, 1)).astype(ml_dtypes.bfloat16),
        "c_ones_row": np.ones((1, 128), np.float32),
        "c_eps_row": np.array([[1e-5]], np.float32),
    }


def _im_l1d(d):
    wall = np.zeros((1024, 4736), np.float32)
    wall[:, 0:768] = d["L1_attn_Wq"]
    wall[:, 768:1536] = d["L1_attn_Wk"]
    wall[:, 1536:3072] = d["L1_attn_Wv"]
    wall[:, 3072:4608] = d["L1_attn_Wg"]
    wall[:, 4608:4620] = d["L1_attn_Wgk"]
    wall[:, 4640:4652] = d["L1_attn_Wb"]
    convw = np.concatenate([d["L1_attn_conv_q"], d["L1_attn_conv_k"],
                            d["L1_attn_conv_v"]], 0)
    bb = np.zeros((44, 1), np.float32)
    bb[32:44, 0] = d["L1_attn_b_b"]
    selone = np.zeros((12, 12, 128), np.float32)
    for h in range(12):
        selone[h, h, :] = 1.0
    selp = np.zeros((6, 12, 128), np.float32)
    for j in range(6):
        selp[j, 2 * j, 0:64] = 1.0
        selp[j, 2 * j + 1, 64:128] = 1.0
    tl = np.tril(np.ones((64, 64)), -1)
    tu = np.triu(np.ones((64, 64)))
    dgm = np.ones((64, 64))
    z = np.zeros((64, 64))
    return {
        "wall": wall,
        "woT": np.ascontiguousarray(d["L1_attn_Wo"].astype(np.float32)),
        "convw": np.ascontiguousarray(
            convw.reshape(24, 128, 4).transpose(1, 0, 2)),
        "convb_z": np.zeros((1, 1), np.float32),
        "dtb12": d["L1_attn_dt_bias"].reshape(12, 1).astype(np.float32),
        "alog12": d["L1_attn_A_log"].reshape(12, 1).astype(np.float32),
        "bb44": bb,
        "gnw": d["L1_attn_g_norm_weight"].reshape(128, 1).astype(np.float32),
        "c_ident": np.eye(128).astype(ml_dtypes.bfloat16),
        "c_ident_h": np.eye(128).astype(np.float16),
        "c_bdones_h": np.block([[np.ones((64, 64)), z],
                                [z, np.ones((64, 64))]]).astype(np.float16),
        "c_ones_h": np.ones((128, 1), np.float16),
        "c_identf": np.eye(128).astype(np.float32),
        "c_bdones": np.block([[np.ones((64, 64)), z],
                              [z, np.ones((64, 64))]]).astype(ml_dtypes.bfloat16),
        "c_ptril_s": np.block([[tl, z], [z, tl]]).astype(ml_dtypes.bfloat16),
        "c_ptriu_i": np.block([[tu, z], [z, tu]]).astype(ml_dtypes.bfloat16),
        "c_pdiag": np.block([[dgm, z], [z, dgm]]).astype(ml_dtypes.bfloat16),
        "c_selone": np.ascontiguousarray(selone.transpose(1, 0, 2)),
        "c_selp": np.ascontiguousarray(selp.transpose(1, 0, 2)),
        "c_selpb": np.ascontiguousarray(selp[:, :, 0:64].transpose(1, 0, 2)),
        "c_ones_b": np.ones((128, 1)).astype(ml_dtypes.bfloat16),
        "c_ones_row": np.ones((1, 128), np.float32),
        "c_eps_row": np.array([[1e-5]], np.float32),
        "c_eps5col": np.full((128, 1), 1e-5, np.float32),
        "c_eps6col": np.full((128, 1), 1e-6, np.float32),
        "c_one12": np.ones((12, 1), np.float32),
        "c_mln8": np.full((128, 1), -np.log(8.0), np.float32),
    }


def _im_l1f(d):
    fold = _l1f_fold(np.asarray(d["L1_ffn_up_weights"], np.float32),
                     np.asarray(d["L1_ffn_down_weights"], np.float32))
    return {
        "wexpT": np.ascontiguousarray(d["L1_ffn_W_exp"].T.astype(np.float32)),
        "wcon": np.ascontiguousarray(d["L1_ffn_W_con"].astype(np.float32)),
        "bexp32": np.ascontiguousarray(
            d["L1_ffn_b_exp"].reshape(32, 128).T.astype(np.float32)),
        "upb32": np.ascontiguousarray(
            d["L1_ffn_up_bias"].reshape(32, 128).T.astype(np.float32)),
        "bcon8": np.ascontiguousarray(
            d["L1_ffn_b_con"].reshape(8, 128).T.astype(np.float32)),
        "dwnb8": np.ascontiguousarray(
            d["L1_ffn_down_bias"].reshape(8, 128).T.astype(np.float32)),
        "c_ones_b": np.ones((128, 1)).astype(ml_dtypes.bfloat16),
        "c_ident": np.eye(128).astype(ml_dtypes.bfloat16),
        "c_ones_row": np.ones((1, 128), np.float32),
        "c_eps_row": np.array([[1e-5]], np.float32),
        **fold,
    }


_NC_CACHE = {}


def _programs():
    if "ncs" not in _NC_CACHE:
        ncs = []
        for bld in (_build_l0m, _build_l0f, _build_l1d, _build_l1f):
            nc = bass.Bass("TRN2", target_bir_lowering=False)
            nc.detect_race_conditions = False
            bld(nc)
            ncs.append(nc)
        _NC_CACHE["ncs"] = ncs
    return _NC_CACHE["ncs"]


def kernel(**inputs):
    d = {k: (np.asarray(v, np.float32) if "ffn" not in k else np.asarray(v))
         for k, v in inputs.items()}
    x = d["L0_attn_x"]
    if "fused" not in _NC_CACHE:
        _NC_CACHE["fused"] = _build_fused()
    nc = _NC_CACHE["fused"]
    base = _fused_inputs(d)
    ims = [dict(base, xT=np.ascontiguousarray(
        x[2 * cc:2 * cc + 2].transpose(0, 2, 1))) for cc in range(8)]
    res = run_bass_kernel_spmd(nc, ims, core_ids=list(range(8)))
    out = np.empty((16, 2048, 1024), np.float32)
    for cc in range(8):
        out[2 * cc:2 * cc + 2] = np.asarray(
            res.results[cc]["x4T"]).transpose(0, 2, 1)
    return out
